# version 01
# baseline (speedup 1.0000x reference)
import math
import numpy as np
import ml_dtypes
import concourse.bass as bass
import concourse.mybir as mybir
from concourse.bass_utils import run_bass_kernel_spmd

F32 = mybir.dt.float32
BF = mybir.dt.bfloat16
AF = mybir.ActivationFunctionType
OP = mybir.AluOpType
AX = mybir.AxisListType

BF_NP = ml_dtypes.bfloat16
T = 4096
DM = 1024
NH = 16
DQ = 64
NBI = 10
NT = T // 128


class Prog:
    def __init__(self):
        self.nc = bass.Bass("TRN2", target_bir_lowering=False)
        self.sched = []
        self.t = 0
        self.chain = self.nc.alloc_semaphore("chain")
        self._uid = 0

    def inp(self, name, shape, dt=F32):
        return self.nc.dram_tensor(name, shape, dt, kind="ExternalInput").ap()

    def outp(self, name, shape, dt=F32):
        return self.nc.dram_tensor(name, shape, dt, kind="ExternalOutput").ap()

    def sb(self, shape, dt=F32):
        self._uid += 1
        return self.nc.alloc_sbuf_tensor("sb%d" % self._uid, shape, dt)

    def ps(self, shape, dt=F32):
        self._uid += 1
        return self.nc.alloc_psum_tensor("ps%d" % self._uid, shape, dt)

    def _op(self, eng, fn, inc=1, wait=True):
        self.sched.append((eng, fn, self.t if wait else -1, inc))
        self.t += inc

    def dma(self, out, in_):
        def f(e, o=out, i=in_):
            return e.dma_start(out=o, in_=i)
        self._op("sync", f, inc=16)

    def mm(self, out, lhsT, rhs, start=True, stop=True, wait=True, inc=1):
        def f(e, o=out, l=lhsT, r=rhs, st=start, sp=stop):
            return e.matmul(o, l, r, start=st, stop=sp, skip_group_check=True)
        self._op("tensor", f, inc=inc, wait=wait)

    def trans(self, out, in_, ident, wait=True, inc=1):
        def f(e, o=out, i=in_, d=ident):
            return e.transpose(o, i, d)
        self._op("tensor", f, inc=inc, wait=wait)

    def tt(self, out, in0, in1, op):
        def f(e, o=out, a=in0, b=in1, p=op):
            return e.tensor_tensor(o, a, b, p)
        self._op("vector", f)

    def ts(self, out, in0, s1, s2, op0, op1):
        def f(e, o=out, a=in0, x=s1, y=s2, p0=op0, p1=op1):
            return e.tensor_scalar(o, a, x, y, p0, p1)
        self._op("vector", f)

    def ts1(self, out, in0, s1, op0):
        if op0 is OP.mult:
            def f(e, o=out, a=in0, x=s1):
                return e.tensor_scalar_mul(o, a, x)
        elif op0 is OP.add:
            def f(e, o=out, a=in0, x=s1):
                return e.tensor_scalar_add(o, a, x)
        elif op0 is OP.min:
            def f(e, o=out, a=in0, x=s1):
                return e.tensor_scalar_min(o, a, x)
        else:
            def f(e, o=out, a=in0, x=s1, p=op0):
                return e.tensor_scalar(o, a, x, None, p)
        self._op("vector", f)

    def stt(self, out, in0, scalar, in1, op0, op1):
        def f(e, o=out, a=in0, s=scalar, b=in1, p0=op0, p1=op1):
            return e.scalar_tensor_tensor(o, a, s, b, p0, p1)
        self._op("vector", f)

    def red(self, out, in_, op):
        def f(e, o=out, i=in_, p=op):
            return e.tensor_reduce(o, i, AX.X, p)
        self._op("vector", f)

    def recip(self, out, in_):
        def f(e, o=out, i=in_):
            return e.reciprocal(o, i)
        self._op("vector", f)

    def memset(self, ap, c):
        def f(e, a=ap, v=c):
            return e.memset(a, v)
        self._op("vector", f)

    def act(self, out, in_, func, bias=None, scale=None):
        def f(e, o=out, i=in_, fu=func, b=bias, s=scale):
            kw = {}
            if b is not None:
                kw["bias"] = b
            if s is not None:
                kw["scale"] = s
            return e.activation(o, i, fu, **kw)
        self._op("scalar", f)

    def copy(self, out, in_):
        def f(e, o=out, i=in_):
            return e.copy(o, i)
        self._op("scalar", f)

    def sign(self, out, in_):
        def f(e, o=out, i=in_):
            return e.sign(o, i)
        self._op("scalar", f)

    def finalize(self):
        nc = self.nc
        chain = self.chain
        total = self.t
        self.sched.append(
            ("sync", lambda e, c=chain, v=total: e.wait_ge(c, v), -1, 0))
        with nc.Block() as blk:
            for engname in ("sync", "tensor", "vector", "scalar", "gpsimd"):
                entries = [(f, w, i) for (e, f, w, i) in self.sched
                           if e == engname]
                if not entries:
                    continue

                def mk(entries, c=chain):
                    def body(eng):
                        for (f, w, i) in entries:
                            if w > 0:
                                eng.wait_ge(c, w)
                            ins = f(eng)
                            if i:
                                ins.then_inc(c, i)
                    return body
                getattr(blk, engname)(mk(entries))
        return nc

    def setup_consts(self):
        vals = [0.0, 1.0, -1.0, 1e-5, 1.0 / 1024, -1.0 / 1024, 0.125,
                1.702, 1.0 / 1.702, 7.0, -7.0, 100.0, -100.0, -1e30]
        cs = self.sb([128, len(vals)])
        cb = self.sb([128, len(vals)], BF)
        for j, v in enumerate(vals):
            self.memset(cs[:, j:j + 1], v)
            self.memset(cb[:, j:j + 1], v)
            self.nc.const_aps.register(F32, float(v), cs[:, j:j + 1])
            self.nc.const_aps.register(BF, float(v), cb[:, j:j + 1])

    def load_w_bf16(self, src_ap, rows, cols, dst_sb, stage, do_sign=False):
        nk = rows // 128
        for k in range(nk):
            self.dma(stage[:, 0:cols], src_ap[k * 128:(k + 1) * 128, :])
            if do_sign:
                self.sign(dst_sb[:, k * cols:(k + 1) * cols], stage[:, 0:cols])
            else:
                self.copy(dst_sb[:, k * cols:(k + 1) * cols], stage[:, 0:cols])

    def layernorm(self, x_sb, out_bf, tmp_xc, tmp_sq, st1, st2):
        self.red(st1[:, 0:1], x_sb[:, 0:DM], OP.add)
        self.ts1(st1[:, 1:2], st1[:, 0:1], -1.0 / DM, OP.mult)
        self.ts1(tmp_xc[:, 0:DM], x_sb[:, 0:DM], st1[:, 1:2], OP.add)
        self.tt(tmp_sq[:, 0:DM], tmp_xc[:, 0:DM], tmp_xc[:, 0:DM], OP.mult)
        self.red(st2[:, 0:1], tmp_sq[:, 0:DM], OP.add)
        self.act(st2[:, 1:2], st2[:, 0:1], AF.Sqrt, bias=1e-5, scale=1.0 / DM)
        self.recip(st2[:, 2:3], st2[:, 1:2])
        self.ts1(out_bf[:, 0:DM], tmp_xc[:, 0:DM], st2[:, 2:3], OP.mult)


def build_A():
    p = Prog()
    p.setup_consts()
    x = p.inp("x", [T, DM])
    Wq = p.inp("Wq", [DM, DM])
    Wk = p.inp("Wk", [DM, DM])
    Wv = p.inp("Wv", [DM, DM])
    Wr = p.inp("Wr", [DM, 160])
    WfT = p.inp("WfT", [DM, DM])
    bqc = p.inp("bqc", [128, 8])
    br1 = p.inp("br1", [1, 160])
    scb = p.inp("scb", [128, DM])
    bib = p.inp("bib", [128, DM])
    ident = p.inp("ident", [128, 128], BF)
    QT = p.outp("QT", [NT, 128, 8, 128], BF)
    KT = p.outp("KT", [NT, 128, 8, 128], BF)
    V = p.outp("V", [NT, 128, DM], BF)
    R = p.outp("R", [NT, 128, 160])
    G = p.outp("G", [NT, 128, DM])

    stage = p.sb([128, DM])
    wq_sb = p.sb([128, 8 * DM], BF)
    wk_sb = p.sb([128, 8 * DM], BF)
    wv_sb = p.sb([128, 8 * DM], BF)
    wr_sb = p.sb([128, 8 * 160], BF)
    wf_sb = p.sb([128, 8 * DM], BF)
    id_sb = p.sb([128, 128], BF)
    ones = p.sb([1, 512], BF)
    bq_sb = p.sb([128, 8])
    bq8 = p.sb([128, 8])
    brf = p.sb([1, 160])
    brb = p.sb([1, 160], BF)
    sc_sb = p.sb([128, DM])
    bi_sb = p.sb([128, DM])

    p.load_w_bf16(Wq, DM, DM, wq_sb, stage)
    p.load_w_bf16(Wk, DM, DM, wk_sb, stage)
    p.load_w_bf16(Wv, DM, DM, wv_sb, stage)
    p.load_w_bf16(Wr, DM, 160, wr_sb, stage)
    p.load_w_bf16(WfT, DM, DM, wf_sb, stage, do_sign=True)
    p.dma(id_sb[:, :], ident)
    p.memset(ones[0:1, :], 1.0)
    p.dma(bq_sb[:, :], bqc)
    p.ts1(bq8[:, 0:8], bq_sb[:, 0:8], 0.125, OP.mult)
    p.dma(brf[0:1, :], br1)
    p.copy(brb[0:1, :], brf[0:1, :])
    p.dma(sc_sb[:, :], scb)
    p.dma(bi_sb[:, :], bib)

    x_sb = p.sb([128, DM])
    xc = p.sb([128, DM])
    sq = p.sb([128, DM])
    st1 = p.sb([128, 4])
    st2 = p.sb([128, 4])
    xn = p.sb([128, DM], BF)
    xnT = p.sb([128, DM], BF)
    qt_sb = p.sb([128, DM], BF)
    kt_sb = p.sb([128, DM], BF)
    v_sb = p.sb([128, DM], BF)
    r_sb = p.sb([128, 160])
    y_sb = p.sb([128, DM])
    g_sb = p.sb([128, DM])
    pst = p.ps([128, 128], BF)
    ps1 = p.ps([128, 128])
    ps5 = p.ps([128, 512])
    psr = p.ps([128, 160])

    for i in range(NT):
        p.dma(x_sb[:, :], x[i * 128:(i + 1) * 128, :])
        p.layernorm(x_sb, xn, xc, sq, st1, st2)
        for k in range(8):
            p.trans(pst[:, :], xn[:, k * 128:(k + 1) * 128], id_sb[:, :])
            p.copy(xnT[:, k * 128:(k + 1) * 128], pst[:, :])
        for (wsb, dst, biased) in ((wq_sb, qt_sb, True), (wk_sb, kt_sb, False)):
            for m in range(8):
                for k in range(8):
                    p.mm(ps1[:, :],
                         wsb[:, k * DM + m * 128:k * DM + (m + 1) * 128],
                         xnT[:, k * 128:(k + 1) * 128],
                         start=(k == 0), stop=(k == 7),
                         wait=(k == 0), inc=(1 if k == 7 else 0))
                if biased:
                    p.ts(dst[:, m * 128:(m + 1) * 128], ps1[:, :], 0.125,
                         bq8[:, m:m + 1], OP.mult, OP.add)
                else:
                    p.copy(dst[:, m * 128:(m + 1) * 128], ps1[:, :])
        for nb in range(2):
            for k in range(8):
                p.mm(ps5[:, :], xnT[:, k * 128:(k + 1) * 128],
                     wv_sb[:, k * DM + nb * 512:k * DM + nb * 512 + 512],
                     start=(k == 0), stop=(k == 7),
                     wait=(k == 0), inc=(1 if k == 7 else 0))
            p.copy(v_sb[:, nb * 512:(nb + 1) * 512], ps5[:, :])
        for k in range(8):
            p.mm(psr[:, :], xnT[:, k * 128:(k + 1) * 128],
                 wr_sb[:, k * 160:(k + 1) * 160],
                 start=(k == 0), stop=False,
                 wait=(k == 0), inc=0)
        p.mm(psr[:, :], ones[0:1, 0:128], brb[0:1, :], start=False, stop=True,
             wait=False, inc=1)
        p.copy(r_sb[:, :], psr[:, :])
        for nb in range(2):
            for k in range(8):
                p.mm(ps5[:, :], xnT[:, k * 128:(k + 1) * 128],
                     wf_sb[:, k * DM + nb * 512:k * DM + nb * 512 + 512],
                     start=(k == 0), stop=(k == 7),
                     wait=(k == 0), inc=(1 if k == 7 else 0))
            p.copy(y_sb[:, nb * 512:(nb + 1) * 512], ps5[:, :])
        p.tt(y_sb[:, 0:DM], y_sb[:, 0:DM], sc_sb[:, 0:DM], OP.mult)
        p.tt(y_sb[:, 0:DM], y_sb[:, 0:DM], bi_sb[:, 0:DM], OP.add)
        p.ts(y_sb[:, 0:DM], y_sb[:, 0:DM], 100.0, -100.0, OP.min, OP.max)
        p.act(g_sb[:, 0:DM], y_sb[:, 0:DM], AF.Silu)
        p.dma(QT[i], qt_sb[:, 0:DM].rearrange("p (m c) -> p m c", m=8))
        p.dma(KT[i], kt_sb[:, 0:DM].rearrange("p (m c) -> p m c", m=8))
        p.dma(V[i], v_sb[:, :])
        p.dma(R[i], r_sb[:, :])
        p.dma(G[i], g_sb[:, :])
    return p.finalize()


def build_B(ilo, ihi):
    p = Prog()
    p.setup_consts()
    QT = p.inp("QT", [NT, 128, 8, 128], BF)
    KT = p.inp("KT", [NT, 128, 8, 128], BF)
    V = p.inp("V", [NT, 128, DM], BF)
    R = p.inp("R", [NT, 128, 160])
    U = p.inp("U", [NBI, 128, 1152])
    MV0 = p.inp("MV0", [128, 128])
    MV8 = p.inp("MV8", [128, 128])
    ident = p.inp("ident", [128, 128], BF)
    AT = p.outp("AT", [NT, 128, 8, 128], BF)

    u_sb = p.sb([128, NBI * 1152])
    for n in range(NBI):
        p.dma(u_sb[:, n * 1152:(n + 1) * 1152], U[n])
    m0_sb = p.sb([128, 128])
    m8_sb = p.sb([128, 128])
    id_sb = p.sb([128, 128], BF)
    p.dma(m0_sb[:, :], MV0)
    p.dma(m8_sb[:, :], MV8)
    p.dma(id_sb[:, :], ident)

    qt_sb = p.sb([128, DM], BF)
    r_sb = p.sb([128, 160])
    ktw = p.sb([128, 8 * 1152], BF)
    vw = p.sb([128, 9 * DM], BF)
    buf = p.sb([128, 1152])
    buf2 = p.sb([128, 1152])
    P_sb = p.sb([128, 1152], BF)
    PT_sb = p.sb([128, 1152], BF)
    at_sb = p.sb([128, DM], BF)
    st = p.sb([128, 4])
    ps_s = p.ps([128, 1536])
    ptp = p.ps([128, 1152], BF)
    av = p.ps([64, 128])

    for i in range(ilo, ihi):
        glo = max(0, i - 8)
        nt = i - glo + 1
        W = nt * 128
        u_off = (9 - nt) * 128
        p.dma(qt_sb[:, 0:DM].rearrange("p (m c) -> p m c", m=8), QT[i])
        p.dma(r_sb[:, :], R[i])
        for m in range(8):
            p.dma(ktw[:, m * 1152:m * 1152 + W]
                  .rearrange("p (i t) -> p i t", i=nt),
                  KT[glo:i + 1, :, m, :].rearrange("i p t -> p i t"))
        for it, gt in enumerate(range(glo, i + 1)):
            p.dma(vw[:, it * DM:(it + 1) * DM], V[gt])
        for h in range(NH):
            hp = 64 * (h % 2)
            hm = h // 2
            p.ts1(buf[:, 0:W],
                  u_sb[:, 0 * 1152 + u_off:0 * 1152 + u_off + W],
                  r_sb[:, h * 10:h * 10 + 1], OP.mult)
            for n in range(1, NBI):
                p.stt(buf[:, 0:W],
                      u_sb[:, n * 1152 + u_off:n * 1152 + u_off + W],
                      r_sb[:, h * 10 + n:h * 10 + n + 1],
                      buf[:, 0:W], OP.mult, OP.add)
            for c0 in range(0, W, 512):
                w = min(512, W - c0)
                p.mm(ps_s[:, c0:c0 + w],
                     qt_sb[hp:hp + 64, hm * 128:(hm + 1) * 128],
                     ktw[hp:hp + 64, hm * 1152 + c0:hm * 1152 + c0 + w])
            for c0 in range(0, W, 512):
                w = min(512, W - c0)
                p.tt(buf[:, c0:c0 + w], ps_s[:, c0:c0 + w],
                     buf[:, c0:c0 + w], OP.add)
            p.tt(buf[:, W - 128:W], buf[:, W - 128:W], m0_sb[:, :], OP.add)
            if nt == 9:
                p.tt(buf[:, 0:128], buf[:, 0:128], m8_sb[:, :], OP.add)
            p.red(st[:, 0:1], buf[:, 0:W], OP.max)
            p.ts1(st[:, 1:2], st[:, 0:1], -1.0, OP.mult)
            p.act(buf2[:, 0:W], buf[:, 0:W], AF.Exp, bias=st[:, 1:2],
                  scale=1.0)
            p.red(st[:, 2:3], buf2[:, 0:W], OP.add)
            p.recip(st[:, 3:4], st[:, 2:3])
            p.ts1(P_sb[:, 0:W], buf2[:, 0:W], st[:, 3:4], OP.mult)
            for kt in range(nt):
                p.trans(ptp[:, kt * 128:(kt + 1) * 128],
                        P_sb[:, kt * 128:(kt + 1) * 128], id_sb[:, :],
                        wait=(kt == 0), inc=(1 if kt == nt - 1 else 0))
            for kt in range(nt):
                p.copy(PT_sb[:, kt * 128:(kt + 1) * 128],
                       ptp[:, kt * 128:(kt + 1) * 128])
            for kt in range(nt):
                p.mm(av[0:64, :], vw[:, kt * DM + h * 64:kt * DM + h * 64 + 64],
                     PT_sb[:, kt * 128:(kt + 1) * 128],
                     start=(kt == 0), stop=(kt == nt - 1),
                     wait=(kt == 0), inc=(1 if kt == nt - 1 else 0))
            p.copy(at_sb[hp:hp + 64, hm * 128:(hm + 1) * 128], av[0:64, :])
        p.dma(AT[i], at_sb[:, 0:DM].rearrange("p (m c) -> p m c", m=8))
    return p.finalize()


def build_C():
    p = Prog()
    p.setup_consts()
    AT = p.inp("AT", [NT, 128, 8, 128], BF)
    Wp = p.inp("Wp", [DM, DM])
    bp1 = p.inp("bp1", [1, DM])
    x = p.inp("x", [T, DM])
    G = p.inp("G", [NT, 128, DM])
    ident = p.inp("ident", [128, 128], BF)
    X1 = p.outp("X1", [T, DM])
    XN1T = p.outp("XN1T", [NT, 128, 8, 128], BF)

    stage = p.sb([128, DM])
    wp_sb = p.sb([128, 8 * DM], BF)
    p.load_w_bf16(Wp, DM, DM, wp_sb, stage)
    bpf = p.sb([1, DM])
    bpb = p.sb([1, DM], BF)
    p.dma(bpf[0:1, :], bp1)
    p.copy(bpb[0:1, :], bpf[0:1, :])
    id_sb = p.sb([128, 128], BF)
    p.dma(id_sb[:, :], ident)
    ones = p.sb([1, 512], BF)
    p.memset(ones[0:1, :], 1.0)

    at_sb = p.sb([128, DM], BF)
    s_sb = p.sb([128, DM])
    x_sb = p.sb([128, DM])
    g_sb = p.sb([128, DM])
    x1_sb = p.sb([128, DM])
    xc = p.sb([128, DM])
    sq = p.sb([128, DM])
    st1 = p.sb([128, 4])
    st2 = p.sb([128, 4])
    xn = p.sb([128, DM], BF)
    xnT = p.sb([128, DM], BF)
    ps5 = p.ps([128, 512])
    pst = p.ps([128, 128], BF)

    for i in range(NT):
        p.dma(at_sb[:, 0:DM].rearrange("p (m c) -> p m c", m=8), AT[i])
        for nb in range(2):
            for m in range(8):
                p.mm(ps5[:, :], at_sb[:, m * 128:(m + 1) * 128],
                     wp_sb[:, m * DM + nb * 512:m * DM + nb * 512 + 512],
                     start=(m == 0), stop=False,
                     wait=(m == 0), inc=0)
            p.mm(ps5[:, :], ones[0:1, 0:128],
                 bpb[0:1, nb * 512:nb * 512 + 512], start=False, stop=True,
                 wait=False, inc=1)
            p.copy(s_sb[:, nb * 512:(nb + 1) * 512], ps5[:, :])
        p.dma(x_sb[:, :], x[i * 128:(i + 1) * 128, :])
        p.dma(g_sb[:, :], G[i])
        p.tt(s_sb[:, 0:DM], s_sb[:, 0:DM], g_sb[:, 0:DM], OP.mult)
        p.tt(x1_sb[:, 0:DM], x_sb[:, 0:DM], s_sb[:, 0:DM], OP.add)
        p.dma(X1[i * 128:(i + 1) * 128, :], x1_sb[:, :])
        p.layernorm(x1_sb, xn, xc, sq, st1, st2)
        for k in range(8):
            p.trans(pst[:, :], xn[:, k * 128:(k + 1) * 128], id_sb[:, :])
            p.copy(xnT[:, k * 128:(k + 1) * 128], pst[:, :])
        p.dma(XN1T[i], xnT[:, 0:DM].rearrange("p (m c) -> p m c", m=8))
    return p.finalize()


def build_D(ilo, ihi):
    p = Prog()
    p.setup_consts()
    XN1T = p.inp("XN1T", [NT, 128, 8, 128], BF)
    wqT = p.inp("wqT", [DM, DM])
    kvdT = p.inp("kvdT", [DM, 512])
    kupT = p.inp("kupT", [512, DM])
    vupT = p.inp("vupT", [512, DM])
    WgT = p.inp("WgT", [DM, 16])
    bg1 = p.inp("bg1", [1, 16])
    cosT = p.inp("cosT", [128, T])
    sinT = p.inp("sinT", [128, T])
    SW = p.inp("SW", [128, 128], BF)
    QT1 = p.outp("QT1", [NT, 128, 8, 128], BF)
    KT1 = p.outp("KT1", [NT, 128, 8, 128], BF)
    V1 = p.outp("V1", [NT, 128, DM], BF)
    CW = p.outp("CW", [NT, 128, 16])

    stage = p.sb([128, DM])
    wq_sb = p.sb([128, 8 * DM], BF)
    kvd_sb = p.sb([128, 8 * 512], BF)
    kup_sb = p.sb([128, 4 * DM], BF)
    vup_sb = p.sb([128, 4 * DM], BF)
    wg_sb = p.sb([128, 8 * 16], BF)
    p.load_w_bf16(wqT, DM, DM, wq_sb, stage)
    p.load_w_bf16(kvdT, DM, 512, kvd_sb, stage)
    p.load_w_bf16(kupT, 512, DM, kup_sb, stage)
    p.load_w_bf16(vupT, 512, DM, vup_sb, stage)
    p.load_w_bf16(WgT, DM, 16, wg_sb, stage)
    bgf = p.sb([1, 16])
    bgb = p.sb([1, 16], BF)
    p.dma(bgf[0:1, :], bg1)
    p.copy(bgb[0:1, :], bgf[0:1, :])
    cos_sb = p.sb([128, T])
    sin_sb = p.sb([128, T])
    p.dma(cos_sb[:, :], cosT)
    p.dma(sin_sb[:, :], sinT)
    sw_sb = p.sb([128, 128], BF)
    p.dma(sw_sb[:, :], SW)
    ones = p.sb([1, 512], BF)
    p.memset(ones[0:1, :], 1.0)

    xt_sb = p.sb([128, DM], BF)
    pre_sb = p.sb([128, DM], BF)
    q_sb = p.sb([128, DM], BF)
    k_sb = p.sb([128, DM], BF)
    c_sb = p.sb([128, 512], BF)
    v_sb = p.sb([128, DM], BF)
    tt1 = p.sb([128, 128])
    tt2 = p.sb([128, 128])
    lg = p.sb([128, 16])
    cwb = p.sb([128, 16])
    st = p.sb([128, 8])
    ps1 = p.ps([128, 128])
    psw = p.ps([128, 128])
    ps5 = p.ps([128, 512])

    def rope(src_sb, dst_sb, i):
        for m in range(8):
            p.mm(psw[:, :], sw_sb[:, :], src_sb[:, m * 128:(m + 1) * 128])
            p.tt(tt1[:, :], src_sb[:, m * 128:(m + 1) * 128],
                 cos_sb[:, i * 128:(i + 1) * 128], OP.mult)
            p.tt(tt2[:, :], psw[:, :], sin_sb[:, i * 128:(i + 1) * 128],
                 OP.mult)
            p.tt(dst_sb[:, m * 128:(m + 1) * 128], tt1[:, :], tt2[:, :],
                 OP.add)

    for i in range(ilo, ihi):
        p.dma(xt_sb[:, 0:DM].rearrange("p (m c) -> p m c", m=8), XN1T[i])
        for m in range(8):
            for k in range(8):
                p.mm(ps1[:, :],
                     wq_sb[:, k * DM + m * 128:k * DM + (m + 1) * 128],
                     xt_sb[:, k * 128:(k + 1) * 128],
                     start=(k == 0), stop=(k == 7),
                     wait=(k == 0), inc=(1 if k == 7 else 0))
            p.copy(pre_sb[:, m * 128:(m + 1) * 128], ps1[:, :])
        rope(pre_sb, q_sb, i)
        for m in range(4):
            for k in range(8):
                p.mm(ps1[:, :],
                     kvd_sb[:, k * 512 + m * 128:k * 512 + (m + 1) * 128],
                     xt_sb[:, k * 128:(k + 1) * 128],
                     start=(k == 0), stop=(k == 7),
                     wait=(k == 0), inc=(1 if k == 7 else 0))
            p.copy(c_sb[:, m * 128:(m + 1) * 128], ps1[:, :])
        for m in range(8):
            for kc in range(4):
                p.mm(ps1[:, :],
                     kup_sb[:, kc * DM + m * 128:kc * DM + (m + 1) * 128],
                     c_sb[:, kc * 128:(kc + 1) * 128],
                     start=(kc == 0), stop=(kc == 3),
                     wait=(kc == 0), inc=(1 if kc == 3 else 0))
            p.copy(pre_sb[:, m * 128:(m + 1) * 128], ps1[:, :])
        rope(pre_sb, k_sb, i)
        p.dma(KT1[i], k_sb[:, 0:DM].rearrange("p (m c) -> p m c", m=8))
        for nb in range(2):
            for kc in range(4):
                p.mm(ps5[:, :], c_sb[:, kc * 128:(kc + 1) * 128],
                     vup_sb[:, kc * DM + nb * 512:kc * DM + nb * 512 + 512],
                     start=(kc == 0), stop=(kc == 3),
                     wait=(kc == 0), inc=(1 if kc == 3 else 0))
            p.copy(v_sb[:, nb * 512:(nb + 1) * 512], ps5[:, :])
        for k in range(8):
            p.mm(ps1[:, 0:16], xt_sb[:, k * 128:(k + 1) * 128],
                 wg_sb[:, k * 16:(k + 1) * 16],
                 start=(k == 0), stop=False, wait=(k == 0), inc=0)
        p.mm(ps1[:, 0:16], ones[0:1, 0:128], bgb[0:1, :], start=False,
             stop=True, wait=False, inc=1)
        p.copy(lg[:, :], ps1[:, 0:16])
        p.red(st[:, 0:1], lg[:, 0:16], OP.max)
        p.ts1(st[:, 1:2], st[:, 0:1], -1.0, OP.mult)
        p.ts1(cwb[:, 0:16], lg[:, 0:16], st[:, 0:1], OP.is_ge)
        p.stt(cwb[:, 0:16], cwb[:, 0:16], -1e30, lg[:, 0:16], OP.mult, OP.add)
        p.red(st[:, 2:3], cwb[:, 0:16], OP.max)
        p.ts1(cwb[:, 0:16], lg[:, 0:16], st[:, 2:3], OP.is_ge)
        p.act(lg[:, 0:16], lg[:, 0:16], AF.Exp, bias=st[:, 1:2], scale=1.0)
        p.tt(lg[:, 0:16], lg[:, 0:16], cwb[:, 0:16], OP.mult)
        p.red(st[:, 3:4], lg[:, 0:16], OP.add)
        p.recip(st[:, 4:5], st[:, 3:4])
        p.ts1(cwb[:, 0:16], lg[:, 0:16], st[:, 4:5], OP.mult)
        p.dma(QT1[i], q_sb[:, 0:DM].rearrange("p (m c) -> p m c", m=8))
        p.dma(V1[i], v_sb[:, :])
        p.dma(CW[i], cwb[:, :])
    return p.finalize()


def build_E(hlo, hhi):
    p = Prog()
    p.setup_consts()
    QT1 = p.inp("QT1", [NT, 128, 8, 128], BF)
    KT1 = p.inp("KT1", [NT, 128, 8, 128], BF)
    V1 = p.inp("V1", [NT, 128, DM], BF)
    M01 = p.inp("M01", [4, 128, 512], BF)
    ones64 = p.inp("ones64", [1, 64])
    ATN = p.outp("ATN", [NT, 128, 8, 128], BF)

    m_sb = p.sb([128, 4 * 512], BF)
    for v in range(4):
        p.dma(m_sb[:, v * 512:(v + 1) * 512], M01[v])
    on_sb = p.sb([1, 64])
    p.dma(on_sb[0:1, :], ones64)

    qb_sb = p.sb([64, 512], BF)
    kt_sb = p.sb([64, 128], BF)
    vext = p.sb([128, 65], BF)
    p.memset(vext[:, 64:65], 1.0)
    pt_sb = p.sb([128, 512], BF)
    rs_sb = p.sb([1, 512])
    bc_sb = p.sb([64, 512])
    atn_sb = p.sb([64, 512], BF)
    ps_s = p.ps([128, 512])
    av = p.ps([128, 512])
    bc = p.ps([64, 512])

    for h in range(hlo, hhi):
        hp = 64 * (h % 2)
        hm = h // 2
        for qb in range(8):
            p.dma(qb_sb[0:64, 0:512].rearrange("p (i t) -> p i t", i=4),
                  QT1[4 * qb:4 * qb + 4, hp:hp + 64, hm, :]
                  .rearrange("i p t -> p i t"))
            last = 4 * qb + 3
            for kt in range(4 * qb + 4):
                p.dma(kt_sb[0:64, :], KT1[kt, hp:hp + 64, hm, :])
                p.dma(vext[:, 0:64], V1[kt, :, h * 64:h * 64 + 64])
                p.mm(ps_s[:, :], kt_sb[0:64, :], qb_sb[0:64, :])
                p.act(pt_sb[:, :], ps_s[:, :], AF.Exp, scale=0.125)
                if kt >= 4 * qb:
                    v = kt - 4 * qb
                    p.tt(pt_sb[:, :], pt_sb[:, :],
                         m_sb[:, v * 512:(v + 1) * 512], OP.mult)
                p.mm(av[0:65, :], vext[:, 0:65], pt_sb[:, :],
                     start=(kt == 0), stop=(kt == last),
                     wait=True, inc=1)
            p.recip(rs_sb[0:1, :], av[64:65, :])
            p.mm(bc[0:64, :], on_sb[0:1, :], rs_sb[0:1, :])
            p.copy(bc_sb[:, :], bc[0:64, :])
            p.tt(atn_sb[:, :], av[0:64, :], bc_sb[:, :], OP.mult)
            p.dma(ATN[4 * qb:4 * qb + 4, hp:hp + 64, hm, :]
                  .rearrange("i p t -> p i t"),
                  atn_sb[0:64, 0:512].rearrange("p (i t) -> p i t", i=4))
    return p.finalize()


def build_EP():
    p = Prog()
    p.setup_consts()
    ATN = p.inp("ATN", [NT, 128, 8, 128], BF)
    woT = p.inp("woT", [DM, DM])
    S1 = p.outp("S1", [T, DM])
    stage = p.sb([128, DM])
    wo_sb = p.sb([128, 8 * DM], BF)
    p.load_w_bf16(woT, DM, DM, wo_sb, stage)
    a_sb = p.sb([128, DM], BF)
    s_sb = p.sb([128, DM])
    ps5 = p.ps([128, 512])
    for i in range(NT):
        p.dma(a_sb[:, 0:DM].rearrange("p (m c) -> p m c", m=8), ATN[i])
        for nb in range(2):
            for m in range(8):
                p.mm(ps5[:, :], a_sb[:, m * 128:(m + 1) * 128],
                     wo_sb[:, m * DM + nb * 512:m * DM + nb * 512 + 512],
                     start=(m == 0), stop=(m == 7),
                     wait=(m == 0), inc=(1 if m == 7 else 0))
            p.copy(s_sb[:, nb * 512:(nb + 1) * 512], ps5[:, :])
        p.dma(S1[i * 128:(i + 1) * 128, :], s_sb[:, :])
    return p.finalize()


def build_F(g):
    p = Prog()
    p.setup_consts()
    XN1T = p.inp("XN1T", [NT, 128, 8, 128], BF)
    CW = p.inp("CW", [NT, 128, 16])
    ACCI = p.inp("ACCI", [T, DM])
    Wgu4 = p.inp("Wgu4", [4, DM, 2048])
    bgu4 = p.inp("bgu4", [4, 2048])
    Wd4 = p.inp("Wd4", [4, DM, DM])
    bd4 = p.inp("bd4", [4, DM])
    ACCO = p.outp("ACCO", [T, DM])
    if g == 3:
        X1 = p.inp("X1", [T, DM])
        S1 = p.inp("S1", [T, DM])
        OUT = p.outp("OUT", [T, DM])

    stage = p.sb([128, 2048])
    wgu_sb = p.sb([128, 8 * 2048], BF)
    wd_sb = p.sb([128, 8 * DM], BF)
    bguf = p.sb([1, 2048])
    bgub = p.sb([1, 2048], BF)
    bdf = p.sb([1, DM])
    bdb = p.sb([1, DM], BF)
    ones = p.sb([1, 512], BF)
    p.memset(ones[0:1, :], 1.0)

    xb_sb = p.sb([128, 8 * 512], BF)
    acc_sb = p.sb([128, 4 * DM])
    cw_sb = p.sb([128, 4 * 16])
    ga = p.sb([128, 512])
    gb = p.sb([128, 512])
    ua = p.sb([128, 512])
    ub = p.sb([128, 512])
    gated = p.sb([128, 8 * 512], BF)
    pg = p.ps([128, 512])
    pu = p.ps([128, 512])
    pd = p.ps([128, 1024])
    if g == 3:
        x1t = p.sb([128, DM])
        s1t = p.sb([128, DM])
        g1 = p.sb([128, DM])
        o_sb = p.sb([128, DM])

    for tb in range(8):
        for k in range(8):
            p.dma(xb_sb[:, k * 512:(k + 1) * 512]
                  .rearrange("p (i t) -> p i t", i=4),
                  XN1T[4 * tb:4 * tb + 4, :, k, :].rearrange("i p t -> p i t"))
        for t in range(4):
            r0 = (4 * tb + t) * 128
            p.dma(acc_sb[:, t * DM:(t + 1) * DM], ACCI[r0:r0 + 128, :])
            p.dma(cw_sb[:, t * 16:(t + 1) * 16], CW[4 * tb + t])
        for e in range(4):
            ge = 4 * g + e
            for k in range(8):
                p.dma(stage[:, 0:2048], Wgu4[e, k * 128:(k + 1) * 128, :])
                p.copy(wgu_sb[:, k * 2048:(k + 1) * 2048], stage[:, 0:2048])
            for k in range(8):
                p.dma(stage[:, 0:DM], Wd4[e, k * 128:(k + 1) * 128, :])
                p.copy(wd_sb[:, k * DM:(k + 1) * DM], stage[:, 0:DM])
            p.dma(bguf[0:1, :], bgu4[e:e + 1, :])
            p.copy(bgub[0:1, :], bguf[0:1, :])
            p.dma(bdf[0:1, :], bd4[e:e + 1, :])
            p.copy(bdb[0:1, :], bdf[0:1, :])
            for j in range(8):
                for (col0, psx) in ((j * 128, pg), (1024 + j * 128, pu)):
                    for k in range(8):
                        p.mm(psx[:, :],
                             wgu_sb[:, k * 2048 + col0:k * 2048 + col0 + 128],
                             xb_sb[:, k * 512:(k + 1) * 512],
                             start=(k == 0), stop=False,
                             wait=(k == 0), inc=0)
                    p.mm(psx[:, :], bgub[0:1, col0:col0 + 128],
                         ones[0:1, 0:512], start=False, stop=True,
                         wait=False, inc=1)
                p.ts1(ga[:, :], pg[:, :], 7.0, OP.min)
                p.act(gb[:, :], ga[:, :], AF.Silu, scale=1.702)
                p.ts(ua[:, :], pu[:, :], 7.0, -7.0, OP.min, OP.max)
                p.act(ub[:, :], ua[:, :], AF.Copy, scale=1.0 / 1.702,
                      bias=1.0 / 1.702)
                p.tt(gated[:, j * 512:(j + 1) * 512], gb[:, :], ub[:, :],
                     OP.mult)
            for t in range(4):
                for nb in range(2):
                    for j in range(8):
                        p.mm(pd[:, nb * 512:nb * 512 + 512],
                             gated[:, j * 512 + t * 128:j * 512 + t * 128 + 128],
                             wd_sb[:, j * DM + nb * 512:j * DM + nb * 512 + 512],
                             start=(j == 0), stop=False,
                             wait=(j == 0), inc=0)
                    p.mm(pd[:, nb * 512:nb * 512 + 512], ones[0:1, 0:128],
                         bdb[0:1, nb * 512:nb * 512 + 512], start=False,
                         stop=True, wait=False, inc=1)
                p.stt(acc_sb[:, t * DM:(t + 1) * DM], pd[:, 0:DM],
                      cw_sb[:, t * 16 + ge:t * 16 + ge + 1],
                      acc_sb[:, t * DM:(t + 1) * DM], OP.mult, OP.add)
        for t in range(4):
            r0 = (4 * tb + t) * 128
            p.dma(ACCO[r0:r0 + 128, :], acc_sb[:, t * DM:(t + 1) * DM])
            if g == 3:
                p.dma(x1t[:, :], X1[r0:r0 + 128, :])
                p.dma(s1t[:, :], S1[r0:r0 + 128, :])
                p.act(g1[:, 0:DM], acc_sb[:, t * DM:(t + 1) * DM], AF.Silu)
                p.tt(g1[:, 0:DM], s1t[:, 0:DM], g1[:, 0:DM], OP.mult)
                p.tt(o_sb[:, 0:DM], x1t[:, 0:DM], g1[:, 0:DM], OP.add)
                p.dma(OUT[r0:r0 + 128, :], o_sb[:, :])
    return p.finalize()


_cache = {}


def _programs():
    if "progs" not in _cache:
        progs = {}
        progs["A"] = build_A()
        progs["B"] = [build_B(lo, hi) for (lo, hi) in
                      ((0, 14), (14, 20), (20, 26), (26, 32))]
        progs["C"] = build_C()
        progs["D"] = [build_D(0, 16), build_D(16, 32)]
        progs["E"] = [build_E(h, h + 4) for h in (0, 4, 8, 12)]
        progs["EP"] = build_EP()
        progs["F"] = [build_F(g) for g in range(4)]
        _cache["progs"] = progs
    return _cache["progs"]


def _consts():
    if "consts" in _cache:
        return _cache["consts"]
    c = {}
    c["ident"] = np.eye(128, dtype=BF_NP)
    q = np.arange(128)[:, None]
    k = np.arange(128)[None, :]
    c["MV0"] = np.where(k <= q, 0.0, -1e9).astype(np.float32)
    c["MV8"] = np.where(k > q, 0.0, -1e9).astype(np.float32)
    r = np.arange(128)[None, :, None]
    cc = np.arange(512)[None, None, :]
    v = np.arange(4)[:, None, None]
    c["M01"] = (128 * v + r <= cc).astype(BF_NP)
    c["ones64"] = np.ones((1, 64), dtype=np.float32)
    d = np.arange(64)
    inv = 1.0 / (10000.0 ** (2 * (d % 32) / 64.0))
    pos = np.arange(T)
    ang = pos[None, :] * inv[:, None]
    cos = np.cos(ang)
    sin = np.sin(ang)
    sgn = np.where(d < 32, -1.0, 1.0)[:, None]
    c["cosT"] = np.tile(cos, (2, 1)).astype(np.float32)
    c["sinT"] = np.tile(sin * sgn, (2, 1)).astype(np.float32)
    S = np.zeros((128, 128), dtype=BF_NP)
    for m in range(128):
        sig = m + 32 if (m % 64) < 32 else m - 32
        S[sig, m] = 1.0
    c["SW"] = S
    _cache["consts"] = c
    return c


def _utables(b_nd):
    qq = np.arange(128)[:, None]
    jj = np.arange(1152)[None, :]
    idx = 1024 + qq - jj
    ok = (idx >= 0) & (idx < 1024)
    idxc = np.clip(idx, 0, 1023)
    U = np.where(ok[None, :, :], b_nd[:, idxc], 0.0).astype(np.float32)
    return U


def _run(nc, in_maps):
    res = run_bass_kernel_spmd(nc, in_maps, core_ids=list(range(8)))
    return res.results


def kernel(L0_attn_X, L0_attn_Wq, L0_attn_bq, L0_attn_Wk, L0_attn_Wv,
           L0_attn_Wproj, L0_attn_bproj, L0_attn_Wr, L0_attn_br,
           L0_attn_b_nd, L0_ffn_weights, L0_ffn_scale, L0_ffn_bias,
           L1_attn_wq, L1_attn_w_kv_down, L1_attn_w_k_up, L1_attn_w_v_up,
           L1_attn_wo, L1_ffn_Wg, L1_ffn_bg, L1_ffn_Wgu, L1_ffn_bgu,
           L1_ffn_Wd, L1_ffn_bd):
    progs = _programs()
    C = _consts()
    f32 = np.float32
    asf = np.ascontiguousarray
    X = np.asarray(L0_attn_X, f32)
    ncores = 8

    U = _utables(np.asarray(L0_attn_b_nd, f32))
    WfT = asf(np.asarray(L0_ffn_weights, f32).T)
    bqc = asf(np.asarray(L0_attn_bq, f32).reshape(8, 128).T)
    br1 = np.asarray(L0_attn_br, f32).reshape(1, 160)
    scb = asf(np.broadcast_to(np.asarray(L0_ffn_scale, f32), (128, DM)))
    bib = asf(np.broadcast_to(np.asarray(L0_ffn_bias, f32), (128, DM)))
    bp1 = np.asarray(L0_attn_bproj, f32).reshape(1, DM)
    wqT = asf(np.asarray(L1_attn_wq, f32).T)
    kvdT = asf(np.asarray(L1_attn_w_kv_down, f32).T)
    kupT = asf(np.asarray(L1_attn_w_k_up, f32).T)
    vupT = asf(np.asarray(L1_attn_w_v_up, f32).T)
    woT = asf(np.asarray(L1_attn_wo, f32).T)
    WgT = asf(np.asarray(L1_ffn_Wg, f32).T)
    bg1 = np.asarray(L1_ffn_bg, f32).reshape(1, 16)
    perm = np.concatenate([np.arange(0, 2048, 2), np.arange(1, 2048, 2)])
    Wgu_p = asf(np.asarray(L1_ffn_Wgu, f32)[:, :, perm])
    bgu_p = asf(np.asarray(L1_ffn_bgu, f32)[:, perm])
    Wd = np.asarray(L1_ffn_Wd, f32)
    bd = np.asarray(L1_ffn_bd, f32)

    shared_A = dict(Wq=np.asarray(L0_attn_Wq, f32), Wk=np.asarray(L0_attn_Wk, f32),
                    Wv=np.asarray(L0_attn_Wv, f32), Wr=np.asarray(L0_attn_Wr, f32),
                    WfT=WfT, bqc=bqc, br1=br1, scb=scb, bib=bib,
                    ident=C["ident"])
    maps = [dict(x=asf(X[c]), **shared_A) for c in range(ncores)]
    rA = _run(progs["A"], maps)
    rB = [dict() for _ in range(ncores)]
    for bi, nc in enumerate(progs["B"]):
        maps = [dict(QT=rA[c]["QT"], KT=rA[c]["KT"], V=rA[c]["V"],
                     R=rA[c]["R"], U=U, MV0=C["MV0"], MV8=C["MV8"],
                     ident=C["ident"]) for c in range(ncores)]
        out = _run(nc, maps)
        for c in range(ncores):
            rB[c].setdefault("AT", np.zeros((NT, 128, 8, 128), BF_NP))
        los = ((0, 14), (14, 20), (20, 26), (26, 32))[bi]
        for c in range(ncores):
            rB[c]["AT"][los[0]:los[1]] = out[c]["AT"][los[0]:los[1]]
    maps = [dict(AT=rB[c]["AT"], Wp=np.asarray(L0_attn_Wproj, f32), bp1=bp1,
                 x=asf(X[c]), G=rA[c]["G"], ident=C["ident"])
            for c in range(ncores)]
    rC = _run(progs["C"], maps)
    rD = [dict(QT1=np.zeros((NT, 128, 8, 128), BF_NP),
               KT1=np.zeros((NT, 128, 8, 128), BF_NP),
               V1=np.zeros((NT, 128, DM), BF_NP),
               CW=np.zeros((NT, 128, 16), f32)) for _ in range(ncores)]
    for di, nc in enumerate(progs["D"]):
        maps = [dict(XN1T=rC[c]["XN1T"], wqT=wqT, kvdT=kvdT, kupT=kupT,
                     vupT=vupT, WgT=WgT, bg1=bg1, cosT=C["cosT"],
                     sinT=C["sinT"], SW=C["SW"]) for c in range(ncores)]
        out = _run(nc, maps)
        lo, hi = ((0, 16), (16, 32))[di]
        for c in range(ncores):
            for kk in ("QT1", "KT1", "V1", "CW"):
                rD[c][kk][lo:hi] = out[c][kk][lo:hi]
    ATN = [np.zeros((NT, 128, 8, 128), BF_NP) for _ in range(ncores)]
    for ei, nc in enumerate(progs["E"]):
        maps = [dict(QT1=rD[c]["QT1"], KT1=rD[c]["KT1"], V1=rD[c]["V1"],
                     M01=C["M01"], ones64=C["ones64"]) for c in range(ncores)]
        out = _run(nc, maps)
        hlo = (0, 4, 8, 12)[ei]
        for c in range(ncores):
            for h in range(hlo, hlo + 4):
                hp = 64 * (h % 2)
                hm = h // 2
                ATN[c][:, hp:hp + 64, hm, :] = \
                    out[c]["ATN"][:, hp:hp + 64, hm, :]
    maps = [dict(ATN=ATN[c], woT=woT) for c in range(ncores)]
    rEP = _run(progs["EP"], maps)
    acc = [np.zeros((T, DM), f32) for _ in range(ncores)]
    outF = None
    for g in range(4):
        maps = []
        for c in range(ncores):
            m = dict(XN1T=rC[c]["XN1T"], CW=rD[c]["CW"], ACCI=acc[c],
                     Wgu4=asf(Wgu_p[4 * g:4 * g + 4]),
                     bgu4=asf(bgu_p[4 * g:4 * g + 4]),
                     Wd4=asf(Wd[4 * g:4 * g + 4]),
                     bd4=asf(bd[4 * g:4 * g + 4]))
            if g == 3:
                m["X1"] = rC[c]["X1"]
                m["S1"] = rEP[c]["S1"]
            maps.append(m)
        out = _run(progs["F"][g], maps)
        for c in range(ncores):
            acc[c] = out[c]["ACCO"]
        if g == 3:
            outF = out
    res = np.stack([outF[c]["OUT"] for c in range(ncores)], axis=0)
    return res.astype(np.float32)


# version 02
# speedup vs baseline: 1.0384x; 1.0384x over previous
import math
import numpy as np
import ml_dtypes
import concourse.bass as bass
import concourse.mybir as mybir
from concourse.bass_utils import run_bass_kernel_spmd

F32 = mybir.dt.float32
BF = mybir.dt.bfloat16
AF = mybir.ActivationFunctionType
OP = mybir.AluOpType
AX = mybir.AxisListType

BF_NP = ml_dtypes.bfloat16
T = 4096
DM = 1024
NH = 16
NBI = 10
NT = T // 128
BF_COLS = 40960
F32_COLS = 16384


class Prog:
    def __init__(self):
        self.nc = bass.Bass("TRN2", target_bir_lowering=False)
        self.sched = []
        self.t = 0
        self.chain = self.nc.alloc_semaphore("chain")
        self._uid = 0
        self._back = {
            BF: self.nc.alloc_sbuf_tensor("arena_bf", [128, BF_COLS], BF),
            F32: self.nc.alloc_sbuf_tensor("arena_f32", [128, F32_COLS], F32),
        }
        self._ptr = {BF: 0, F32: 0}
        self._persist = {BF: 0, F32: 0}

    def inp(self, name, shape, dt=F32):
        return self.nc.dram_tensor(name, shape, dt, kind="ExternalInput").ap()

    def outp(self, name, shape, dt=F32):
        return self.nc.dram_tensor(name, shape, dt, kind="ExternalOutput").ap()

    def scratch(self, name, shape, dt=F32):
        return self.nc.dram_tensor(name, shape, dt, kind="Internal").ap()

    def sb(self, shape, dt=F32):
        cols = (shape[1] + 15) // 16 * 16
        off = self._ptr[dt]
        self._ptr[dt] = off + cols
        assert self._ptr[dt] <= self._back[dt].shape[1], \
            "arena overflow %s %d" % (dt, self._ptr[dt])
        return self._back[dt][0:shape[0], off:off + shape[1]]

    def ps(self, shape, dt=F32):
        self._uid += 1
        return self.nc.alloc_psum_tensor("ps%d" % self._uid, shape, dt)

    def mark_persist(self):
        self._persist = dict(self._ptr)

    def stage_reset(self):
        self._ptr = dict(self._persist)

    def _op(self, eng, fn, inc=1, wait=True):
        self.sched.append((eng, fn, self.t if wait else -1, inc))
        self.t += inc

    def dma(self, out, in_):
        def f(e, o=out, i=in_):
            return e.dma_start(out=o, in_=i)
        self._op("sync", f, inc=16)

    def mm(self, out, lhsT, rhs, start=True, stop=True, wait=True, inc=1):
        def f(e, o=out, l=lhsT, r=rhs, st=start, sp=stop):
            return e.matmul(o, l, r, start=st, stop=sp, skip_group_check=True)
        self._op("tensor", f, inc=inc, wait=wait)

    def trans(self, out, in_, ident, wait=True, inc=1):
        def f(e, o=out, i=in_, d=ident):
            return e.transpose(o, i, d)
        self._op("tensor", f, inc=inc, wait=wait)

    def tt(self, out, in0, in1, op):
        def f(e, o=out, a=in0, b=in1, p=op):
            return e.tensor_tensor(o, a, b, p)
        self._op("vector", f)

    def ts(self, out, in0, s1, s2, op0, op1):
        def f(e, o=out, a=in0, x=s1, y=s2, p0=op0, p1=op1):
            return e.tensor_scalar(o, a, x, y, p0, p1)
        self._op("vector", f)

    def ts1(self, out, in0, s1, op0):
        if op0 is OP.mult:
            def f(e, o=out, a=in0, x=s1):
                return e.tensor_scalar_mul(o, a, x)
        elif op0 is OP.add:
            def f(e, o=out, a=in0, x=s1):
                return e.tensor_scalar_add(o, a, x)
        elif op0 is OP.min:
            def f(e, o=out, a=in0, x=s1):
                return e.tensor_scalar_min(o, a, x)
        else:
            def f(e, o=out, a=in0, x=s1, p=op0):
                return e.tensor_scalar(o, a, x, None, p)
        self._op("vector", f)

    def stt(self, out, in0, scalar, in1, op0, op1):
        def f(e, o=out, a=in0, s=scalar, b=in1, p0=op0, p1=op1):
            return e.scalar_tensor_tensor(o, a, s, b, p0, p1)
        self._op("vector", f)

    def red(self, out, in_, op):
        def f(e, o=out, i=in_, p=op):
            return e.tensor_reduce(o, i, AX.X, p)
        self._op("vector", f)

    def recip(self, out, in_):
        def f(e, o=out, i=in_):
            return e.reciprocal(o, i)
        self._op("vector", f)

    def memset(self, ap, c):
        def f(e, a=ap, v=c):
            return e.memset(a, v)
        self._op("vector", f)

    def act(self, out, in_, func, bias=None, scale=None):
        def f(e, o=out, i=in_, fu=func, b=bias, s=scale):
            kw = {}
            if b is not None:
                kw["bias"] = b
            if s is not None:
                kw["scale"] = s
            return e.activation(o, i, fu, **kw)
        self._op("scalar", f)

    def copy(self, out, in_):
        def f(e, o=out, i=in_):
            return e.copy(o, i)
        self._op("scalar", f)

    def sign(self, out, in_):
        def f(e, o=out, i=in_):
            return e.sign(o, i)
        self._op("scalar", f)

    def finalize(self):
        nc = self.nc
        chain = self.chain
        total = self.t
        self.sched.append(
            ("sync", lambda e, c=chain, v=total: e.wait_ge(c, v), -1, 0))
        with nc.Block() as blk:
            for engname in ("sync", "tensor", "vector", "scalar", "gpsimd"):
                entries = [(f, w, i) for (e, f, w, i) in self.sched
                           if e == engname]
                if not entries:
                    continue

                def mk(entries, c=chain):
                    def body(eng):
                        for (f, w, i) in entries:
                            if w > 0:
                                eng.wait_ge(c, w)
                            ins = f(eng)
                            if i:
                                ins.then_inc(c, i)
                    return body
                getattr(blk, engname)(mk(entries))
        return nc

    def setup_consts(self):
        vals = [0.0, 1.0, -1.0, 1e-5, 1.0 / 1024, -1.0 / 1024, 0.125,
                1.702, 1.0 / 1.702, 7.0, -7.0, 100.0, -100.0, -1e30]
        cs = self.sb([128, len(vals)])
        cb = self.sb([128, len(vals)], BF)
        for j, v in enumerate(vals):
            self.memset(cs[:, j:j + 1], v)
            self.memset(cb[:, j:j + 1], v)
            self.nc.const_aps.register(F32, float(v), cs[:, j:j + 1])
            self.nc.const_aps.register(BF, float(v), cb[:, j:j + 1])

    def load_w_bf16(self, src_ap, rows, cols, dst_sb, stage, do_sign=False):
        nk = rows // 128
        for k in range(nk):
            self.dma(stage[:, 0:cols], src_ap[k * 128:(k + 1) * 128, :])
            if do_sign:
                self.sign(dst_sb[:, k * cols:(k + 1) * cols], stage[:, 0:cols])
            else:
                self.copy(dst_sb[:, k * cols:(k + 1) * cols], stage[:, 0:cols])

    def layernorm(self, x_sb, out_bf, tmp_xc, tmp_sq, st1, st2):
        self.red(st1[:, 0:1], x_sb[:, 0:DM], OP.add)
        self.ts1(st1[:, 1:2], st1[:, 0:1], -1.0 / DM, OP.mult)
        self.ts1(tmp_xc[:, 0:DM], x_sb[:, 0:DM], st1[:, 1:2], OP.add)
        self.tt(tmp_sq[:, 0:DM], tmp_xc[:, 0:DM], tmp_xc[:, 0:DM], OP.mult)
        self.red(st2[:, 0:1], tmp_sq[:, 0:DM], OP.add)
        self.act(st2[:, 1:2], st2[:, 0:1], AF.Sqrt, bias=1e-5, scale=1.0 / DM)
        self.recip(st2[:, 2:3], st2[:, 1:2])
        self.ts1(out_bf[:, 0:DM], tmp_xc[:, 0:DM], st2[:, 2:3], OP.mult)


def emit_A(p, V, PSF, PSB, id_sb, ones):
    stage = p.sb([128, DM])
    wq_sb = p.sb([128, 8 * DM], BF)
    wk_sb = p.sb([128, 8 * DM], BF)
    wv_sb = p.sb([128, 8 * DM], BF)
    wr_sb = p.sb([128, 8 * 160], BF)
    wf_sb = p.sb([128, 8 * DM], BF)
    bq_sb = p.sb([128, 8])
    bq8 = p.sb([128, 8])
    brf = p.sb([1, 160])
    brb = p.sb([1, 160], BF)
    sc_sb = p.sb([128, DM])
    bi_sb = p.sb([128, DM])

    p.load_w_bf16(V["Wq"], DM, DM, wq_sb, stage)
    p.load_w_bf16(V["Wk"], DM, DM, wk_sb, stage)
    p.load_w_bf16(V["Wv"], DM, DM, wv_sb, stage)
    p.load_w_bf16(V["Wr"], DM, 160, wr_sb, stage)
    p.load_w_bf16(V["WfT"], DM, DM, wf_sb, stage, do_sign=True)
    p.dma(bq_sb[:, :], V["bqc"])
    p.ts1(bq8[:, 0:8], bq_sb[:, 0:8], 0.125, OP.mult)
    p.dma(brf[0:1, :], V["br1"])
    p.copy(brb[0:1, :], brf[0:1, :])
    p.dma(sc_sb[:, :], V["scb"])
    p.dma(bi_sb[:, :], V["bib"])

    x_sb = p.sb([128, DM])
    xc = p.sb([128, DM])
    sq = p.sb([128, DM])
    st1 = p.sb([128, 4])
    st2 = p.sb([128, 4])
    xn = p.sb([128, DM], BF)
    xnT = p.sb([128, DM], BF)
    qt_sb = p.sb([128, DM], BF)
    kt_sb = p.sb([128, DM], BF)
    v_sb = p.sb([128, DM], BF)
    r_sb = p.sb([128, 160])
    y_sb = p.sb([128, DM])
    g_sb = p.sb([128, DM])
    pst = PSB[:, 0:128]
    ps1 = PSF[:, 0:128]
    ps5 = PSF[:, 512:1024]
    psr = PSF[:, 1024:1184]

    for i in range(NT):
        p.dma(x_sb[:, :], V["x"][i * 128:(i + 1) * 128, :])
        p.layernorm(x_sb, xn, xc, sq, st1, st2)
        for k in range(8):
            p.trans(pst[:, :], xn[:, k * 128:(k + 1) * 128], id_sb[:, :])
            p.copy(xnT[:, k * 128:(k + 1) * 128], pst[:, :])
        for (wsb, dst, biased) in ((wq_sb, qt_sb, True), (wk_sb, kt_sb, False)):
            for m in range(8):
                for k in range(8):
                    p.mm(ps1[:, :],
                         wsb[:, k * DM + m * 128:k * DM + (m + 1) * 128],
                         xnT[:, k * 128:(k + 1) * 128],
                         start=(k == 0), stop=(k == 7),
                         wait=(k == 0), inc=(1 if k == 7 else 0))
                if biased:
                    p.ts(dst[:, m * 128:(m + 1) * 128], ps1[:, :], 0.125,
                         bq8[:, m:m + 1], OP.mult, OP.add)
                else:
                    p.copy(dst[:, m * 128:(m + 1) * 128], ps1[:, :])
        for nb in range(2):
            for k in range(8):
                p.mm(ps5[:, :], xnT[:, k * 128:(k + 1) * 128],
                     wv_sb[:, k * DM + nb * 512:k * DM + nb * 512 + 512],
                     start=(k == 0), stop=(k == 7),
                     wait=(k == 0), inc=(1 if k == 7 else 0))
            p.copy(v_sb[:, nb * 512:(nb + 1) * 512], ps5[:, :])
        for k in range(8):
            p.mm(psr[:, :], xnT[:, k * 128:(k + 1) * 128],
                 wr_sb[:, k * 160:(k + 1) * 160],
                 start=(k == 0), stop=False,
                 wait=(k == 0), inc=0)
        p.mm(psr[:, :], ones[0:1, 0:128], brb[0:1, :], start=False, stop=True,
             wait=False, inc=1)
        p.copy(r_sb[:, :], psr[:, :])
        for nb in range(2):
            for k in range(8):
                p.mm(ps5[:, :], xnT[:, k * 128:(k + 1) * 128],
                     wf_sb[:, k * DM + nb * 512:k * DM + nb * 512 + 512],
                     start=(k == 0), stop=(k == 7),
                     wait=(k == 0), inc=(1 if k == 7 else 0))
            p.copy(y_sb[:, nb * 512:(nb + 1) * 512], ps5[:, :])
        p.tt(y_sb[:, 0:DM], y_sb[:, 0:DM], sc_sb[:, 0:DM], OP.mult)
        p.tt(y_sb[:, 0:DM], y_sb[:, 0:DM], bi_sb[:, 0:DM], OP.add)
        p.ts(y_sb[:, 0:DM], y_sb[:, 0:DM], 100.0, -100.0, OP.min, OP.max)
        p.act(g_sb[:, 0:DM], y_sb[:, 0:DM], AF.Silu)
        p.dma(V["QT"][i], qt_sb[:, 0:DM].rearrange("p (m c) -> p m c", m=8))
        p.dma(V["KT"][i], kt_sb[:, 0:DM].rearrange("p (m c) -> p m c", m=8))
        p.dma(V["V"][i], v_sb[:, :])
        p.dma(V["R"][i], r_sb[:, :])
        p.dma(V["G"][i], g_sb[:, :])


def emit_B(p, V, PSF, PSB, id_sb, ones):
    u_sb = p.sb([128, NBI * 1152])
    for n in range(NBI):
        p.dma(u_sb[:, n * 1152:(n + 1) * 1152], V["U"][n])
    m0_sb = p.sb([128, 128])
    m8_sb = p.sb([128, 128])
    p.dma(m0_sb[:, :], V["MV0"])
    p.dma(m8_sb[:, :], V["MV8"])

    qt_sb = p.sb([128, DM], BF)
    r_sb = p.sb([128, 160])
    ktw = p.sb([128, 8 * 1152], BF)
    vw = p.sb([128, 9 * DM], BF)
    buf = p.sb([128, 1152])
    buf2 = p.sb([128, 1152])
    P_sb = p.sb([128, 1152], BF)
    PT_sb = p.sb([128, 1152], BF)
    at_sb = p.sb([128, DM], BF)
    st = p.sb([128, 4])
    ps_s = PSF[:, 0:1536]
    ptp = PSB[:, 0:1152]
    av = PSF[0:64, 0:128]

    for i in range(NT):
        glo = max(0, i - 8)
        nt = i - glo + 1
        W = nt * 128
        u_off = (9 - nt) * 128
        p.dma(qt_sb[:, 0:DM].rearrange("p (m c) -> p m c", m=8), V["QT"][i])
        p.dma(r_sb[:, :], V["R"][i])
        for m in range(8):
            p.dma(ktw[:, m * 1152:m * 1152 + W]
                  .rearrange("p (i t) -> p i t", i=nt),
                  V["KT"][glo:i + 1, :, m, :].rearrange("i p t -> p i t"))
        for it, gt in enumerate(range(glo, i + 1)):
            p.dma(vw[:, it * DM:(it + 1) * DM], V["V"][gt])
        for h in range(NH):
            hp = 64 * (h % 2)
            hm = h // 2
            p.ts1(buf[:, 0:W],
                  u_sb[:, 0 * 1152 + u_off:0 * 1152 + u_off + W],
                  r_sb[:, h * 10:h * 10 + 1], OP.mult)
            for n in range(1, NBI):
                p.stt(buf[:, 0:W],
                      u_sb[:, n * 1152 + u_off:n * 1152 + u_off + W],
                      r_sb[:, h * 10 + n:h * 10 + n + 1],
                      buf[:, 0:W], OP.mult, OP.add)
            for c0 in range(0, W, 512):
                w = min(512, W - c0)
                p.mm(ps_s[:, c0:c0 + w],
                     qt_sb[hp:hp + 64, hm * 128:(hm + 1) * 128],
                     ktw[hp:hp + 64, hm * 1152 + c0:hm * 1152 + c0 + w])
            for c0 in range(0, W, 512):
                w = min(512, W - c0)
                p.tt(buf[:, c0:c0 + w], ps_s[:, c0:c0 + w],
                     buf[:, c0:c0 + w], OP.add)
            p.tt(buf[:, W - 128:W], buf[:, W - 128:W], m0_sb[:, :], OP.add)
            if nt == 9:
                p.tt(buf[:, 0:128], buf[:, 0:128], m8_sb[:, :], OP.add)
            p.red(st[:, 0:1], buf[:, 0:W], OP.max)
            p.ts1(st[:, 1:2], st[:, 0:1], -1.0, OP.mult)
            p.act(buf2[:, 0:W], buf[:, 0:W], AF.Exp, bias=st[:, 1:2],
                  scale=1.0)
            p.red(st[:, 2:3], buf2[:, 0:W], OP.add)
            p.recip(st[:, 3:4], st[:, 2:3])
            p.ts1(P_sb[:, 0:W], buf2[:, 0:W], st[:, 3:4], OP.mult)
            for kt in range(nt):
                p.trans(ptp[:, kt * 128:(kt + 1) * 128],
                        P_sb[:, kt * 128:(kt + 1) * 128], id_sb[:, :],
                        wait=(kt == 0), inc=(1 if kt == nt - 1 else 0))
            for kt in range(nt):
                p.copy(PT_sb[:, kt * 128:(kt + 1) * 128],
                       ptp[:, kt * 128:(kt + 1) * 128])
            for kt in range(nt):
                p.mm(av[0:64, :],
                     vw[:, kt * DM + h * 64:kt * DM + h * 64 + 64],
                     PT_sb[:, kt * 128:(kt + 1) * 128],
                     start=(kt == 0), stop=(kt == nt - 1),
                     wait=(kt == 0), inc=(1 if kt == nt - 1 else 0))
            p.copy(at_sb[hp:hp + 64, hm * 128:(hm + 1) * 128], av[0:64, :])
        p.dma(V["AT"][i], at_sb[:, 0:DM].rearrange("p (m c) -> p m c", m=8))


def emit_C(p, V, PSF, PSB, id_sb, ones):
    stage = p.sb([128, DM])
    wp_sb = p.sb([128, 8 * DM], BF)
    p.load_w_bf16(V["Wp"], DM, DM, wp_sb, stage)
    bpf = p.sb([1, DM])
    bpb = p.sb([1, DM], BF)
    p.dma(bpf[0:1, :], V["bp1"])
    p.copy(bpb[0:1, :], bpf[0:1, :])

    at_sb = p.sb([128, DM], BF)
    s_sb = p.sb([128, DM])
    x_sb = p.sb([128, DM])
    g_sb = p.sb([128, DM])
    x1_sb = p.sb([128, DM])
    xc = p.sb([128, DM])
    sq = p.sb([128, DM])
    st1 = p.sb([128, 4])
    st2 = p.sb([128, 4])
    xn = p.sb([128, DM], BF)
    xnT = p.sb([128, DM], BF)
    ps5 = PSF[:, 0:512]
    pst = PSB[:, 0:128]

    for i in range(NT):
        p.dma(at_sb[:, 0:DM].rearrange("p (m c) -> p m c", m=8), V["AT"][i])
        for nb in range(2):
            for m in range(8):
                p.mm(ps5[:, :], at_sb[:, m * 128:(m + 1) * 128],
                     wp_sb[:, m * DM + nb * 512:m * DM + nb * 512 + 512],
                     start=(m == 0), stop=False,
                     wait=(m == 0), inc=0)
            p.mm(ps5[:, :], ones[0:1, 0:128],
                 bpb[0:1, nb * 512:nb * 512 + 512], start=False, stop=True,
                 wait=False, inc=1)
            p.copy(s_sb[:, nb * 512:(nb + 1) * 512], ps5[:, :])
        p.dma(x_sb[:, :], V["x"][i * 128:(i + 1) * 128, :])
        p.dma(g_sb[:, :], V["G"][i])
        p.tt(s_sb[:, 0:DM], s_sb[:, 0:DM], g_sb[:, 0:DM], OP.mult)
        p.tt(x1_sb[:, 0:DM], x_sb[:, 0:DM], s_sb[:, 0:DM], OP.add)
        p.dma(V["X1"][i * 128:(i + 1) * 128, :], x1_sb[:, :])
        p.layernorm(x1_sb, xn, xc, sq, st1, st2)
        for k in range(8):
            p.trans(pst[:, :], xn[:, k * 128:(k + 1) * 128], id_sb[:, :])
            p.copy(xnT[:, k * 128:(k + 1) * 128], pst[:, :])
        p.dma(V["XN1T"][i], xnT[:, 0:DM].rearrange("p (m c) -> p m c", m=8))


def emit_D(p, V, PSF, PSB, id_sb, ones):
    stage = p.sb([128, DM])
    wq_sb = p.sb([128, 8 * DM], BF)
    kvd_sb = p.sb([128, 8 * 512], BF)
    kup_sb = p.sb([128, 4 * DM], BF)
    vup_sb = p.sb([128, 4 * DM], BF)
    wg_sb = p.sb([128, 8 * 16], BF)
    p.load_w_bf16(V["wqT"], DM, DM, wq_sb, stage)
    p.load_w_bf16(V["kvdT"], DM, 512, kvd_sb, stage)
    p.load_w_bf16(V["kupT"], 512, DM, kup_sb, stage)
    p.load_w_bf16(V["vupT"], 512, DM, vup_sb, stage)
    p.load_w_bf16(V["WgT"], DM, 16, wg_sb, stage)
    bgf = p.sb([1, 16])
    bgb = p.sb([1, 16], BF)
    p.dma(bgf[0:1, :], V["bg1"])
    p.copy(bgb[0:1, :], bgf[0:1, :])
    cos_sb = p.sb([128, T])
    sin_sb = p.sb([128, T])
    p.dma(cos_sb[:, :], V["cosT"])
    p.dma(sin_sb[:, :], V["sinT"])
    sw_sb = p.sb([128, 128], BF)
    p.dma(sw_sb[:, :], V["SW"])

    xt_sb = p.sb([128, DM], BF)
    pre_sb = p.sb([128, DM], BF)
    q_sb = p.sb([128, DM], BF)
    k_sb = p.sb([128, DM], BF)
    c_sb = p.sb([128, 512], BF)
    v_sb = p.sb([128, DM], BF)
    tt1 = p.sb([128, 128])
    tt2 = p.sb([128, 128])
    lg = p.sb([128, 16])
    cwb = p.sb([128, 16])
    st = p.sb([128, 8])
    ps1 = PSF[:, 0:128]
    psw = PSF[:, 512:640]
    ps5 = PSF[:, 1024:1536]

    def rope(src_sb, dst_sb, i):
        for m in range(8):
            p.mm(psw[:, :], sw_sb[:, :], src_sb[:, m * 128:(m + 1) * 128])
            p.tt(tt1[:, :], src_sb[:, m * 128:(m + 1) * 128],
                 cos_sb[:, i * 128:(i + 1) * 128], OP.mult)
            p.tt(tt2[:, :], psw[:, :], sin_sb[:, i * 128:(i + 1) * 128],
                 OP.mult)
            p.tt(dst_sb[:, m * 128:(m + 1) * 128], tt1[:, :], tt2[:, :],
                 OP.add)

    for i in range(NT):
        p.dma(xt_sb[:, 0:DM].rearrange("p (m c) -> p m c", m=8), V["XN1T"][i])
        for m in range(8):
            for k in range(8):
                p.mm(ps1[:, :],
                     wq_sb[:, k * DM + m * 128:k * DM + (m + 1) * 128],
                     xt_sb[:, k * 128:(k + 1) * 128],
                     start=(k == 0), stop=(k == 7),
                     wait=(k == 0), inc=(1 if k == 7 else 0))
            p.copy(pre_sb[:, m * 128:(m + 1) * 128], ps1[:, :])
        rope(pre_sb, q_sb, i)
        for m in range(4):
            for k in range(8):
                p.mm(ps1[:, :],
                     kvd_sb[:, k * 512 + m * 128:k * 512 + (m + 1) * 128],
                     xt_sb[:, k * 128:(k + 1) * 128],
                     start=(k == 0), stop=(k == 7),
                     wait=(k == 0), inc=(1 if k == 7 else 0))
            p.copy(c_sb[:, m * 128:(m + 1) * 128], ps1[:, :])
        for m in range(8):
            for kc in range(4):
                p.mm(ps1[:, :],
                     kup_sb[:, kc * DM + m * 128:kc * DM + (m + 1) * 128],
                     c_sb[:, kc * 128:(kc + 1) * 128],
                     start=(kc == 0), stop=(kc == 3),
                     wait=(kc == 0), inc=(1 if kc == 3 else 0))
            p.copy(pre_sb[:, m * 128:(m + 1) * 128], ps1[:, :])
        rope(pre_sb, k_sb, i)
        p.dma(V["KT1"][i], k_sb[:, 0:DM].rearrange("p (m c) -> p m c", m=8))
        for nb in range(2):
            for kc in range(4):
                p.mm(ps5[:, :], c_sb[:, kc * 128:(kc + 1) * 128],
                     vup_sb[:, kc * DM + nb * 512:kc * DM + nb * 512 + 512],
                     start=(kc == 0), stop=(kc == 3),
                     wait=(kc == 0), inc=(1 if kc == 3 else 0))
            p.copy(v_sb[:, nb * 512:(nb + 1) * 512], ps5[:, :])
        for k in range(8):
            p.mm(ps1[:, 0:16], xt_sb[:, k * 128:(k + 1) * 128],
                 wg_sb[:, k * 16:(k + 1) * 16],
                 start=(k == 0), stop=False, wait=(k == 0), inc=0)
        p.mm(ps1[:, 0:16], ones[0:1, 0:128], bgb[0:1, :], start=False,
             stop=True, wait=False, inc=1)
        p.copy(lg[:, :], ps1[:, 0:16])
        p.red(st[:, 0:1], lg[:, 0:16], OP.max)
        p.ts1(st[:, 1:2], st[:, 0:1], -1.0, OP.mult)
        p.ts1(cwb[:, 0:16], lg[:, 0:16], st[:, 0:1], OP.is_ge)
        p.stt(cwb[:, 0:16], cwb[:, 0:16], -1e30, lg[:, 0:16], OP.mult, OP.add)
        p.red(st[:, 2:3], cwb[:, 0:16], OP.max)
        p.ts1(cwb[:, 0:16], lg[:, 0:16], st[:, 2:3], OP.is_ge)
        p.act(lg[:, 0:16], lg[:, 0:16], AF.Exp, bias=st[:, 1:2], scale=1.0)
        p.tt(lg[:, 0:16], lg[:, 0:16], cwb[:, 0:16], OP.mult)
        p.red(st[:, 3:4], lg[:, 0:16], OP.add)
        p.recip(st[:, 4:5], st[:, 3:4])
        p.ts1(cwb[:, 0:16], lg[:, 0:16], st[:, 4:5], OP.mult)
        p.dma(V["QT1"][i], q_sb[:, 0:DM].rearrange("p (m c) -> p m c", m=8))
        p.dma(V["V1"][i], v_sb[:, :])
        p.dma(V["CW"][i], cwb[:, :])


def emit_E(p, V, PSF, PSB, id_sb, ones):
    m_sb = p.sb([128, 4 * 512], BF)
    for v in range(4):
        p.dma(m_sb[:, v * 512:(v + 1) * 512], V["M01"][v])
    on_sb = p.sb([1, 64])
    p.dma(on_sb[0:1, :], V["ones64"])

    qb_sb = p.sb([64, 512], BF)
    kt_sb = p.sb([64, 128], BF)
    vext = p.sb([128, 65], BF)
    p.memset(vext[:, 64:65], 1.0)
    pt_sb = p.sb([128, 512], BF)
    rs_sb = p.sb([1, 512])
    bc_sb = p.sb([64, 512])
    atn_sb = p.sb([64, 512], BF)
    ps_s = PSF[:, 0:512]
    av = PSF[:, 512:1024]
    bc = PSF[0:64, 1024:1536]

    for h in range(NH):
        hp = 64 * (h % 2)
        hm = h // 2
        for qb in range(8):
            p.dma(qb_sb[0:64, 0:512].rearrange("p (i t) -> p i t", i=4),
                  V["QT1"][4 * qb:4 * qb + 4, hp:hp + 64, hm, :]
                  .rearrange("i p t -> p i t"))
            last = 4 * qb + 3
            for kt in range(4 * qb + 4):
                p.dma(kt_sb[0:64, :], V["KT1"][kt, hp:hp + 64, hm, :])
                p.dma(vext[:, 0:64], V["V1"][kt, :, h * 64:h * 64 + 64])
                p.mm(ps_s[:, :], kt_sb[0:64, :], qb_sb[0:64, :])
                p.act(pt_sb[:, :], ps_s[:, :], AF.Exp, scale=0.125)
                if kt >= 4 * qb:
                    vv = kt - 4 * qb
                    p.tt(pt_sb[:, :], pt_sb[:, :],
                         m_sb[:, vv * 512:(vv + 1) * 512], OP.mult)
                p.mm(av[0:65, :], vext[:, 0:65], pt_sb[:, :],
                     start=(kt == 0), stop=(kt == last),
                     wait=True, inc=1)
            p.recip(rs_sb[0:1, :], av[64:65, :])
            p.mm(bc[0:64, :], on_sb[0:1, :], rs_sb[0:1, :])
            p.copy(bc_sb[:, :], bc[0:64, :])
            p.tt(atn_sb[:, :], av[0:64, :], bc_sb[:, :], OP.mult)
            p.dma(V["ATN"][4 * qb:4 * qb + 4, hp:hp + 64, hm, :]
                  .rearrange("i p t -> p i t"),
                  atn_sb[0:64, 0:512].rearrange("p (i t) -> p i t", i=4))


def emit_EP(p, V, PSF, PSB, id_sb, ones):
    stage = p.sb([128, DM])
    wo_sb = p.sb([128, 8 * DM], BF)
    p.load_w_bf16(V["woT"], DM, DM, wo_sb, stage)
    a_sb = p.sb([128, DM], BF)
    s_sb = p.sb([128, DM])
    ps5 = PSF[:, 0:512]
    for i in range(NT):
        p.dma(a_sb[:, 0:DM].rearrange("p (m c) -> p m c", m=8), V["ATN"][i])
        for nb in range(2):
            for m in range(8):
                p.mm(ps5[:, :], a_sb[:, m * 128:(m + 1) * 128],
                     wo_sb[:, m * DM + nb * 512:m * DM + nb * 512 + 512],
                     start=(m == 0), stop=(m == 7),
                     wait=(m == 0), inc=(1 if m == 7 else 0))
            p.copy(s_sb[:, nb * 512:(nb + 1) * 512], ps5[:, :])
        p.dma(V["S1"][i * 128:(i + 1) * 128, :], s_sb[:, :])


def emit_F(p, V, PSF, PSB, id_sb, ones):
    stage = p.sb([128, 2048])
    wgu_sb = p.sb([128, 8 * 2048], BF)
    wd_sb = p.sb([128, 8 * DM], BF)
    bguf = p.sb([1, 2048])
    bgub = p.sb([1, 2048], BF)
    bdf = p.sb([1, DM])
    bdb = p.sb([1, DM], BF)

    xb_sb = p.sb([128, 8 * 512], BF)
    acc_sb = p.sb([128, 4 * DM])
    cw_sb = p.sb([128, 4 * 16])
    ga = p.sb([128, 512])
    gb = p.sb([128, 512])
    ua = p.sb([128, 512])
    ub = p.sb([128, 512])
    gated = p.sb([128, 8 * 512], BF)
    x1t = p.sb([128, DM])
    s1t = p.sb([128, DM])
    g1 = p.sb([128, DM])
    o_sb = p.sb([128, DM])
    pg = PSF[:, 0:512]
    pu = PSF[:, 512:1024]
    pd = PSF[:, 0:1024]

    for tb in range(8):
        for k in range(8):
            p.dma(xb_sb[:, k * 512:(k + 1) * 512]
                  .rearrange("p (i t) -> p i t", i=4),
                  V["XN1T"][4 * tb:4 * tb + 4, :, k, :]
                  .rearrange("i p t -> p i t"))
        p.memset(acc_sb[:, 0:4 * DM], 0.0)
        for t in range(4):
            p.dma(cw_sb[:, t * 16:(t + 1) * 16], V["CW"][4 * tb + t])
        for ge in range(16):
            for k in range(8):
                p.dma(stage[:, 0:2048], V["Wgu"][ge, k * 128:(k + 1) * 128, :])
                p.copy(wgu_sb[:, k * 2048:(k + 1) * 2048], stage[:, 0:2048])
            for k in range(8):
                p.dma(stage[:, 0:DM], V["Wd"][ge, k * 128:(k + 1) * 128, :])
                p.copy(wd_sb[:, k * DM:(k + 1) * DM], stage[:, 0:DM])
            p.dma(bguf[0:1, :], V["bgu"][ge:ge + 1, :])
            p.copy(bgub[0:1, :], bguf[0:1, :])
            p.dma(bdf[0:1, :], V["bd"][ge:ge + 1, :])
            p.copy(bdb[0:1, :], bdf[0:1, :])
            for j in range(8):
                for (col0, psx) in ((j * 128, pg), (1024 + j * 128, pu)):
                    for k in range(8):
                        p.mm(psx[:, :],
                             wgu_sb[:, k * 2048 + col0:k * 2048 + col0 + 128],
                             xb_sb[:, k * 512:(k + 1) * 512],
                             start=(k == 0), stop=False,
                             wait=(k == 0), inc=0)
                    p.mm(psx[:, :], bgub[0:1, col0:col0 + 128],
                         ones[0:1, 0:512], start=False, stop=True,
                         wait=False, inc=1)
                p.ts1(ga[:, :], pg[:, :], 7.0, OP.min)
                p.act(gb[:, :], ga[:, :], AF.Silu, scale=1.702)
                p.ts(ua[:, :], pu[:, :], 7.0, -7.0, OP.min, OP.max)
                p.act(ub[:, :], ua[:, :], AF.Copy, scale=1.0 / 1.702,
                      bias=1.0 / 1.702)
                p.tt(gated[:, j * 512:(j + 1) * 512], gb[:, :], ub[:, :],
                     OP.mult)
            for t in range(4):
                for nb in range(2):
                    for j in range(8):
                        p.mm(pd[:, nb * 512:nb * 512 + 512],
                             gated[:, j * 512 + t * 128:
                                    j * 512 + t * 128 + 128],
                             wd_sb[:, j * DM + nb * 512:
                                   j * DM + nb * 512 + 512],
                             start=(j == 0), stop=False,
                             wait=(j == 0), inc=0)
                    p.mm(pd[:, nb * 512:nb * 512 + 512], ones[0:1, 0:128],
                         bdb[0:1, nb * 512:nb * 512 + 512], start=False,
                         stop=True, wait=False, inc=1)
                p.stt(acc_sb[:, t * DM:(t + 1) * DM], pd[:, 0:DM],
                      cw_sb[:, t * 16 + ge:t * 16 + ge + 1],
                      acc_sb[:, t * DM:(t + 1) * DM], OP.mult, OP.add)
        for t in range(4):
            r0 = (4 * tb + t) * 128
            p.dma(x1t[:, :], V["X1"][r0:r0 + 128, :])
            p.dma(s1t[:, :], V["S1"][r0:r0 + 128, :])
            p.act(g1[:, 0:DM], acc_sb[:, t * DM:(t + 1) * DM], AF.Silu)
            p.tt(g1[:, 0:DM], s1t[:, 0:DM], g1[:, 0:DM], OP.mult)
            p.tt(o_sb[:, 0:DM], x1t[:, 0:DM], g1[:, 0:DM], OP.add)
            p.dma(V["OUT"][r0:r0 + 128, :], o_sb[:, :])


def build_fused():
    p = Prog()
    p.setup_consts()
    V = {}
    V["x"] = p.inp("x", [T, DM])
    V["Wq"] = p.inp("Wq", [DM, DM])
    V["Wk"] = p.inp("Wk", [DM, DM])
    V["Wv"] = p.inp("Wv", [DM, DM])
    V["Wr"] = p.inp("Wr", [DM, 160])
    V["WfT"] = p.inp("WfT", [DM, DM])
    V["bqc"] = p.inp("bqc", [128, 8])
    V["br1"] = p.inp("br1", [1, 160])
    V["scb"] = p.inp("scb", [128, DM])
    V["bib"] = p.inp("bib", [128, DM])
    V["ident"] = p.inp("ident", [128, 128], BF)
    V["U"] = p.inp("U", [NBI, 128, 1152])
    V["MV0"] = p.inp("MV0", [128, 128])
    V["MV8"] = p.inp("MV8", [128, 128])
    V["Wp"] = p.inp("Wp", [DM, DM])
    V["bp1"] = p.inp("bp1", [1, DM])
    V["wqT"] = p.inp("wqT", [DM, DM])
    V["kvdT"] = p.inp("kvdT", [DM, 512])
    V["kupT"] = p.inp("kupT", [512, DM])
    V["vupT"] = p.inp("vupT", [512, DM])
    V["WgT"] = p.inp("WgT", [DM, 16])
    V["bg1"] = p.inp("bg1", [1, 16])
    V["cosT"] = p.inp("cosT", [128, T])
    V["sinT"] = p.inp("sinT", [128, T])
    V["SW"] = p.inp("SW", [128, 128], BF)
    V["M01"] = p.inp("M01", [4, 128, 512], BF)
    V["ones64"] = p.inp("ones64", [1, 64])
    V["woT"] = p.inp("woT", [DM, DM])
    V["Wgu"] = p.inp("Wgu", [16, DM, 2048])
    V["bgu"] = p.inp("bgu", [16, 2048])
    V["Wd"] = p.inp("Wd", [16, DM, DM])
    V["bd"] = p.inp("bd", [16, DM])
    V["QT"] = p.scratch("sQT", [NT, 128, 8, 128], BF)
    V["KT"] = p.scratch("sKT", [NT, 128, 8, 128], BF)
    V["V"] = p.scratch("sV", [NT, 128, DM], BF)
    V["R"] = p.scratch("sR", [NT, 128, 160])
    V["G"] = p.scratch("sG", [NT, 128, DM])
    V["AT"] = p.scratch("sAT", [NT, 128, 8, 128], BF)
    V["X1"] = p.scratch("sX1", [T, DM])
    V["XN1T"] = p.scratch("sXN1T", [NT, 128, 8, 128], BF)
    V["QT1"] = p.scratch("sQT1", [NT, 128, 8, 128], BF)
    V["KT1"] = p.scratch("sKT1", [NT, 128, 8, 128], BF)
    V["V1"] = p.scratch("sV1", [NT, 128, DM], BF)
    V["CW"] = p.scratch("sCW", [NT, 128, 16])
    V["ATN"] = p.scratch("sATN", [NT, 128, 8, 128], BF)
    V["S1"] = p.scratch("sS1", [T, DM])
    V["OUT"] = p.outp("OUT", [T, DM])

    PSF = p.ps([128, 1536], F32)
    PSB = p.ps([128, 1152], BF)
    id_sb = p.sb([128, 128], BF)
    ones = p.sb([1, 512], BF)
    p.dma(id_sb[:, :], V["ident"])
    p.memset(ones[0:1, :], 1.0)
    p.mark_persist()

    for emit in (emit_A, emit_B, emit_C, emit_D, emit_E, emit_EP, emit_F):
        emit(p, V, PSF, PSB, id_sb, ones)
        p.stage_reset()
    return p.finalize()


_cache = {}


def _consts():
    if "consts" in _cache:
        return _cache["consts"]
    c = {}
    c["ident"] = np.eye(128, dtype=BF_NP)
    q = np.arange(128)[:, None]
    k = np.arange(128)[None, :]
    c["MV0"] = np.where(k <= q, 0.0, -1e9).astype(np.float32)
    c["MV8"] = np.where(k > q, 0.0, -1e9).astype(np.float32)
    r = np.arange(128)[None, :, None]
    cc = np.arange(512)[None, None, :]
    v = np.arange(4)[:, None, None]
    c["M01"] = (128 * v + r <= cc).astype(BF_NP)
    c["ones64"] = np.ones((1, 64), dtype=np.float32)
    d = np.arange(64)
    inv = 1.0 / (10000.0 ** (2 * (d % 32) / 64.0))
    pos = np.arange(T)
    ang = pos[None, :] * inv[:, None]
    cos = np.cos(ang)
    sin = np.sin(ang)
    sgn = np.where(d < 32, -1.0, 1.0)[:, None]
    c["cosT"] = np.tile(cos, (2, 1)).astype(np.float32)
    c["sinT"] = np.tile(sin * sgn, (2, 1)).astype(np.float32)
    S = np.zeros((128, 128), dtype=BF_NP)
    for m in range(128):
        sig = m + 32 if (m % 64) < 32 else m - 32
        S[sig, m] = 1.0
    c["SW"] = S
    _cache["consts"] = c
    return c


def _utables(b_nd):
    qq = np.arange(128)[:, None]
    jj = np.arange(1152)[None, :]
    idx = 1024 + qq - jj
    ok = (idx >= 0) & (idx < 1024)
    idxc = np.clip(idx, 0, 1023)
    U = np.where(ok[None, :, :], b_nd[:, idxc], 0.0).astype(np.float32)
    return U


def kernel(L0_attn_X, L0_attn_Wq, L0_attn_bq, L0_attn_Wk, L0_attn_Wv,
           L0_attn_Wproj, L0_attn_bproj, L0_attn_Wr, L0_attn_br,
           L0_attn_b_nd, L0_ffn_weights, L0_ffn_scale, L0_ffn_bias,
           L1_attn_wq, L1_attn_w_kv_down, L1_attn_w_k_up, L1_attn_w_v_up,
           L1_attn_wo, L1_ffn_Wg, L1_ffn_bg, L1_ffn_Wgu, L1_ffn_bgu,
           L1_ffn_Wd, L1_ffn_bd):
    if "fused" not in _cache:
        _cache["fused"] = build_fused()
    nc = _cache["fused"]
    C = _consts()
    f32 = np.float32
    asf = np.ascontiguousarray
    X = np.asarray(L0_attn_X, f32)

    perm = np.concatenate([np.arange(0, 2048, 2), np.arange(1, 2048, 2)])
    shared = dict(
        Wq=np.asarray(L0_attn_Wq, f32), Wk=np.asarray(L0_attn_Wk, f32),
        Wv=np.asarray(L0_attn_Wv, f32), Wr=np.asarray(L0_attn_Wr, f32),
        WfT=asf(np.asarray(L0_ffn_weights, f32).T),
        bqc=asf(np.asarray(L0_attn_bq, f32).reshape(8, 128).T),
        br1=np.asarray(L0_attn_br, f32).reshape(1, 160),
        scb=asf(np.broadcast_to(np.asarray(L0_ffn_scale, f32), (128, DM))),
        bib=asf(np.broadcast_to(np.asarray(L0_ffn_bias, f32), (128, DM))),
        ident=C["ident"], U=_utables(np.asarray(L0_attn_b_nd, f32)),
        MV0=C["MV0"], MV8=C["MV8"],
        Wp=np.asarray(L0_attn_Wproj, f32),
        bp1=np.asarray(L0_attn_bproj, f32).reshape(1, DM),
        wqT=asf(np.asarray(L1_attn_wq, f32).T),
        kvdT=asf(np.asarray(L1_attn_w_kv_down, f32).T),
        kupT=asf(np.asarray(L1_attn_w_k_up, f32).T),
        vupT=asf(np.asarray(L1_attn_w_v_up, f32).T),
        WgT=asf(np.asarray(L1_ffn_Wg, f32).T),
        bg1=np.asarray(L1_ffn_bg, f32).reshape(1, 16),
        cosT=C["cosT"], sinT=C["sinT"], SW=C["SW"], M01=C["M01"],
        ones64=C["ones64"],
        woT=asf(np.asarray(L1_attn_wo, f32).T),
        Wgu=asf(np.asarray(L1_ffn_Wgu, f32)[:, :, perm]),
        bgu=asf(np.asarray(L1_ffn_bgu, f32)[:, perm]),
        Wd=np.asarray(L1_ffn_Wd, f32),
        bd=np.asarray(L1_ffn_bd, f32),
    )
    maps = [dict(x=asf(X[c]), **shared) for c in range(8)]
    res = run_bass_kernel_spmd(nc, maps, core_ids=list(range(8)))
    out = np.stack([res.results[c]["OUT"] for c in range(8)], axis=0)
    return out.astype(np.float32)


# version 03
# speedup vs baseline: 1.1873x; 1.1433x over previous
import math
import numpy as np
import ml_dtypes
import concourse.bass as bass
import concourse.mybir as mybir
from concourse.bass_utils import run_bass_kernel_spmd

F32 = mybir.dt.float32
BF = mybir.dt.bfloat16
AF = mybir.ActivationFunctionType
OP = mybir.AluOpType
AX = mybir.AxisListType

BF_NP = ml_dtypes.bfloat16
T = 4096
DM = 1024
NH = 16
NBI = 10
NT = T // 128
BF_COLS = 40960
F32_COLS = 20480


class Prog:
    def __init__(self):
        self.nc = bass.Bass("TRN2", target_bir_lowering=False)
        self.sched = []
        self.t = 0
        self.chain = self.nc.alloc_semaphore("chain")
        self._uid = 0
        self._back = {
            BF: self.nc.alloc_sbuf_tensor("arena_bf", [128, BF_COLS], BF),
            F32: self.nc.alloc_sbuf_tensor("arena_f32", [128, F32_COLS], F32),
        }
        self._ptr = {BF: 0, F32: 0}
        self._persist = {BF: 0, F32: 0}

    def inp(self, name, shape, dt=F32):
        return self.nc.dram_tensor(name, shape, dt, kind="ExternalInput").ap()

    def outp(self, name, shape, dt=F32):
        return self.nc.dram_tensor(name, shape, dt, kind="ExternalOutput").ap()

    def scratch(self, name, shape, dt=F32):
        return self.nc.dram_tensor(name, shape, dt, kind="Internal").ap()

    def sb(self, shape, dt=F32):
        cols = (shape[1] + 15) // 16 * 16
        off = self._ptr[dt]
        self._ptr[dt] = off + cols
        assert self._ptr[dt] <= self._back[dt].shape[1], \
            "arena overflow %s %d" % (dt, self._ptr[dt])
        return self._back[dt][0:shape[0], off:off + shape[1]]

    def ps(self, shape, dt=F32):
        self._uid += 1
        return self.nc.alloc_psum_tensor("ps%d" % self._uid, shape, dt)

    def mark_persist(self):
        self._persist = dict(self._ptr)

    def stage_reset(self):
        self._ptr = dict(self._persist)

    def _op(self, eng, fn, inc=1, wait=True, wait_at=None):
        w = wait_at if wait_at is not None else (self.t if wait else -1)
        self.sched.append((eng, fn, w, inc))
        self.t += inc

    def dma(self, out, in_):
        def f(e, o=out, i=in_):
            return e.dma_start(out=o, in_=i)
        self._op("sync", f, inc=16)

    def dma_par(self, pairs):
        engs = ("sync", "gpsimd", "scalar")
        t0 = self.t
        for idx, (out, in_) in enumerate(pairs):
            def f(e, o=out, i=in_):
                return e.dma_start(out=o, in_=i)
            self._op(engs[idx % len(engs)], f, inc=16, wait_at=t0)

    def mm(self, out, lhsT, rhs, start=True, stop=True, wait=True, inc=1):
        def f(e, o=out, l=lhsT, r=rhs, st=start, sp=stop):
            return e.matmul(o, l, r, start=st, stop=sp, skip_group_check=True)
        self._op("tensor", f, inc=inc, wait=wait)

    def trans(self, out, in_, ident, wait=True, inc=1):
        def f(e, o=out, i=in_, d=ident):
            return e.transpose(o, i, d)
        self._op("tensor", f, inc=inc, wait=wait)

    def tt(self, out, in0, in1, op):
        def f(e, o=out, a=in0, b=in1, p=op):
            return e.tensor_tensor(o, a, b, p)
        self._op("vector", f)

    def ts(self, out, in0, s1, s2, op0, op1):
        def f(e, o=out, a=in0, x=s1, y=s2, p0=op0, p1=op1):
            return e.tensor_scalar(o, a, x, y, p0, p1)
        self._op("vector", f)

    def ts1(self, out, in0, s1, op0):
        if op0 is OP.mult:
            def f(e, o=out, a=in0, x=s1):
                return e.tensor_scalar_mul(o, a, x)
        elif op0 is OP.add:
            def f(e, o=out, a=in0, x=s1):
                return e.tensor_scalar_add(o, a, x)
        elif op0 is OP.min:
            def f(e, o=out, a=in0, x=s1):
                return e.tensor_scalar_min(o, a, x)
        else:
            def f(e, o=out, a=in0, x=s1, p=op0):
                return e.tensor_scalar(o, a, x, None, p)
        self._op("vector", f)

    def stt(self, out, in0, scalar, in1, op0, op1):
        def f(e, o=out, a=in0, s=scalar, b=in1, p0=op0, p1=op1):
            return e.scalar_tensor_tensor(o, a, s, b, p0, p1)
        self._op("vector", f)

    def red(self, out, in_, op):
        def f(e, o=out, i=in_, p=op):
            return e.tensor_reduce(o, i, AX.X, p)
        self._op("vector", f)

    def recip(self, out, in_):
        def f(e, o=out, i=in_):
            return e.reciprocal(o, i)
        self._op("vector", f)

    def memset(self, ap, c):
        def f(e, a=ap, v=c):
            return e.memset(a, v)
        self._op("vector", f)

    def act(self, out, in_, func, bias=None, scale=None):
        def f(e, o=out, i=in_, fu=func, b=bias, s=scale):
            kw = {}
            if b is not None:
                kw["bias"] = b
            if s is not None:
                kw["scale"] = s
            return e.activation(o, i, fu, **kw)
        self._op("scalar", f)

    def copy(self, out, in_):
        def f(e, o=out, i=in_):
            return e.copy(o, i)
        self._op("scalar", f)

    def sign(self, out, in_):
        def f(e, o=out, i=in_):
            return e.sign(o, i)
        self._op("scalar", f)

    def finalize(self):
        nc = self.nc
        chain = self.chain
        total = self.t
        self.sched.append(
            ("sync", lambda e, c=chain, v=total: e.wait_ge(c, v), -1, 0))
        with nc.Block() as blk:
            for engname in ("sync", "tensor", "vector", "scalar", "gpsimd"):
                entries = [(f, w, i) for (e, f, w, i) in self.sched
                           if e == engname]
                if not entries:
                    continue

                def mk(entries, c=chain):
                    def body(eng):
                        for (f, w, i) in entries:
                            if w > 0:
                                eng.wait_ge(c, w)
                            ins = f(eng)
                            if i:
                                ins.then_inc(c, i)
                    return body
                getattr(blk, engname)(mk(entries))
        return nc

    def setup_consts(self):
        vals = [0.0, 1.0, -1.0, 1e-5, 1.0 / 1024, -1.0 / 1024, 0.125,
                1.702, 1.0 / 1.702, 7.0, -7.0, 100.0, -100.0, -1e30]
        cs = self.sb([128, len(vals)])
        cb = self.sb([128, len(vals)], BF)
        for j, v in enumerate(vals):
            self.memset(cs[:, j:j + 1], v)
            self.memset(cb[:, j:j + 1], v)
            self.nc.const_aps.register(F32, float(v), cs[:, j:j + 1])
            self.nc.const_aps.register(BF, float(v), cb[:, j:j + 1])

    def load_w_bf16(self, src_ap, rows, cols, dst_sb, stage, do_sign=False):
        nk = rows // 128
        for k in range(nk):
            self.dma(stage[:, 0:cols], src_ap[k * 128:(k + 1) * 128, :])
            if do_sign:
                self.sign(dst_sb[:, k * cols:(k + 1) * cols], stage[:, 0:cols])
            else:
                self.copy(dst_sb[:, k * cols:(k + 1) * cols], stage[:, 0:cols])

    def layernorm(self, x_sb, out_bf, tmp_xc, tmp_sq, st1, st2):
        self.red(st1[:, 0:1], x_sb[:, 0:DM], OP.add)
        self.ts1(st1[:, 1:2], st1[:, 0:1], -1.0 / DM, OP.mult)
        self.ts1(tmp_xc[:, 0:DM], x_sb[:, 0:DM], st1[:, 1:2], OP.add)
        self.tt(tmp_sq[:, 0:DM], tmp_xc[:, 0:DM], tmp_xc[:, 0:DM], OP.mult)
        self.red(st2[:, 0:1], tmp_sq[:, 0:DM], OP.add)
        self.act(st2[:, 1:2], st2[:, 0:1], AF.Sqrt, bias=1e-5, scale=1.0 / DM)
        self.recip(st2[:, 2:3], st2[:, 1:2])
        self.ts1(out_bf[:, 0:DM], tmp_xc[:, 0:DM], st2[:, 2:3], OP.mult)


def emit_A(p, V, PSF, PSB, id_sb, ones):
    stage = p.sb([128, DM])
    wq_sb = p.sb([128, 8 * DM], BF)
    wk_sb = p.sb([128, 8 * DM], BF)
    wv_sb = p.sb([128, 8 * DM], BF)
    wr_sb = p.sb([128, 8 * 160], BF)
    wf_sb = p.sb([128, 8 * DM], BF)
    bq_sb = p.sb([128, 8])
    bq8 = p.sb([128, 8])
    brf = p.sb([1, 160])
    brb = p.sb([1, 160], BF)
    sc_sb = p.sb([128, DM])
    bi_sb = p.sb([128, DM])

    p.load_w_bf16(V["Wq"], DM, DM, wq_sb, stage)
    p.load_w_bf16(V["Wk"], DM, DM, wk_sb, stage)
    p.load_w_bf16(V["Wv"], DM, DM, wv_sb, stage)
    p.load_w_bf16(V["Wr"], DM, 160, wr_sb, stage)
    p.load_w_bf16(V["WfT"], DM, DM, wf_sb, stage, do_sign=True)
    p.dma(bq_sb[:, :], V["bqc"])
    p.ts1(bq8[:, 0:8], bq_sb[:, 0:8], 0.125, OP.mult)
    p.dma(brf[0:1, :], V["br1"])
    p.copy(brb[0:1, :], brf[0:1, :])
    p.dma(sc_sb[:, :], V["scb"])
    p.dma(bi_sb[:, :], V["bib"])

    x_sb = p.sb([128, DM])
    xc = p.sb([128, DM])
    sq = p.sb([128, DM])
    st1 = p.sb([128, 4])
    st2 = p.sb([128, 4])
    xn = p.sb([128, DM], BF)
    xnT = p.sb([128, DM], BF)
    qt_sb = p.sb([128, DM], BF)
    kt_sb = p.sb([128, DM], BF)
    v_sb = p.sb([128, DM], BF)
    r_sb = p.sb([128, 160])
    y_sb = p.sb([128, DM])
    g_sb = p.sb([128, DM])
    pst = PSB[:, 0:128]
    ps1 = PSF[:, 0:128]
    ps5 = PSF[:, 512:1024]
    psr = PSF[:, 1024:1184]

    for i in range(NT):
        p.dma(x_sb[:, :], V["x"][i * 128:(i + 1) * 128, :])
        p.layernorm(x_sb, xn, xc, sq, st1, st2)
        for k in range(8):
            p.trans(pst[:, :], xn[:, k * 128:(k + 1) * 128], id_sb[:, :])
            p.copy(xnT[:, k * 128:(k + 1) * 128], pst[:, :])
        for (wsb, dst, biased) in ((wq_sb, qt_sb, True), (wk_sb, kt_sb, False)):
            for m in range(8):
                for k in range(8):
                    p.mm(ps1[:, :],
                         wsb[:, k * DM + m * 128:k * DM + (m + 1) * 128],
                         xnT[:, k * 128:(k + 1) * 128],
                         start=(k == 0), stop=(k == 7),
                         wait=(k == 0), inc=(1 if k == 7 else 0))
                if biased:
                    p.ts(dst[:, m * 128:(m + 1) * 128], ps1[:, :], 0.125,
                         bq8[:, m:m + 1], OP.mult, OP.add)
                else:
                    p.copy(dst[:, m * 128:(m + 1) * 128], ps1[:, :])
        for nb in range(2):
            for k in range(8):
                p.mm(ps5[:, :], xnT[:, k * 128:(k + 1) * 128],
                     wv_sb[:, k * DM + nb * 512:k * DM + nb * 512 + 512],
                     start=(k == 0), stop=(k == 7),
                     wait=(k == 0), inc=(1 if k == 7 else 0))
            p.copy(v_sb[:, nb * 512:(nb + 1) * 512], ps5[:, :])
        for k in range(8):
            p.mm(psr[:, :], xnT[:, k * 128:(k + 1) * 128],
                 wr_sb[:, k * 160:(k + 1) * 160],
                 start=(k == 0), stop=False,
                 wait=(k == 0), inc=0)
        p.mm(psr[:, :], ones[0:1, 0:128], brb[0:1, :], start=False, stop=True,
             wait=False, inc=1)
        p.copy(r_sb[:, :], psr[:, :])
        for nb in range(2):
            for k in range(8):
                p.mm(ps5[:, :], xnT[:, k * 128:(k + 1) * 128],
                     wf_sb[:, k * DM + nb * 512:k * DM + nb * 512 + 512],
                     start=(k == 0), stop=(k == 7),
                     wait=(k == 0), inc=(1 if k == 7 else 0))
            p.copy(y_sb[:, nb * 512:(nb + 1) * 512], ps5[:, :])
        p.tt(y_sb[:, 0:DM], y_sb[:, 0:DM], sc_sb[:, 0:DM], OP.mult)
        p.tt(y_sb[:, 0:DM], y_sb[:, 0:DM], bi_sb[:, 0:DM], OP.add)
        p.ts(y_sb[:, 0:DM], y_sb[:, 0:DM], 100.0, -100.0, OP.min, OP.max)
        p.act(g_sb[:, 0:DM], y_sb[:, 0:DM], AF.Silu)
        p.dma_par([
            (V["QT"][i], qt_sb[:, 0:DM].rearrange("p (m c) -> p m c", m=8)),
            (V["KT"][i], kt_sb[:, 0:DM].rearrange("p (m c) -> p m c", m=8)),
            (V["V"][i], v_sb[:, :]),
            (V["R"][i], r_sb[:, :]),
            (V["G"][i], g_sb[:, :])])


def emit_B(p, V, PSF, PSB, id_sb, ones):
    u_sb = p.sb([128, NBI * 1152])
    for n in range(NBI):
        p.dma(u_sb[:, n * 1152:(n + 1) * 1152], V["U"][n])
    m0_sb = p.sb([128, 128])
    m8_sb = p.sb([128, 128])
    p.dma(m0_sb[:, :], V["MV0"])
    p.dma(m8_sb[:, :], V["MV8"])

    qt_sb = p.sb([128, DM], BF)
    r_sb = p.sb([128, 160])
    ktw = p.sb([128, 8 * 1152], BF)
    vw = p.sb([128, 9 * DM], BF)
    buf = p.sb([128, 1152])
    buf2 = p.sb([128, 1152])
    P_sb = p.sb([128, 1152], BF)
    PT_sb = p.sb([128, 1152], BF)
    at_sb = p.sb([128, DM], BF)
    st = p.sb([128, 4])
    ps_s = PSF[:, 0:1536]
    ptp = PSB[:, 0:1152]
    av = PSF[0:64, 0:128]

    for i in range(NT):
        glo = max(0, i - 8)
        nt = i - glo + 1
        W = nt * 128
        u_off = (9 - nt) * 128
        pairs = [(qt_sb[:, 0:DM].rearrange("p (m c) -> p m c", m=8),
                  V["QT"][i]),
                 (r_sb[:, :], V["R"][i])]
        for m in range(8):
            pairs.append((ktw[:, m * 1152:m * 1152 + W]
                          .rearrange("p (i t) -> p i t", i=nt),
                          V["KT"][glo:i + 1, :, m, :]
                          .rearrange("i p t -> p i t")))
        for it, gt in enumerate(range(glo, i + 1)):
            pairs.append((vw[:, it * DM:(it + 1) * DM], V["V"][gt]))
        p.dma_par(pairs)
        for h in range(NH):
            hp = 64 * (h % 2)
            hm = h // 2
            p.ts1(buf[:, 0:W],
                  u_sb[:, 0 * 1152 + u_off:0 * 1152 + u_off + W],
                  r_sb[:, h * 10:h * 10 + 1], OP.mult)
            for n in range(1, NBI):
                p.stt(buf[:, 0:W],
                      u_sb[:, n * 1152 + u_off:n * 1152 + u_off + W],
                      r_sb[:, h * 10 + n:h * 10 + n + 1],
                      buf[:, 0:W], OP.mult, OP.add)
            for c0 in range(0, W, 512):
                w = min(512, W - c0)
                p.mm(ps_s[:, c0:c0 + w],
                     qt_sb[hp:hp + 64, hm * 128:(hm + 1) * 128],
                     ktw[hp:hp + 64, hm * 1152 + c0:hm * 1152 + c0 + w])
            for c0 in range(0, W, 512):
                w = min(512, W - c0)
                p.tt(buf[:, c0:c0 + w], ps_s[:, c0:c0 + w],
                     buf[:, c0:c0 + w], OP.add)
            p.tt(buf[:, W - 128:W], buf[:, W - 128:W], m0_sb[:, :], OP.add)
            if nt == 9:
                p.tt(buf[:, 0:128], buf[:, 0:128], m8_sb[:, :], OP.add)
            p.red(st[:, 0:1], buf[:, 0:W], OP.max)
            p.ts1(st[:, 1:2], st[:, 0:1], -1.0, OP.mult)
            p.act(buf2[:, 0:W], buf[:, 0:W], AF.Exp, bias=st[:, 1:2],
                  scale=1.0)
            p.red(st[:, 2:3], buf2[:, 0:W], OP.add)
            p.recip(st[:, 3:4], st[:, 2:3])
            p.ts1(P_sb[:, 0:W], buf2[:, 0:W], st[:, 3:4], OP.mult)
            for kt in range(nt):
                p.trans(ptp[:, kt * 128:(kt + 1) * 128],
                        P_sb[:, kt * 128:(kt + 1) * 128], id_sb[:, :],
                        wait=(kt == 0), inc=(1 if kt == nt - 1 else 0))
            for kt in range(nt):
                p.copy(PT_sb[:, kt * 128:(kt + 1) * 128],
                       ptp[:, kt * 128:(kt + 1) * 128])
            for kt in range(nt):
                p.mm(av[0:64, :],
                     vw[:, kt * DM + h * 64:kt * DM + h * 64 + 64],
                     PT_sb[:, kt * 128:(kt + 1) * 128],
                     start=(kt == 0), stop=(kt == nt - 1),
                     wait=(kt == 0), inc=(1 if kt == nt - 1 else 0))
            p.copy(at_sb[hp:hp + 64, hm * 128:(hm + 1) * 128], av[0:64, :])
        p.dma(V["AT"][i], at_sb[:, 0:DM].rearrange("p (m c) -> p m c", m=8))


def emit_C(p, V, PSF, PSB, id_sb, ones):
    stage = p.sb([128, DM])
    wp_sb = p.sb([128, 8 * DM], BF)
    p.load_w_bf16(V["Wp"], DM, DM, wp_sb, stage)
    bpf = p.sb([1, DM])
    bpb = p.sb([1, DM], BF)
    p.dma(bpf[0:1, :], V["bp1"])
    p.copy(bpb[0:1, :], bpf[0:1, :])

    at_sb = p.sb([128, DM], BF)
    s_sb = p.sb([128, DM])
    x_sb = p.sb([128, DM])
    g_sb = p.sb([128, DM])
    x1_sb = p.sb([128, DM])
    xc = p.sb([128, DM])
    sq = p.sb([128, DM])
    st1 = p.sb([128, 4])
    st2 = p.sb([128, 4])
    xn = p.sb([128, DM], BF)
    xnT = p.sb([128, DM], BF)
    ps5 = PSF[:, 0:512]
    pst = PSB[:, 0:128]

    for i in range(NT):
        p.dma(at_sb[:, 0:DM].rearrange("p (m c) -> p m c", m=8), V["AT"][i])
        for nb in range(2):
            for m in range(8):
                p.mm(ps5[:, :], at_sb[:, m * 128:(m + 1) * 128],
                     wp_sb[:, m * DM + nb * 512:m * DM + nb * 512 + 512],
                     start=(m == 0), stop=False,
                     wait=(m == 0), inc=0)
            p.mm(ps5[:, :], ones[0:1, 0:128],
                 bpb[0:1, nb * 512:nb * 512 + 512], start=False, stop=True,
                 wait=False, inc=1)
            p.copy(s_sb[:, nb * 512:(nb + 1) * 512], ps5[:, :])
        p.dma_par([(x_sb[:, :], V["x"][i * 128:(i + 1) * 128, :]),
                   (g_sb[:, :], V["G"][i])])
        p.tt(s_sb[:, 0:DM], s_sb[:, 0:DM], g_sb[:, 0:DM], OP.mult)
        p.tt(x1_sb[:, 0:DM], x_sb[:, 0:DM], s_sb[:, 0:DM], OP.add)
        p.dma(V["X1"][i * 128:(i + 1) * 128, :], x1_sb[:, :])
        p.layernorm(x1_sb, xn, xc, sq, st1, st2)
        for k in range(8):
            p.trans(pst[:, :], xn[:, k * 128:(k + 1) * 128], id_sb[:, :])
            p.copy(xnT[:, k * 128:(k + 1) * 128], pst[:, :])
        p.dma(V["XN1T"][i], xnT[:, 0:DM].rearrange("p (m c) -> p m c", m=8))


def emit_D(p, V, PSF, PSB, id_sb, ones):
    stage = p.sb([128, DM])
    wq_sb = p.sb([128, 8 * DM], BF)
    kvd_sb = p.sb([128, 8 * 512], BF)
    kup_sb = p.sb([128, 4 * DM], BF)
    vup_sb = p.sb([128, 4 * DM], BF)
    wg_sb = p.sb([128, 8 * 16], BF)
    p.load_w_bf16(V["wqT"], DM, DM, wq_sb, stage)
    p.load_w_bf16(V["kvdT"], DM, 512, kvd_sb, stage)
    p.load_w_bf16(V["kupT"], 512, DM, kup_sb, stage)
    p.load_w_bf16(V["vupT"], 512, DM, vup_sb, stage)
    p.load_w_bf16(V["WgT"], DM, 16, wg_sb, stage)
    bgf = p.sb([1, 16])
    bgb = p.sb([1, 16], BF)
    p.dma(bgf[0:1, :], V["bg1"])
    p.copy(bgb[0:1, :], bgf[0:1, :])
    cos_sb = p.sb([128, T])
    sin_sb = p.sb([128, T])
    p.dma(cos_sb[:, :], V["cosT"])
    p.dma(sin_sb[:, :], V["sinT"])
    sw_sb = p.sb([128, 128], BF)
    p.dma(sw_sb[:, :], V["SW"])

    xt_sb = p.sb([128, DM], BF)
    pre_sb = p.sb([128, DM], BF)
    q_sb = p.sb([128, DM], BF)
    k_sb = p.sb([128, DM], BF)
    c_sb = p.sb([128, 512], BF)
    v_sb = p.sb([128, DM], BF)
    tt1 = p.sb([128, 128])
    tt2 = p.sb([128, 128])
    lg = p.sb([128, 16])
    cwb = p.sb([128, 16])
    st = p.sb([128, 8])
    ps1 = PSF[:, 0:128]
    psw = PSF[:, 512:640]
    ps5 = PSF[:, 1024:1536]

    def rope(src_sb, dst_sb, i):
        for m in range(8):
            p.mm(psw[:, :], sw_sb[:, :], src_sb[:, m * 128:(m + 1) * 128])
            p.tt(tt1[:, :], src_sb[:, m * 128:(m + 1) * 128],
                 cos_sb[:, i * 128:(i + 1) * 128], OP.mult)
            p.tt(tt2[:, :], psw[:, :], sin_sb[:, i * 128:(i + 1) * 128],
                 OP.mult)
            p.tt(dst_sb[:, m * 128:(m + 1) * 128], tt1[:, :], tt2[:, :],
                 OP.add)

    for i in range(NT):
        p.dma(xt_sb[:, 0:DM].rearrange("p (m c) -> p m c", m=8), V["XN1T"][i])
        for m in range(8):
            for k in range(8):
                p.mm(ps1[:, :],
                     wq_sb[:, k * DM + m * 128:k * DM + (m + 1) * 128],
                     xt_sb[:, k * 128:(k + 1) * 128],
                     start=(k == 0), stop=(k == 7),
                     wait=(k == 0), inc=(1 if k == 7 else 0))
            p.copy(pre_sb[:, m * 128:(m + 1) * 128], ps1[:, :])
        rope(pre_sb, q_sb, i)
        for m in range(4):
            for k in range(8):
                p.mm(ps1[:, :],
                     kvd_sb[:, k * 512 + m * 128:k * 512 + (m + 1) * 128],
                     xt_sb[:, k * 128:(k + 1) * 128],
                     start=(k == 0), stop=(k == 7),
                     wait=(k == 0), inc=(1 if k == 7 else 0))
            p.copy(c_sb[:, m * 128:(m + 1) * 128], ps1[:, :])
        for m in range(8):
            for kc in range(4):
                p.mm(ps1[:, :],
                     kup_sb[:, kc * DM + m * 128:kc * DM + (m + 1) * 128],
                     c_sb[:, kc * 128:(kc + 1) * 128],
                     start=(kc == 0), stop=(kc == 3),
                     wait=(kc == 0), inc=(1 if kc == 3 else 0))
            p.copy(pre_sb[:, m * 128:(m + 1) * 128], ps1[:, :])
        rope(pre_sb, k_sb, i)
        p.dma(V["KT1"][i], k_sb[:, 0:DM].rearrange("p (m c) -> p m c", m=8))
        for nb in range(2):
            for kc in range(4):
                p.mm(ps5[:, :], c_sb[:, kc * 128:(kc + 1) * 128],
                     vup_sb[:, kc * DM + nb * 512:kc * DM + nb * 512 + 512],
                     start=(kc == 0), stop=(kc == 3),
                     wait=(kc == 0), inc=(1 if kc == 3 else 0))
            p.copy(v_sb[:, nb * 512:(nb + 1) * 512], ps5[:, :])
        for k in range(8):
            p.mm(ps1[:, 0:16], xt_sb[:, k * 128:(k + 1) * 128],
                 wg_sb[:, k * 16:(k + 1) * 16],
                 start=(k == 0), stop=False, wait=(k == 0), inc=0)
        p.mm(ps1[:, 0:16], ones[0:1, 0:128], bgb[0:1, :], start=False,
             stop=True, wait=False, inc=1)
        p.copy(lg[:, :], ps1[:, 0:16])
        p.red(st[:, 0:1], lg[:, 0:16], OP.max)
        p.ts1(st[:, 1:2], st[:, 0:1], -1.0, OP.mult)
        p.ts1(cwb[:, 0:16], lg[:, 0:16], st[:, 0:1], OP.is_ge)
        p.stt(cwb[:, 0:16], cwb[:, 0:16], -1e30, lg[:, 0:16], OP.mult, OP.add)
        p.red(st[:, 2:3], cwb[:, 0:16], OP.max)
        p.ts1(cwb[:, 0:16], lg[:, 0:16], st[:, 2:3], OP.is_ge)
        p.act(lg[:, 0:16], lg[:, 0:16], AF.Exp, bias=st[:, 1:2], scale=1.0)
        p.tt(lg[:, 0:16], lg[:, 0:16], cwb[:, 0:16], OP.mult)
        p.red(st[:, 3:4], lg[:, 0:16], OP.add)
        p.recip(st[:, 4:5], st[:, 3:4])
        p.ts1(cwb[:, 0:16], lg[:, 0:16], st[:, 4:5], OP.mult)
        p.dma(V["QT1"][i], q_sb[:, 0:DM].rearrange("p (m c) -> p m c", m=8))
        p.dma(V["V1"][i], v_sb[:, :])
        p.dma(V["CW"][i], cwb[:, :])


def emit_E(p, V, PSF, PSB, id_sb, ones):
    m_sb = p.sb([128, 4 * 512], BF)
    for v in range(4):
        p.dma(m_sb[:, v * 512:(v + 1) * 512], V["M01"][v])
    on_sb = p.sb([1, 64])
    p.dma(on_sb[0:1, :], V["ones64"])

    qb_sb = p.sb([64, 512], BF)
    kt_sb = p.sb([64, 128], BF)
    vext = p.sb([128, 65], BF)
    p.memset(vext[:, 64:65], 1.0)
    pt_sb = p.sb([128, 512], BF)
    rs_sb = p.sb([1, 512])
    bc_sb = p.sb([64, 512])
    atn_sb = p.sb([64, 512], BF)
    ps_s = PSF[:, 0:512]
    av = PSF[:, 512:1024]
    bc = PSF[0:64, 1024:1536]

    for h in range(NH):
        hp = 64 * (h % 2)
        hm = h // 2
        for qb in range(8):
            p.dma(qb_sb[0:64, 0:512].rearrange("p (i t) -> p i t", i=4),
                  V["QT1"][4 * qb:4 * qb + 4, hp:hp + 64, hm, :]
                  .rearrange("i p t -> p i t"))
            last = 4 * qb + 3
            for kt in range(4 * qb + 4):
                p.dma_par([(kt_sb[0:64, :], V["KT1"][kt, hp:hp + 64, hm, :]),
                           (vext[:, 0:64], V["V1"][kt, :, h * 64:h * 64 + 64])])
                p.mm(ps_s[:, :], kt_sb[0:64, :], qb_sb[0:64, :])
                p.act(pt_sb[:, :], ps_s[:, :], AF.Exp, scale=0.125)
                if kt >= 4 * qb:
                    vv = kt - 4 * qb
                    p.tt(pt_sb[:, :], pt_sb[:, :],
                         m_sb[:, vv * 512:(vv + 1) * 512], OP.mult)
                p.mm(av[0:65, :], vext[:, 0:65], pt_sb[:, :],
                     start=(kt == 0), stop=(kt == last),
                     wait=True, inc=1)
            p.recip(rs_sb[0:1, :], av[64:65, :])
            p.mm(bc[0:64, :], on_sb[0:1, :], rs_sb[0:1, :])
            p.copy(bc_sb[:, :], bc[0:64, :])
            p.tt(atn_sb[:, :], av[0:64, :], bc_sb[:, :], OP.mult)
            p.dma(V["ATN"][4 * qb:4 * qb + 4, hp:hp + 64, hm, :]
                  .rearrange("i p t -> p i t"),
                  atn_sb[0:64, 0:512].rearrange("p (i t) -> p i t", i=4))


def emit_EP(p, V, PSF, PSB, id_sb, ones):
    stage = p.sb([128, DM])
    wo_sb = p.sb([128, 8 * DM], BF)
    p.load_w_bf16(V["woT"], DM, DM, wo_sb, stage)
    a_sb = p.sb([128, DM], BF)
    s_sb = p.sb([128, DM])
    ps5 = PSF[:, 0:512]
    for i in range(NT):
        p.dma(a_sb[:, 0:DM].rearrange("p (m c) -> p m c", m=8), V["ATN"][i])
        for nb in range(2):
            for m in range(8):
                p.mm(ps5[:, :], a_sb[:, m * 128:(m + 1) * 128],
                     wo_sb[:, m * DM + nb * 512:m * DM + nb * 512 + 512],
                     start=(m == 0), stop=(m == 7),
                     wait=(m == 0), inc=(1 if m == 7 else 0))
            p.copy(s_sb[:, nb * 512:(nb + 1) * 512], ps5[:, :])
        p.dma(V["S1"][i * 128:(i + 1) * 128, :], s_sb[:, :])


def emit_F(p, V, PSF, PSB, id_sb, ones):
    stage = p.sb([128, 3 * 2048])
    wgu_sb = p.sb([128, 8 * 2048], BF)
    wd_sb = p.sb([128, 8 * DM], BF)
    bguf = p.sb([1, 2048])
    bgub = p.sb([1, 2048], BF)
    bdf = p.sb([1, DM])
    bdb = p.sb([1, DM], BF)

    xb_sb = p.sb([128, 8 * 512], BF)
    acc_sb = p.sb([128, 4 * DM])
    cw_sb = p.sb([128, 4 * 16])
    ga = p.sb([128, 512])
    gb = p.sb([128, 512])
    ua = p.sb([128, 512])
    ub = p.sb([128, 512])
    gated = p.sb([128, 8 * 512], BF)
    x1t = p.sb([128, DM])
    s1t = p.sb([128, DM])
    g1 = p.sb([128, DM])
    o_sb = p.sb([128, DM])
    pg = PSF[:, 0:512]
    pu = PSF[:, 512:1024]
    pd = PSF[:, 0:1024]

    for tb in range(8):
        pairs = [(xb_sb[:, k * 512:(k + 1) * 512]
                  .rearrange("p (i t) -> p i t", i=4),
                  V["XN1T"][4 * tb:4 * tb + 4, :, k, :]
                  .rearrange("i p t -> p i t")) for k in range(8)]
        pairs += [(cw_sb[:, t * 16:(t + 1) * 16], V["CW"][4 * tb + t])
                  for t in range(4)]
        p.dma_par(pairs)
        p.memset(acc_sb[:, 0:4 * DM], 0.0)
        for ge in range(16):
            for k0 in range(0, 8, 3):
                ks = list(range(k0, min(k0 + 3, 8)))
                p.dma_par([(stage[:, (k - k0) * 2048:(k - k0) * 2048 + 2048],
                            V["Wgu"][ge, k * 128:(k + 1) * 128, :])
                           for k in ks])
                for k in ks:
                    p.copy(wgu_sb[:, k * 2048:(k + 1) * 2048],
                           stage[:, (k - k0) * 2048:(k - k0) * 2048 + 2048])
            for k0 in range(0, 8, 3):
                ks = list(range(k0, min(k0 + 3, 8)))
                p.dma_par([(stage[:, (k - k0) * 2048:(k - k0) * 2048 + DM],
                            V["Wd"][ge, k * 128:(k + 1) * 128, :])
                           for k in ks])
                for k in ks:
                    p.copy(wd_sb[:, k * DM:(k + 1) * DM],
                           stage[:, (k - k0) * 2048:(k - k0) * 2048 + DM])
            p.dma_par([(bguf[0:1, :], V["bgu"][ge:ge + 1, :]),
                       (bdf[0:1, :], V["bd"][ge:ge + 1, :])])
            p.copy(bgub[0:1, :], bguf[0:1, :])
            p.copy(bdb[0:1, :], bdf[0:1, :])
            for j in range(8):
                for (col0, psx) in ((j * 128, pg), (1024 + j * 128, pu)):
                    for k in range(8):
                        p.mm(psx[:, :],
                             wgu_sb[:, k * 2048 + col0:k * 2048 + col0 + 128],
                             xb_sb[:, k * 512:(k + 1) * 512],
                             start=(k == 0), stop=False,
                             wait=(k == 0), inc=0)
                    p.mm(psx[:, :], bgub[0:1, col0:col0 + 128],
                         ones[0:1, 0:512], start=False, stop=True,
                         wait=False, inc=1)
                p.ts1(ga[:, :], pg[:, :], 7.0, OP.min)
                p.act(gb[:, :], ga[:, :], AF.Silu, scale=1.702)
                p.ts(ua[:, :], pu[:, :], 7.0, -7.0, OP.min, OP.max)
                p.act(ub[:, :], ua[:, :], AF.Copy, scale=1.0 / 1.702,
                      bias=1.0 / 1.702)
                p.tt(gated[:, j * 512:(j + 1) * 512], gb[:, :], ub[:, :],
                     OP.mult)
            for t in range(4):
                for nb in range(2):
                    for j in range(8):
                        p.mm(pd[:, nb * 512:nb * 512 + 512],
                             gated[:, j * 512 + t * 128:
                                    j * 512 + t * 128 + 128],
                             wd_sb[:, j * DM + nb * 512:
                                   j * DM + nb * 512 + 512],
                             start=(j == 0), stop=False,
                             wait=(j == 0), inc=0)
                    p.mm(pd[:, nb * 512:nb * 512 + 512], ones[0:1, 0:128],
                         bdb[0:1, nb * 512:nb * 512 + 512], start=False,
                         stop=True, wait=False, inc=1)
                p.stt(acc_sb[:, t * DM:(t + 1) * DM], pd[:, 0:DM],
                      cw_sb[:, t * 16 + ge:t * 16 + ge + 1],
                      acc_sb[:, t * DM:(t + 1) * DM], OP.mult, OP.add)
        for t in range(4):
            r0 = (4 * tb + t) * 128
            p.dma_par([(x1t[:, :], V["X1"][r0:r0 + 128, :]),
                       (s1t[:, :], V["S1"][r0:r0 + 128, :])])
            p.act(g1[:, 0:DM], acc_sb[:, t * DM:(t + 1) * DM], AF.Silu)
            p.tt(g1[:, 0:DM], s1t[:, 0:DM], g1[:, 0:DM], OP.mult)
            p.tt(o_sb[:, 0:DM], x1t[:, 0:DM], g1[:, 0:DM], OP.add)
            p.dma(V["OUT"][r0:r0 + 128, :], o_sb[:, :])


def build_fused():
    p = Prog()
    p.setup_consts()
    V = {}
    V["x"] = p.inp("x", [T, DM])
    V["Wq"] = p.inp("Wq", [DM, DM])
    V["Wk"] = p.inp("Wk", [DM, DM])
    V["Wv"] = p.inp("Wv", [DM, DM])
    V["Wr"] = p.inp("Wr", [DM, 160])
    V["WfT"] = p.inp("WfT", [DM, DM])
    V["bqc"] = p.inp("bqc", [128, 8])
    V["br1"] = p.inp("br1", [1, 160])
    V["scb"] = p.inp("scb", [128, DM])
    V["bib"] = p.inp("bib", [128, DM])
    V["ident"] = p.inp("ident", [128, 128], BF)
    V["U"] = p.inp("U", [NBI, 128, 1152])
    V["MV0"] = p.inp("MV0", [128, 128])
    V["MV8"] = p.inp("MV8", [128, 128])
    V["Wp"] = p.inp("Wp", [DM, DM])
    V["bp1"] = p.inp("bp1", [1, DM])
    V["wqT"] = p.inp("wqT", [DM, DM])
    V["kvdT"] = p.inp("kvdT", [DM, 512])
    V["kupT"] = p.inp("kupT", [512, DM])
    V["vupT"] = p.inp("vupT", [512, DM])
    V["WgT"] = p.inp("WgT", [DM, 16])
    V["bg1"] = p.inp("bg1", [1, 16])
    V["cosT"] = p.inp("cosT", [128, T])
    V["sinT"] = p.inp("sinT", [128, T])
    V["SW"] = p.inp("SW", [128, 128], BF)
    V["M01"] = p.inp("M01", [4, 128, 512], BF)
    V["ones64"] = p.inp("ones64", [1, 64])
    V["woT"] = p.inp("woT", [DM, DM])
    V["Wgu"] = p.inp("Wgu", [16, DM, 2048])
    V["bgu"] = p.inp("bgu", [16, 2048])
    V["Wd"] = p.inp("Wd", [16, DM, DM])
    V["bd"] = p.inp("bd", [16, DM])
    V["QT"] = p.scratch("sQT", [NT, 128, 8, 128], BF)
    V["KT"] = p.scratch("sKT", [NT, 128, 8, 128], BF)
    V["V"] = p.scratch("sV", [NT, 128, DM], BF)
    V["R"] = p.scratch("sR", [NT, 128, 160])
    V["G"] = p.scratch("sG", [NT, 128, DM])
    V["AT"] = p.scratch("sAT", [NT, 128, 8, 128], BF)
    V["X1"] = p.scratch("sX1", [T, DM])
    V["XN1T"] = p.scratch("sXN1T", [NT, 128, 8, 128], BF)
    V["QT1"] = p.scratch("sQT1", [NT, 128, 8, 128], BF)
    V["KT1"] = p.scratch("sKT1", [NT, 128, 8, 128], BF)
    V["V1"] = p.scratch("sV1", [NT, 128, DM], BF)
    V["CW"] = p.scratch("sCW", [NT, 128, 16])
    V["ATN"] = p.scratch("sATN", [NT, 128, 8, 128], BF)
    V["S1"] = p.scratch("sS1", [T, DM])
    V["OUT"] = p.outp("OUT", [T, DM])

    PSF = p.ps([128, 1536], F32)
    PSB = p.ps([128, 1152], BF)
    id_sb = p.sb([128, 128], BF)
    ones = p.sb([1, 512], BF)
    p.dma(id_sb[:, :], V["ident"])
    p.memset(ones[0:1, :], 1.0)
    p.mark_persist()

    for emit in (emit_A, emit_B, emit_C, emit_D, emit_E, emit_EP, emit_F):
        emit(p, V, PSF, PSB, id_sb, ones)
        p.stage_reset()
    return p.finalize()


_cache = {}


def _consts():
    if "consts" in _cache:
        return _cache["consts"]
    c = {}
    c["ident"] = np.eye(128, dtype=BF_NP)
    q = np.arange(128)[:, None]
    k = np.arange(128)[None, :]
    c["MV0"] = np.where(k <= q, 0.0, -1e9).astype(np.float32)
    c["MV8"] = np.where(k > q, 0.0, -1e9).astype(np.float32)
    r = np.arange(128)[None, :, None]
    cc = np.arange(512)[None, None, :]
    v = np.arange(4)[:, None, None]
    c["M01"] = (128 * v + r <= cc).astype(BF_NP)
    c["ones64"] = np.ones((1, 64), dtype=np.float32)
    d = np.arange(64)
    inv = 1.0 / (10000.0 ** (2 * (d % 32) / 64.0))
    pos = np.arange(T)
    ang = pos[None, :] * inv[:, None]
    cos = np.cos(ang)
    sin = np.sin(ang)
    sgn = np.where(d < 32, -1.0, 1.0)[:, None]
    c["cosT"] = np.tile(cos, (2, 1)).astype(np.float32)
    c["sinT"] = np.tile(sin * sgn, (2, 1)).astype(np.float32)
    S = np.zeros((128, 128), dtype=BF_NP)
    for m in range(128):
        sig = m + 32 if (m % 64) < 32 else m - 32
        S[sig, m] = 1.0
    c["SW"] = S
    _cache["consts"] = c
    return c


def _utables(b_nd):
    qq = np.arange(128)[:, None]
    jj = np.arange(1152)[None, :]
    idx = 1024 + qq - jj
    ok = (idx >= 0) & (idx < 1024)
    idxc = np.clip(idx, 0, 1023)
    U = np.where(ok[None, :, :], b_nd[:, idxc], 0.0).astype(np.float32)
    return U


def kernel(L0_attn_X, L0_attn_Wq, L0_attn_bq, L0_attn_Wk, L0_attn_Wv,
           L0_attn_Wproj, L0_attn_bproj, L0_attn_Wr, L0_attn_br,
           L0_attn_b_nd, L0_ffn_weights, L0_ffn_scale, L0_ffn_bias,
           L1_attn_wq, L1_attn_w_kv_down, L1_attn_w_k_up, L1_attn_w_v_up,
           L1_attn_wo, L1_ffn_Wg, L1_ffn_bg, L1_ffn_Wgu, L1_ffn_bgu,
           L1_ffn_Wd, L1_ffn_bd):
    if "fused" not in _cache:
        _cache["fused"] = build_fused()
    nc = _cache["fused"]
    C = _consts()
    f32 = np.float32
    asf = np.ascontiguousarray
    X = np.asarray(L0_attn_X, f32)

    perm = np.concatenate([np.arange(0, 2048, 2), np.arange(1, 2048, 2)])
    shared = dict(
        Wq=np.asarray(L0_attn_Wq, f32), Wk=np.asarray(L0_attn_Wk, f32),
        Wv=np.asarray(L0_attn_Wv, f32), Wr=np.asarray(L0_attn_Wr, f32),
        WfT=asf(np.asarray(L0_ffn_weights, f32).T),
        bqc=asf(np.asarray(L0_attn_bq, f32).reshape(8, 128).T),
        br1=np.asarray(L0_attn_br, f32).reshape(1, 160),
        scb=asf(np.broadcast_to(np.asarray(L0_ffn_scale, f32), (128, DM))),
        bib=asf(np.broadcast_to(np.asarray(L0_ffn_bias, f32), (128, DM))),
        ident=C["ident"], U=_utables(np.asarray(L0_attn_b_nd, f32)),
        MV0=C["MV0"], MV8=C["MV8"],
        Wp=np.asarray(L0_attn_Wproj, f32),
        bp1=np.asarray(L0_attn_bproj, f32).reshape(1, DM),
        wqT=asf(np.asarray(L1_attn_wq, f32).T),
        kvdT=asf(np.asarray(L1_attn_w_kv_down, f32).T),
        kupT=asf(np.asarray(L1_attn_w_k_up, f32).T),
        vupT=asf(np.asarray(L1_attn_w_v_up, f32).T),
        WgT=asf(np.asarray(L1_ffn_Wg, f32).T),
        bg1=np.asarray(L1_ffn_bg, f32).reshape(1, 16),
        cosT=C["cosT"], sinT=C["sinT"], SW=C["SW"], M01=C["M01"],
        ones64=C["ones64"],
        woT=asf(np.asarray(L1_attn_wo, f32).T),
        Wgu=asf(np.asarray(L1_ffn_Wgu, f32)[:, :, perm]),
        bgu=asf(np.asarray(L1_ffn_bgu, f32)[:, perm]),
        Wd=np.asarray(L1_ffn_Wd, f32),
        bd=np.asarray(L1_ffn_bd, f32),
    )
    maps = [dict(x=asf(X[c]), **shared) for c in range(8)]
    res = run_bass_kernel_spmd(nc, maps, core_ids=list(range(8)))
    out = np.stack([res.results[c]["OUT"] for c in range(8)], axis=0)
    return out.astype(np.float32)


# version 05
# speedup vs baseline: 1.2158x; 1.0240x over previous
import math
import numpy as np
import ml_dtypes
import concourse.bass as bass
import concourse.mybir as mybir
from concourse.bass_utils import run_bass_kernel_spmd

F32 = mybir.dt.float32
BF = mybir.dt.bfloat16
AF = mybir.ActivationFunctionType
OP = mybir.AluOpType
AX = mybir.AxisListType

BF_NP = ml_dtypes.bfloat16
T = 4096
DM = 1024
NH = 16
NBI = 10
NT = T // 128
BF_COLS = 40960
F32_COLS = 20480


class Prog:
    def __init__(self):
        self.nc = bass.Bass("TRN2", target_bir_lowering=False)
        self.sched = []
        self.t = 0
        self.chain = self.nc.alloc_semaphore("chain")
        self._uid = 0
        self._back = {
            BF: self.nc.alloc_sbuf_tensor("arena_bf", [128, BF_COLS], BF),
            F32: self.nc.alloc_sbuf_tensor("arena_f32", [128, F32_COLS], F32),
        }
        self._ptr = {BF: 0, F32: 0}
        self._persist = {BF: 0, F32: 0}

    def inp(self, name, shape, dt=F32):
        return self.nc.dram_tensor(name, shape, dt, kind="ExternalInput").ap()

    def outp(self, name, shape, dt=F32):
        return self.nc.dram_tensor(name, shape, dt, kind="ExternalOutput").ap()

    def scratch(self, name, shape, dt=F32):
        return self.nc.dram_tensor(name, shape, dt, kind="Internal").ap()

    def sb(self, shape, dt=F32):
        cols = (shape[1] + 15) // 16 * 16
        off = self._ptr[dt]
        self._ptr[dt] = off + cols
        assert self._ptr[dt] <= self._back[dt].shape[1], \
            "arena overflow %s %d" % (dt, self._ptr[dt])
        return self._back[dt][0:shape[0], off:off + shape[1]]

    def ps(self, shape, dt=F32):
        self._uid += 1
        return self.nc.alloc_psum_tensor("ps%d" % self._uid, shape, dt)

    def mark_persist(self):
        self._persist = dict(self._ptr)

    def stage_reset(self):
        self._ptr = dict(self._persist)

    def _op(self, eng, fn, inc=1, wait=True, wait_at=None):
        w = wait_at if wait_at is not None else (self.t if wait else -1)
        self.sched.append((eng, fn, w, inc))
        self.t += inc

    def dma(self, out, in_):
        def f(e, o=out, i=in_):
            return e.dma_start(out=o, in_=i)
        self._op("sync", f, inc=16)

    def dma_par(self, pairs):
        engs = ("sync", "gpsimd", "scalar")
        t0 = self.t
        for idx, (out, in_) in enumerate(pairs):
            def f(e, o=out, i=in_):
                return e.dma_start(out=o, in_=i)
            self._op(engs[idx % len(engs)], f, inc=16, wait_at=t0)

    def mm(self, out, lhsT, rhs, start=True, stop=True, wait=True, inc=1):
        def f(e, o=out, l=lhsT, r=rhs, st=start, sp=stop):
            return e.matmul(o, l, r, start=st, stop=sp, skip_group_check=True)
        self._op("tensor", f, inc=inc, wait=wait)

    def trans(self, out, in_, ident, wait=True, inc=1):
        def f(e, o=out, i=in_, d=ident):
            return e.transpose(o, i, d)
        self._op("tensor", f, inc=inc, wait=wait)

    def tt(self, out, in0, in1, op):
        def f(e, o=out, a=in0, b=in1, p=op):
            return e.tensor_tensor(o, a, b, p)
        self._op("vector", f)

    def ts(self, out, in0, s1, s2, op0, op1):
        def f(e, o=out, a=in0, x=s1, y=s2, p0=op0, p1=op1):
            return e.tensor_scalar(o, a, x, y, p0, p1)
        self._op("vector", f)

    def ts1(self, out, in0, s1, op0):
        if op0 is OP.mult:
            def f(e, o=out, a=in0, x=s1):
                return e.tensor_scalar_mul(o, a, x)
        elif op0 is OP.add:
            def f(e, o=out, a=in0, x=s1):
                return e.tensor_scalar_add(o, a, x)
        elif op0 is OP.min:
            def f(e, o=out, a=in0, x=s1):
                return e.tensor_scalar_min(o, a, x)
        else:
            def f(e, o=out, a=in0, x=s1, p=op0):
                return e.tensor_scalar(o, a, x, None, p)
        self._op("vector", f)

    def stt(self, out, in0, scalar, in1, op0, op1):
        def f(e, o=out, a=in0, s=scalar, b=in1, p0=op0, p1=op1):
            return e.scalar_tensor_tensor(o, a, s, b, p0, p1)
        self._op("vector", f)

    def red(self, out, in_, op):
        def f(e, o=out, i=in_, p=op):
            return e.tensor_reduce(o, i, AX.X, p)
        self._op("vector", f)

    def recip(self, out, in_):
        def f(e, o=out, i=in_):
            return e.reciprocal(o, i)
        self._op("vector", f)

    def memset(self, ap, c):
        def f(e, a=ap, v=c):
            return e.memset(a, v)
        self._op("vector", f)

    def act(self, out, in_, func, bias=None, scale=None):
        def f(e, o=out, i=in_, fu=func, b=bias, s=scale):
            kw = {}
            if b is not None:
                kw["bias"] = b
            if s is not None:
                kw["scale"] = s
            return e.activation(o, i, fu, **kw)
        self._op("scalar", f)

    def copy(self, out, in_):
        def f(e, o=out, i=in_):
            return e.copy(o, i)
        self._op("scalar", f)

    def sign(self, out, in_):
        def f(e, o=out, i=in_):
            return e.sign(o, i)
        self._op("scalar", f)

    def finalize(self):
        nc = self.nc
        chain = self.chain
        total = self.t
        self.sched.append(
            ("sync", lambda e, c=chain, v=total: e.wait_ge(c, v), -1, 0))
        with nc.Block() as blk:
            for engname in ("sync", "tensor", "vector", "scalar", "gpsimd"):
                entries = [(f, w, i) for (e, f, w, i) in self.sched
                           if e == engname]
                if not entries:
                    continue

                def mk(entries, c=chain):
                    def body(eng):
                        for (f, w, i) in entries:
                            if w > 0:
                                eng.wait_ge(c, w)
                            ins = f(eng)
                            if i:
                                ins.then_inc(c, i)
                    return body
                getattr(blk, engname)(mk(entries))
        return nc

    def setup_consts(self):
        vals = [0.0, 1.0, -1.0, 1e-5, 1.0 / 1024, -1.0 / 1024, 0.125,
                1.702, 1.0 / 1.702, 7.0, -7.0, 100.0, -100.0, -1e30]
        cs = self.sb([128, len(vals)])
        cb = self.sb([128, len(vals)], BF)
        for j, v in enumerate(vals):
            self.memset(cs[:, j:j + 1], v)
            self.memset(cb[:, j:j + 1], v)
            self.nc.const_aps.register(F32, float(v), cs[:, j:j + 1])
            self.nc.const_aps.register(BF, float(v), cb[:, j:j + 1])

    def load_w_bf16(self, src_ap, rows, cols, dst_sb, stage, do_sign=False):
        nk = rows // 128
        for k in range(nk):
            self.dma(stage[:, 0:cols], src_ap[k * 128:(k + 1) * 128, :])
            if do_sign:
                self.sign(dst_sb[:, k * cols:(k + 1) * cols], stage[:, 0:cols])
            else:
                self.copy(dst_sb[:, k * cols:(k + 1) * cols], stage[:, 0:cols])

    def layernorm(self, x_sb, out_bf, tmp_xc, tmp_sq, st1, st2):
        self.red(st1[:, 0:1], x_sb[:, 0:DM], OP.add)
        self.ts1(st1[:, 1:2], st1[:, 0:1], -1.0 / DM, OP.mult)
        self.ts1(tmp_xc[:, 0:DM], x_sb[:, 0:DM], st1[:, 1:2], OP.add)
        self.tt(tmp_sq[:, 0:DM], tmp_xc[:, 0:DM], tmp_xc[:, 0:DM], OP.mult)
        self.red(st2[:, 0:1], tmp_sq[:, 0:DM], OP.add)
        self.act(st2[:, 1:2], st2[:, 0:1], AF.Sqrt, bias=1e-5, scale=1.0 / DM)
        self.recip(st2[:, 2:3], st2[:, 1:2])
        self.ts1(out_bf[:, 0:DM], tmp_xc[:, 0:DM], st2[:, 2:3], OP.mult)


def emit_A(p, V, PSF, PSB, id_sb, ones):
    stage = p.sb([128, DM])
    wq_sb = p.sb([128, 8 * DM], BF)
    wk_sb = p.sb([128, 8 * DM], BF)
    wv_sb = p.sb([128, 8 * DM], BF)
    wr_sb = p.sb([128, 8 * 160], BF)
    wf_sb = p.sb([128, 8 * DM], BF)
    bq_sb = p.sb([128, 8])
    bq8 = p.sb([128, 8])
    brf = p.sb([1, 160])
    brb = p.sb([1, 160], BF)
    sc_sb = p.sb([128, DM])
    bi_sb = p.sb([128, DM])

    p.load_w_bf16(V["Wq"], DM, DM, wq_sb, stage)
    p.load_w_bf16(V["Wk"], DM, DM, wk_sb, stage)
    p.load_w_bf16(V["Wv"], DM, DM, wv_sb, stage)
    p.load_w_bf16(V["Wr"], DM, 160, wr_sb, stage)
    p.load_w_bf16(V["WfT"], DM, DM, wf_sb, stage, do_sign=True)
    p.dma(bq_sb[:, :], V["bqc"])
    p.ts1(bq8[:, 0:8], bq_sb[:, 0:8], 0.125, OP.mult)
    p.dma(brf[0:1, :], V["br1"])
    p.copy(brb[0:1, :], brf[0:1, :])
    p.dma(sc_sb[:, :], V["scb"])
    p.dma(bi_sb[:, :], V["bib"])

    x_sb = p.sb([128, DM])
    xc = p.sb([128, DM])
    sq = p.sb([128, DM])
    st1 = p.sb([128, 4])
    st2 = p.sb([128, 4])
    xn = p.sb([128, DM], BF)
    xnT = p.sb([128, DM], BF)
    qt_sb = p.sb([128, DM], BF)
    kt_sb = p.sb([128, DM], BF)
    v_sb = p.sb([128, DM], BF)
    r_sb = p.sb([128, 160])
    y_sb = p.sb([128, DM])
    g_sb = p.sb([128, DM])
    pst = PSB[:, 0:128]
    ps1 = PSF[:, 0:128]
    ps5 = PSF[:, 512:1024]
    psr = PSF[:, 1024:1184]

    for i in range(NT):
        p.dma(x_sb[:, :], V["x"][i * 128:(i + 1) * 128, :])
        p.layernorm(x_sb, xn, xc, sq, st1, st2)
        for k in range(8):
            p.trans(PSB[:, k * 128:(k + 1) * 128],
                    xn[:, k * 128:(k + 1) * 128], id_sb[:, :],
                    wait=(k == 0), inc=(1 if k == 7 else 0))
        p.copy(xnT[:, 0:DM], PSB[:, 0:DM])
        for (wsb, dst, biased) in ((wq_sb, qt_sb, True), (wk_sb, kt_sb, False)):
            for m in range(8):
                for k in range(8):
                    p.mm(ps1[:, :],
                         wsb[:, k * DM + m * 128:k * DM + (m + 1) * 128],
                         xnT[:, k * 128:(k + 1) * 128],
                         start=(k == 0), stop=(k == 7),
                         wait=(k == 0), inc=(1 if k == 7 else 0))
                if biased:
                    p.ts(dst[:, m * 128:(m + 1) * 128], ps1[:, :], 0.125,
                         bq8[:, m:m + 1], OP.mult, OP.add)
                else:
                    p.copy(dst[:, m * 128:(m + 1) * 128], ps1[:, :])
        for nb in range(2):
            for k in range(8):
                p.mm(ps5[:, :], xnT[:, k * 128:(k + 1) * 128],
                     wv_sb[:, k * DM + nb * 512:k * DM + nb * 512 + 512],
                     start=(k == 0), stop=(k == 7),
                     wait=(k == 0), inc=(1 if k == 7 else 0))
            p.copy(v_sb[:, nb * 512:(nb + 1) * 512], ps5[:, :])
        for k in range(8):
            p.mm(psr[:, :], xnT[:, k * 128:(k + 1) * 128],
                 wr_sb[:, k * 160:(k + 1) * 160],
                 start=(k == 0), stop=False,
                 wait=(k == 0), inc=0)
        p.mm(psr[:, :], ones[0:1, 0:128], brb[0:1, :], start=False, stop=True,
             wait=False, inc=1)
        p.copy(r_sb[:, :], psr[:, :])
        for nb in range(2):
            for k in range(8):
                p.mm(ps5[:, :], xnT[:, k * 128:(k + 1) * 128],
                     wf_sb[:, k * DM + nb * 512:k * DM + nb * 512 + 512],
                     start=(k == 0), stop=(k == 7),
                     wait=(k == 0), inc=(1 if k == 7 else 0))
            p.copy(y_sb[:, nb * 512:(nb + 1) * 512], ps5[:, :])
        p.tt(y_sb[:, 0:DM], y_sb[:, 0:DM], sc_sb[:, 0:DM], OP.mult)
        p.tt(y_sb[:, 0:DM], y_sb[:, 0:DM], bi_sb[:, 0:DM], OP.add)
        p.ts(y_sb[:, 0:DM], y_sb[:, 0:DM], 100.0, -100.0, OP.min, OP.max)
        p.act(g_sb[:, 0:DM], y_sb[:, 0:DM], AF.Silu)
        p.dma_par([
            (V["QT"][i], qt_sb[:, 0:DM].rearrange("p (m c) -> p m c", m=8)),
            (V["KT"][i], kt_sb[:, 0:DM].rearrange("p (m c) -> p m c", m=8)),
            (V["V"][i], v_sb[:, :]),
            (V["R"][i], r_sb[:, :]),
            (V["G"][i], g_sb[:, :])])


def emit_B(p, V, PSF, PSB, id_sb, ones):
    u_sb = p.sb([128, NBI * 1152])
    for n in range(NBI):
        p.dma(u_sb[:, n * 1152:(n + 1) * 1152], V["U"][n])
    m0_sb = p.sb([128, 128])
    m8_sb = p.sb([128, 128])
    p.dma(m0_sb[:, :], V["MV0"])
    p.dma(m8_sb[:, :], V["MV8"])

    qt_sb = p.sb([128, DM], BF)
    r_sb = p.sb([128, 160])
    ktw = p.sb([128, 8 * 1152], BF)
    vw = p.sb([128, 9 * DM], BF)
    buf = p.sb([128, 1152])
    buf2 = p.sb([128, 1152])
    P_sb = p.sb([128, 1152], BF)
    PT_sb = p.sb([128, 1152], BF)
    at_sb = p.sb([128, DM], BF)
    st = p.sb([128, 4])
    ps_s = PSF[:, 0:1536]
    ptp = PSB[:, 0:1152]
    av = PSF[0:64, 0:128]

    for i in range(NT):
        glo = max(0, i - 8)
        nt = i - glo + 1
        W = nt * 128
        u_off = (9 - nt) * 128
        pairs = [(qt_sb[:, 0:DM].rearrange("p (m c) -> p m c", m=8),
                  V["QT"][i]),
                 (r_sb[:, :], V["R"][i])]
        for m in range(8):
            pairs.append((ktw[:, m * 1152:m * 1152 + W]
                          .rearrange("p (i t) -> p i t", i=nt),
                          V["KT"][glo:i + 1, :, m, :]
                          .rearrange("i p t -> p i t")))
        for it, gt in enumerate(range(glo, i + 1)):
            pairs.append((vw[:, it * DM:(it + 1) * DM], V["V"][gt]))
        p.dma_par(pairs)
        for h in range(NH):
            hp = 64 * (h % 2)
            hm = h // 2
            p.ts1(buf[:, 0:W],
                  u_sb[:, 0 * 1152 + u_off:0 * 1152 + u_off + W],
                  r_sb[:, h * 10:h * 10 + 1], OP.mult)
            for n in range(1, NBI):
                p.stt(buf[:, 0:W],
                      u_sb[:, n * 1152 + u_off:n * 1152 + u_off + W],
                      r_sb[:, h * 10 + n:h * 10 + n + 1],
                      buf[:, 0:W], OP.mult, OP.add)
            for c0 in range(0, W, 512):
                w = min(512, W - c0)
                p.mm(ps_s[:, c0:c0 + w],
                     qt_sb[hp:hp + 64, hm * 128:(hm + 1) * 128],
                     ktw[hp:hp + 64, hm * 1152 + c0:hm * 1152 + c0 + w])
            p.tt(buf[:, 0:W], ps_s[:, 0:W], buf[:, 0:W], OP.add)
            p.tt(buf[:, W - 128:W], buf[:, W - 128:W], m0_sb[:, :], OP.add)
            if nt == 9:
                p.tt(buf[:, 0:128], buf[:, 0:128], m8_sb[:, :], OP.add)
            p.red(st[:, 0:1], buf[:, 0:W], OP.max)
            p.ts1(st[:, 1:2], st[:, 0:1], -1.0, OP.mult)
            p.act(buf2[:, 0:W], buf[:, 0:W], AF.Exp, bias=st[:, 1:2],
                  scale=1.0)
            p.red(st[:, 2:3], buf2[:, 0:W], OP.add)
            p.recip(st[:, 3:4], st[:, 2:3])
            p.ts1(P_sb[:, 0:W], buf2[:, 0:W], st[:, 3:4], OP.mult)
            for kt in range(nt):
                p.trans(ptp[:, kt * 128:(kt + 1) * 128],
                        P_sb[:, kt * 128:(kt + 1) * 128], id_sb[:, :],
                        wait=(kt == 0), inc=(1 if kt == nt - 1 else 0))
            p.copy(PT_sb[:, 0:W], ptp[:, 0:W])
            for kt in range(nt):
                p.mm(av[0:64, :],
                     vw[:, kt * DM + h * 64:kt * DM + h * 64 + 64],
                     PT_sb[:, kt * 128:(kt + 1) * 128],
                     start=(kt == 0), stop=(kt == nt - 1),
                     wait=(kt == 0), inc=(1 if kt == nt - 1 else 0))
            p.copy(at_sb[hp:hp + 64, hm * 128:(hm + 1) * 128], av[0:64, :])
        p.dma(V["AT"][i], at_sb[:, 0:DM].rearrange("p (m c) -> p m c", m=8))


def emit_C(p, V, PSF, PSB, id_sb, ones):
    stage = p.sb([128, DM])
    wp_sb = p.sb([128, 8 * DM], BF)
    p.load_w_bf16(V["Wp"], DM, DM, wp_sb, stage)
    bpf = p.sb([1, DM])
    bpb = p.sb([1, DM], BF)
    p.dma(bpf[0:1, :], V["bp1"])
    p.copy(bpb[0:1, :], bpf[0:1, :])

    at_sb = p.sb([128, DM], BF)
    s_sb = p.sb([128, DM])
    x_sb = p.sb([128, DM])
    g_sb = p.sb([128, DM])
    x1_sb = p.sb([128, DM])
    xc = p.sb([128, DM])
    sq = p.sb([128, DM])
    st1 = p.sb([128, 4])
    st2 = p.sb([128, 4])
    xn = p.sb([128, DM], BF)
    xnT = p.sb([128, DM], BF)
    ps5 = PSF[:, 0:512]
    pst = PSB[:, 0:128]

    for i in range(NT):
        p.dma(at_sb[:, 0:DM].rearrange("p (m c) -> p m c", m=8), V["AT"][i])
        for nb in range(2):
            for m in range(8):
                p.mm(ps5[:, :], at_sb[:, m * 128:(m + 1) * 128],
                     wp_sb[:, m * DM + nb * 512:m * DM + nb * 512 + 512],
                     start=(m == 0), stop=False,
                     wait=(m == 0), inc=0)
            p.mm(ps5[:, :], ones[0:1, 0:128],
                 bpb[0:1, nb * 512:nb * 512 + 512], start=False, stop=True,
                 wait=False, inc=1)
            p.copy(s_sb[:, nb * 512:(nb + 1) * 512], ps5[:, :])
        p.dma_par([(x_sb[:, :], V["x"][i * 128:(i + 1) * 128, :]),
                   (g_sb[:, :], V["G"][i])])
        p.tt(s_sb[:, 0:DM], s_sb[:, 0:DM], g_sb[:, 0:DM], OP.mult)
        p.tt(x1_sb[:, 0:DM], x_sb[:, 0:DM], s_sb[:, 0:DM], OP.add)
        p.dma(V["X1"][i * 128:(i + 1) * 128, :], x1_sb[:, :])
        p.layernorm(x1_sb, xn, xc, sq, st1, st2)
        for k in range(8):
            p.trans(PSB[:, k * 128:(k + 1) * 128],
                    xn[:, k * 128:(k + 1) * 128], id_sb[:, :],
                    wait=(k == 0), inc=(1 if k == 7 else 0))
        p.copy(xnT[:, 0:DM], PSB[:, 0:DM])
        p.dma(V["XN1T"][i], xnT[:, 0:DM].rearrange("p (m c) -> p m c", m=8))


def emit_D(p, V, PSF, PSB, id_sb, ones):
    stage = p.sb([128, DM])
    wq_sb = p.sb([128, 8 * DM], BF)
    kvd_sb = p.sb([128, 8 * 512], BF)
    kup_sb = p.sb([128, 4 * DM], BF)
    vup_sb = p.sb([128, 4 * DM], BF)
    wg_sb = p.sb([128, 8 * 16], BF)
    p.load_w_bf16(V["wqT"], DM, DM, wq_sb, stage)
    p.load_w_bf16(V["kvdT"], DM, 512, kvd_sb, stage)
    p.load_w_bf16(V["kupT"], 512, DM, kup_sb, stage)
    p.load_w_bf16(V["vupT"], 512, DM, vup_sb, stage)
    p.load_w_bf16(V["WgT"], DM, 16, wg_sb, stage)
    bgf = p.sb([1, 16])
    bgb = p.sb([1, 16], BF)
    p.dma(bgf[0:1, :], V["bg1"])
    p.copy(bgb[0:1, :], bgf[0:1, :])
    cos_sb = p.sb([128, T])
    sin_sb = p.sb([128, T])
    p.dma(cos_sb[:, :], V["cosT"])
    p.dma(sin_sb[:, :], V["sinT"])
    sw_sb = p.sb([128, 128], BF)
    p.dma(sw_sb[:, :], V["SW"])

    xt_sb = p.sb([128, DM], BF)
    pre_sb = p.sb([128, DM], BF)
    q_sb = p.sb([128, DM], BF)
    k_sb = p.sb([128, DM], BF)
    c_sb = p.sb([128, 512], BF)
    v_sb = p.sb([128, DM], BF)
    tt1 = p.sb([128, 128])
    tt2 = p.sb([128, 128])
    lg = p.sb([128, 16])
    cwb = p.sb([128, 16])
    st = p.sb([128, 8])
    ps1 = PSF[:, 0:128]
    psw = PSF[:, 512:640]
    ps5 = PSF[:, 1024:1536]

    def rope(src_sb, dst_sb, i):
        for m in range(8):
            p.mm(psw[:, :], sw_sb[:, :], src_sb[:, m * 128:(m + 1) * 128])
            p.tt(tt1[:, :], src_sb[:, m * 128:(m + 1) * 128],
                 cos_sb[:, i * 128:(i + 1) * 128], OP.mult)
            p.tt(tt2[:, :], psw[:, :], sin_sb[:, i * 128:(i + 1) * 128],
                 OP.mult)
            p.tt(dst_sb[:, m * 128:(m + 1) * 128], tt1[:, :], tt2[:, :],
                 OP.add)

    for i in range(NT):
        p.dma(xt_sb[:, 0:DM].rearrange("p (m c) -> p m c", m=8), V["XN1T"][i])
        for m in range(8):
            for k in range(8):
                p.mm(ps1[:, :],
                     wq_sb[:, k * DM + m * 128:k * DM + (m + 1) * 128],
                     xt_sb[:, k * 128:(k + 1) * 128],
                     start=(k == 0), stop=(k == 7),
                     wait=(k == 0), inc=(1 if k == 7 else 0))
            p.copy(pre_sb[:, m * 128:(m + 1) * 128], ps1[:, :])
        rope(pre_sb, q_sb, i)
        for m in range(4):
            for k in range(8):
                p.mm(ps1[:, :],
                     kvd_sb[:, k * 512 + m * 128:k * 512 + (m + 1) * 128],
                     xt_sb[:, k * 128:(k + 1) * 128],
                     start=(k == 0), stop=(k == 7),
                     wait=(k == 0), inc=(1 if k == 7 else 0))
            p.copy(c_sb[:, m * 128:(m + 1) * 128], ps1[:, :])
        for m in range(8):
            for kc in range(4):
                p.mm(ps1[:, :],
                     kup_sb[:, kc * DM + m * 128:kc * DM + (m + 1) * 128],
                     c_sb[:, kc * 128:(kc + 1) * 128],
                     start=(kc == 0), stop=(kc == 3),
                     wait=(kc == 0), inc=(1 if kc == 3 else 0))
            p.copy(pre_sb[:, m * 128:(m + 1) * 128], ps1[:, :])
        rope(pre_sb, k_sb, i)
        p.dma(V["KT1"][i], k_sb[:, 0:DM].rearrange("p (m c) -> p m c", m=8))
        for nb in range(2):
            for kc in range(4):
                p.mm(ps5[:, :], c_sb[:, kc * 128:(kc + 1) * 128],
                     vup_sb[:, kc * DM + nb * 512:kc * DM + nb * 512 + 512],
                     start=(kc == 0), stop=(kc == 3),
                     wait=(kc == 0), inc=(1 if kc == 3 else 0))
            p.copy(v_sb[:, nb * 512:(nb + 1) * 512], ps5[:, :])
        for k in range(8):
            p.mm(ps1[:, 0:16], xt_sb[:, k * 128:(k + 1) * 128],
                 wg_sb[:, k * 16:(k + 1) * 16],
                 start=(k == 0), stop=False, wait=(k == 0), inc=0)
        p.mm(ps1[:, 0:16], ones[0:1, 0:128], bgb[0:1, :], start=False,
             stop=True, wait=False, inc=1)
        p.copy(lg[:, :], ps1[:, 0:16])
        p.red(st[:, 0:1], lg[:, 0:16], OP.max)
        p.ts1(st[:, 1:2], st[:, 0:1], -1.0, OP.mult)
        p.ts1(cwb[:, 0:16], lg[:, 0:16], st[:, 0:1], OP.is_ge)
        p.stt(cwb[:, 0:16], cwb[:, 0:16], -1e30, lg[:, 0:16], OP.mult, OP.add)
        p.red(st[:, 2:3], cwb[:, 0:16], OP.max)
        p.ts1(cwb[:, 0:16], lg[:, 0:16], st[:, 2:3], OP.is_ge)
        p.act(lg[:, 0:16], lg[:, 0:16], AF.Exp, bias=st[:, 1:2], scale=1.0)
        p.tt(lg[:, 0:16], lg[:, 0:16], cwb[:, 0:16], OP.mult)
        p.red(st[:, 3:4], lg[:, 0:16], OP.add)
        p.recip(st[:, 4:5], st[:, 3:4])
        p.ts1(cwb[:, 0:16], lg[:, 0:16], st[:, 4:5], OP.mult)
        p.dma(V["QT1"][i], q_sb[:, 0:DM].rearrange("p (m c) -> p m c", m=8))
        p.dma(V["V1"][i], v_sb[:, :])
        p.dma(V["CW"][i], cwb[:, :])


def emit_E(p, V, PSF, PSB, id_sb, ones):
    m_sb = p.sb([128, 4 * 512], BF)
    for v in range(4):
        p.dma(m_sb[:, v * 512:(v + 1) * 512], V["M01"][v])
    on_sb = p.sb([1, 64])
    p.dma(on_sb[0:1, :], V["ones64"])

    qb_sb = p.sb([64, 512], BF)
    kt_sb = p.sb([64, 128], BF)
    vext = p.sb([128, 65], BF)
    p.memset(vext[:, 64:65], 1.0)
    pt_sb = p.sb([128, 512], BF)
    rs_sb = p.sb([1, 512])
    bc_sb = p.sb([64, 512])
    atn_sb = p.sb([64, 512], BF)
    ps_s = PSF[:, 0:512]
    av = PSF[:, 512:1024]
    bc = PSF[0:64, 1024:1536]

    for h in range(NH):
        hp = 64 * (h % 2)
        hm = h // 2
        for qb in range(8):
            p.dma(qb_sb[0:64, 0:512].rearrange("p (i t) -> p i t", i=4),
                  V["QT1"][4 * qb:4 * qb + 4, hp:hp + 64, hm, :]
                  .rearrange("i p t -> p i t"))
            last = 4 * qb + 3
            for kt in range(4 * qb + 4):
                p.dma_par([(kt_sb[0:64, :], V["KT1"][kt, hp:hp + 64, hm, :]),
                           (vext[:, 0:64], V["V1"][kt, :, h * 64:h * 64 + 64])])
                p.mm(ps_s[:, :], kt_sb[0:64, :], qb_sb[0:64, :])
                p.act(pt_sb[:, :], ps_s[:, :], AF.Exp, scale=0.125)
                if kt >= 4 * qb:
                    vv = kt - 4 * qb
                    p.tt(pt_sb[:, :], pt_sb[:, :],
                         m_sb[:, vv * 512:(vv + 1) * 512], OP.mult)
                p.mm(av[0:65, :], vext[:, 0:65], pt_sb[:, :],
                     start=(kt == 0), stop=(kt == last),
                     wait=True, inc=1)
            p.recip(rs_sb[0:1, :], av[64:65, :])
            p.mm(bc[0:64, :], on_sb[0:1, :], rs_sb[0:1, :])
            p.copy(bc_sb[:, :], bc[0:64, :])
            p.tt(atn_sb[:, :], av[0:64, :], bc_sb[:, :], OP.mult)
            p.dma(V["ATN"][4 * qb:4 * qb + 4, hp:hp + 64, hm, :]
                  .rearrange("i p t -> p i t"),
                  atn_sb[0:64, 0:512].rearrange("p (i t) -> p i t", i=4))


def emit_EP(p, V, PSF, PSB, id_sb, ones):
    stage = p.sb([128, DM])
    wo_sb = p.sb([128, 8 * DM], BF)
    p.load_w_bf16(V["woT"], DM, DM, wo_sb, stage)
    a_sb = p.sb([128, DM], BF)
    s_sb = p.sb([128, DM])
    ps5 = PSF[:, 0:512]
    for i in range(NT):
        p.dma(a_sb[:, 0:DM].rearrange("p (m c) -> p m c", m=8), V["ATN"][i])
        for nb in range(2):
            for m in range(8):
                p.mm(ps5[:, :], a_sb[:, m * 128:(m + 1) * 128],
                     wo_sb[:, m * DM + nb * 512:m * DM + nb * 512 + 512],
                     start=(m == 0), stop=(m == 7),
                     wait=(m == 0), inc=(1 if m == 7 else 0))
            p.copy(s_sb[:, nb * 512:(nb + 1) * 512], ps5[:, :])
        p.dma(V["S1"][i * 128:(i + 1) * 128, :], s_sb[:, :])


def emit_F(p, V, PSF, PSB, id_sb, ones):
    stage = p.sb([128, 3 * 2048])
    wgu_sb = p.sb([128, 8 * 2048], BF)
    wd_sb = p.sb([128, 8 * DM], BF)
    bguf = p.sb([1, 2048])
    bgub = p.sb([1, 2048], BF)
    bdf = p.sb([1, DM])
    bdb = p.sb([1, DM], BF)

    xb_sb = p.sb([128, 8 * 512], BF)
    acc_sb = p.sb([128, 4 * DM])
    cw_sb = p.sb([128, 4 * 16])
    ga = p.sb([128, 512])
    gb = p.sb([128, 512])
    ua = p.sb([128, 512])
    ub = p.sb([128, 512])
    gated = p.sb([128, 8 * 512], BF)
    x1t = p.sb([128, DM])
    s1t = p.sb([128, DM])
    g1 = p.sb([128, DM])
    o_sb = p.sb([128, DM])
    pg = PSF[:, 0:512]
    pu = PSF[:, 512:1024]
    pd = PSF[:, 0:1024]

    for tb in range(8):
        pairs = [(xb_sb[:, k * 512:(k + 1) * 512]
                  .rearrange("p (i t) -> p i t", i=4),
                  V["XN1T"][4 * tb:4 * tb + 4, :, k, :]
                  .rearrange("i p t -> p i t")) for k in range(8)]
        pairs += [(cw_sb[:, t * 16:(t + 1) * 16], V["CW"][4 * tb + t])
                  for t in range(4)]
        p.dma_par(pairs)
        p.memset(acc_sb[:, 0:4 * DM], 0.0)
        for ge in range(16):
            for k0 in range(0, 8, 3):
                ks = list(range(k0, min(k0 + 3, 8)))
                p.dma_par([(stage[:, (k - k0) * 2048:(k - k0) * 2048 + 2048],
                            V["Wgu"][ge, k * 128:(k + 1) * 128, :])
                           for k in ks])
                for k in ks:
                    p.copy(wgu_sb[:, k * 2048:(k + 1) * 2048],
                           stage[:, (k - k0) * 2048:(k - k0) * 2048 + 2048])
            for k0 in range(0, 8, 3):
                ks = list(range(k0, min(k0 + 3, 8)))
                p.dma_par([(stage[:, (k - k0) * 2048:(k - k0) * 2048 + DM],
                            V["Wd"][ge, k * 128:(k + 1) * 128, :])
                           for k in ks])
                for k in ks:
                    p.copy(wd_sb[:, k * DM:(k + 1) * DM],
                           stage[:, (k - k0) * 2048:(k - k0) * 2048 + DM])
            p.dma_par([(bguf[0:1, :], V["bgu"][ge:ge + 1, :]),
                       (bdf[0:1, :], V["bd"][ge:ge + 1, :])])
            p.copy(bgub[0:1, :], bguf[0:1, :])
            p.copy(bdb[0:1, :], bdf[0:1, :])
            for j in range(8):
                for (col0, psx) in ((j * 128, pg), (1024 + j * 128, pu)):
                    for k in range(8):
                        p.mm(psx[:, :],
                             wgu_sb[:, k * 2048 + col0:k * 2048 + col0 + 128],
                             xb_sb[:, k * 512:(k + 1) * 512],
                             start=(k == 0), stop=False,
                             wait=(k == 0), inc=0)
                    p.mm(psx[:, :], bgub[0:1, col0:col0 + 128],
                         ones[0:1, 0:512], start=False, stop=True,
                         wait=False, inc=1)
                p.ts1(ga[:, :], pg[:, :], 7.0, OP.min)
                p.act(gb[:, :], ga[:, :], AF.Silu, scale=1.702)
                p.ts(ua[:, :], pu[:, :], 7.0, -7.0, OP.min, OP.max)
                p.act(ub[:, :], ua[:, :], AF.Copy, scale=1.0 / 1.702,
                      bias=1.0 / 1.702)
                p.tt(gated[:, j * 512:(j + 1) * 512], gb[:, :], ub[:, :],
                     OP.mult)
            for t in range(4):
                for nb in range(2):
                    for j in range(8):
                        p.mm(pd[:, nb * 512:nb * 512 + 512],
                             gated[:, j * 512 + t * 128:
                                    j * 512 + t * 128 + 128],
                             wd_sb[:, j * DM + nb * 512:
                                   j * DM + nb * 512 + 512],
                             start=(j == 0), stop=False,
                             wait=(j == 0 and nb == 0), inc=0)
                    p.mm(pd[:, nb * 512:nb * 512 + 512], ones[0:1, 0:128],
                         bdb[0:1, nb * 512:nb * 512 + 512], start=False,
                         stop=True, wait=False, inc=(1 if nb == 1 else 0))
                p.stt(acc_sb[:, t * DM:(t + 1) * DM], pd[:, 0:DM],
                      cw_sb[:, t * 16 + ge:t * 16 + ge + 1],
                      acc_sb[:, t * DM:(t + 1) * DM], OP.mult, OP.add)
        for t in range(4):
            r0 = (4 * tb + t) * 128
            p.dma_par([(x1t[:, :], V["X1"][r0:r0 + 128, :]),
                       (s1t[:, :], V["S1"][r0:r0 + 128, :])])
            p.act(g1[:, 0:DM], acc_sb[:, t * DM:(t + 1) * DM], AF.Silu)
            p.tt(g1[:, 0:DM], s1t[:, 0:DM], g1[:, 0:DM], OP.mult)
            p.tt(o_sb[:, 0:DM], x1t[:, 0:DM], g1[:, 0:DM], OP.add)
            p.dma(V["OUT"][r0:r0 + 128, :], o_sb[:, :])


def build_fused():
    p = Prog()
    p.setup_consts()
    V = {}
    V["x"] = p.inp("x", [T, DM])
    V["Wq"] = p.inp("Wq", [DM, DM])
    V["Wk"] = p.inp("Wk", [DM, DM])
    V["Wv"] = p.inp("Wv", [DM, DM])
    V["Wr"] = p.inp("Wr", [DM, 160])
    V["WfT"] = p.inp("WfT", [DM, DM])
    V["bqc"] = p.inp("bqc", [128, 8])
    V["br1"] = p.inp("br1", [1, 160])
    V["scb"] = p.inp("scb", [128, DM])
    V["bib"] = p.inp("bib", [128, DM])
    V["ident"] = p.inp("ident", [128, 128], BF)
    V["U"] = p.inp("U", [NBI, 128, 1152])
    V["MV0"] = p.inp("MV0", [128, 128])
    V["MV8"] = p.inp("MV8", [128, 128])
    V["Wp"] = p.inp("Wp", [DM, DM])
    V["bp1"] = p.inp("bp1", [1, DM])
    V["wqT"] = p.inp("wqT", [DM, DM])
    V["kvdT"] = p.inp("kvdT", [DM, 512])
    V["kupT"] = p.inp("kupT", [512, DM])
    V["vupT"] = p.inp("vupT", [512, DM])
    V["WgT"] = p.inp("WgT", [DM, 16])
    V["bg1"] = p.inp("bg1", [1, 16])
    V["cosT"] = p.inp("cosT", [128, T])
    V["sinT"] = p.inp("sinT", [128, T])
    V["SW"] = p.inp("SW", [128, 128], BF)
    V["M01"] = p.inp("M01", [4, 128, 512], BF)
    V["ones64"] = p.inp("ones64", [1, 64])
    V["woT"] = p.inp("woT", [DM, DM])
    V["Wgu"] = p.inp("Wgu", [16, DM, 2048])
    V["bgu"] = p.inp("bgu", [16, 2048])
    V["Wd"] = p.inp("Wd", [16, DM, DM])
    V["bd"] = p.inp("bd", [16, DM])
    V["QT"] = p.scratch("sQT", [NT, 128, 8, 128], BF)
    V["KT"] = p.scratch("sKT", [NT, 128, 8, 128], BF)
    V["V"] = p.scratch("sV", [NT, 128, DM], BF)
    V["R"] = p.scratch("sR", [NT, 128, 160])
    V["G"] = p.scratch("sG", [NT, 128, DM])
    V["AT"] = p.scratch("sAT", [NT, 128, 8, 128], BF)
    V["X1"] = p.scratch("sX1", [T, DM])
    V["XN1T"] = p.scratch("sXN1T", [NT, 128, 8, 128], BF)
    V["QT1"] = p.scratch("sQT1", [NT, 128, 8, 128], BF)
    V["KT1"] = p.scratch("sKT1", [NT, 128, 8, 128], BF)
    V["V1"] = p.scratch("sV1", [NT, 128, DM], BF)
    V["CW"] = p.scratch("sCW", [NT, 128, 16])
    V["ATN"] = p.scratch("sATN", [NT, 128, 8, 128], BF)
    V["S1"] = p.scratch("sS1", [T, DM])
    V["OUT"] = p.outp("OUT", [T, DM])

    PSF = p.ps([128, 1536], F32)
    PSB = p.ps([128, 1152], BF)
    id_sb = p.sb([128, 128], BF)
    ones = p.sb([1, 512], BF)
    p.dma(id_sb[:, :], V["ident"])
    p.memset(ones[0:1, :], 1.0)
    p.mark_persist()

    for emit in (emit_A, emit_B, emit_C, emit_D, emit_E, emit_EP, emit_F):
        emit(p, V, PSF, PSB, id_sb, ones)
        p.stage_reset()
    return p.finalize()


_cache = {}


def _consts():
    if "consts" in _cache:
        return _cache["consts"]
    c = {}
    c["ident"] = np.eye(128, dtype=BF_NP)
    q = np.arange(128)[:, None]
    k = np.arange(128)[None, :]
    c["MV0"] = np.where(k <= q, 0.0, -1e9).astype(np.float32)
    c["MV8"] = np.where(k > q, 0.0, -1e9).astype(np.float32)
    r = np.arange(128)[None, :, None]
    cc = np.arange(512)[None, None, :]
    v = np.arange(4)[:, None, None]
    c["M01"] = (128 * v + r <= cc).astype(BF_NP)
    c["ones64"] = np.ones((1, 64), dtype=np.float32)
    d = np.arange(64)
    inv = 1.0 / (10000.0 ** (2 * (d % 32) / 64.0))
    pos = np.arange(T)
    ang = pos[None, :] * inv[:, None]
    cos = np.cos(ang)
    sin = np.sin(ang)
    sgn = np.where(d < 32, -1.0, 1.0)[:, None]
    c["cosT"] = np.tile(cos, (2, 1)).astype(np.float32)
    c["sinT"] = np.tile(sin * sgn, (2, 1)).astype(np.float32)
    S = np.zeros((128, 128), dtype=BF_NP)
    for m in range(128):
        sig = m + 32 if (m % 64) < 32 else m - 32
        S[sig, m] = 1.0
    c["SW"] = S
    _cache["consts"] = c
    return c


def _utables(b_nd):
    qq = np.arange(128)[:, None]
    jj = np.arange(1152)[None, :]
    idx = 1024 + qq - jj
    ok = (idx >= 0) & (idx < 1024)
    idxc = np.clip(idx, 0, 1023)
    U = np.where(ok[None, :, :], b_nd[:, idxc], 0.0).astype(np.float32)
    return U


def kernel(L0_attn_X, L0_attn_Wq, L0_attn_bq, L0_attn_Wk, L0_attn_Wv,
           L0_attn_Wproj, L0_attn_bproj, L0_attn_Wr, L0_attn_br,
           L0_attn_b_nd, L0_ffn_weights, L0_ffn_scale, L0_ffn_bias,
           L1_attn_wq, L1_attn_w_kv_down, L1_attn_w_k_up, L1_attn_w_v_up,
           L1_attn_wo, L1_ffn_Wg, L1_ffn_bg, L1_ffn_Wgu, L1_ffn_bgu,
           L1_ffn_Wd, L1_ffn_bd):
    if "fused" not in _cache:
        _cache["fused"] = build_fused()
    nc = _cache["fused"]
    C = _consts()
    f32 = np.float32
    asf = np.ascontiguousarray
    X = np.asarray(L0_attn_X, f32)

    perm = np.concatenate([np.arange(0, 2048, 2), np.arange(1, 2048, 2)])
    shared = dict(
        Wq=np.asarray(L0_attn_Wq, f32), Wk=np.asarray(L0_attn_Wk, f32),
        Wv=np.asarray(L0_attn_Wv, f32), Wr=np.asarray(L0_attn_Wr, f32),
        WfT=asf(np.asarray(L0_ffn_weights, f32).T),
        bqc=asf(np.asarray(L0_attn_bq, f32).reshape(8, 128).T),
        br1=np.asarray(L0_attn_br, f32).reshape(1, 160),
        scb=asf(np.broadcast_to(np.asarray(L0_ffn_scale, f32), (128, DM))),
        bib=asf(np.broadcast_to(np.asarray(L0_ffn_bias, f32), (128, DM))),
        ident=C["ident"], U=_utables(np.asarray(L0_attn_b_nd, f32)),
        MV0=C["MV0"], MV8=C["MV8"],
        Wp=np.asarray(L0_attn_Wproj, f32),
        bp1=np.asarray(L0_attn_bproj, f32).reshape(1, DM),
        wqT=asf(np.asarray(L1_attn_wq, f32).T),
        kvdT=asf(np.asarray(L1_attn_w_kv_down, f32).T),
        kupT=asf(np.asarray(L1_attn_w_k_up, f32).T),
        vupT=asf(np.asarray(L1_attn_w_v_up, f32).T),
        WgT=asf(np.asarray(L1_ffn_Wg, f32).T),
        bg1=np.asarray(L1_ffn_bg, f32).reshape(1, 16),
        cosT=C["cosT"], sinT=C["sinT"], SW=C["SW"], M01=C["M01"],
        ones64=C["ones64"],
        woT=asf(np.asarray(L1_attn_wo, f32).T),
        Wgu=asf(np.asarray(L1_ffn_Wgu, f32)[:, :, perm]),
        bgu=asf(np.asarray(L1_ffn_bgu, f32)[:, perm]),
        Wd=np.asarray(L1_ffn_Wd, f32),
        bd=np.asarray(L1_ffn_bd, f32),
    )
    maps = [dict(x=asf(X[c]), **shared) for c in range(8)]
    res = run_bass_kernel_spmd(nc, maps, core_ids=list(range(8)))
    out = np.stack([res.results[c]["OUT"] for c in range(8)], axis=0)
    return out.astype(np.float32)


# version 06 (final)
# speedup vs baseline: 1.3725x; 1.1289x over previous
import math
import numpy as np
import ml_dtypes
import concourse.bass as bass
import concourse.mybir as mybir
from concourse.bass_utils import run_bass_kernel_spmd

F32 = mybir.dt.float32
BF = mybir.dt.bfloat16
AF = mybir.ActivationFunctionType
OP = mybir.AluOpType
AX = mybir.AxisListType

BF_NP = ml_dtypes.bfloat16
T = 4096
DM = 1024
NH = 16
NBI = 10
NT = T // 128
BF_COLS = 40960
F32_COLS = 20480


class Prog:
    def __init__(self):
        self.nc = bass.Bass("TRN2", target_bir_lowering=False)
        self.sched = []
        self.t = 0
        self.chain = self.nc.alloc_semaphore("chain")
        self._uid = 0
        self._back = {
            BF: self.nc.alloc_sbuf_tensor("arena_bf", [128, BF_COLS], BF),
            F32: self.nc.alloc_sbuf_tensor("arena_f32", [128, F32_COLS], F32),
        }
        self._ptr = {BF: 0, F32: 0}
        self._persist = {BF: 0, F32: 0}

    def inp(self, name, shape, dt=F32):
        return self.nc.dram_tensor(name, shape, dt, kind="ExternalInput").ap()

    def outp(self, name, shape, dt=F32):
        return self.nc.dram_tensor(name, shape, dt, kind="ExternalOutput").ap()

    def scratch(self, name, shape, dt=F32):
        return self.nc.dram_tensor(name, shape, dt, kind="Internal").ap()

    def sb(self, shape, dt=F32):
        cols = (shape[1] + 15) // 16 * 16
        off = self._ptr[dt]
        self._ptr[dt] = off + cols
        assert self._ptr[dt] <= self._back[dt].shape[1], \
            "arena overflow %s %d" % (dt, self._ptr[dt])
        return self._back[dt][0:shape[0], off:off + shape[1]]

    def ps(self, shape, dt=F32):
        self._uid += 1
        return self.nc.alloc_psum_tensor("ps%d" % self._uid, shape, dt)

    def mark_persist(self):
        self._persist = dict(self._ptr)

    def stage_reset(self):
        self._ptr = dict(self._persist)

    def _op(self, eng, fn, inc=1, wait=True, wait_at=None):
        w = wait_at if wait_at is not None else (self.t if wait else -1)
        self.sched.append((eng, fn, w, inc))
        self.t += inc

    def dma(self, out, in_):
        def f(e, o=out, i=in_):
            return e.dma_start(out=o, in_=i)
        self._op("sync", f, inc=16)

    def dma_par(self, pairs):
        engs = ("sync", "gpsimd", "scalar")
        t0 = self.t
        for idx, (out, in_) in enumerate(pairs):
            def f(e, o=out, i=in_):
                return e.dma_start(out=o, in_=i)
            self._op(engs[idx % len(engs)], f, inc=16, wait_at=t0)

    def mm(self, out, lhsT, rhs, start=True, stop=True, wait=True, inc=1):
        def f(e, o=out, l=lhsT, r=rhs, st=start, sp=stop):
            return e.matmul(o, l, r, start=st, stop=sp, skip_group_check=True)
        self._op("tensor", f, inc=inc, wait=wait)

    def trans(self, out, in_, ident, wait=True, inc=1):
        def f(e, o=out, i=in_, d=ident):
            return e.transpose(o, i, d)
        self._op("tensor", f, inc=inc, wait=wait)

    def tt(self, out, in0, in1, op):
        def f(e, o=out, a=in0, b=in1, p=op):
            return e.tensor_tensor(o, a, b, p)
        self._op("vector", f)

    def ts(self, out, in0, s1, s2, op0, op1):
        def f(e, o=out, a=in0, x=s1, y=s2, p0=op0, p1=op1):
            return e.tensor_scalar(o, a, x, y, p0, p1)
        self._op("vector", f)

    def ts1(self, out, in0, s1, op0):
        if op0 is OP.mult:
            def f(e, o=out, a=in0, x=s1):
                return e.tensor_scalar_mul(o, a, x)
        elif op0 is OP.add:
            def f(e, o=out, a=in0, x=s1):
                return e.tensor_scalar_add(o, a, x)
        elif op0 is OP.min:
            def f(e, o=out, a=in0, x=s1):
                return e.tensor_scalar_min(o, a, x)
        else:
            def f(e, o=out, a=in0, x=s1, p=op0):
                return e.tensor_scalar(o, a, x, None, p)
        self._op("vector", f)

    def stt(self, out, in0, scalar, in1, op0, op1):
        def f(e, o=out, a=in0, s=scalar, b=in1, p0=op0, p1=op1):
            return e.scalar_tensor_tensor(o, a, s, b, p0, p1)
        self._op("vector", f)

    def red(self, out, in_, op):
        def f(e, o=out, i=in_, p=op):
            return e.tensor_reduce(o, i, AX.X, p)
        self._op("vector", f)

    def recip(self, out, in_):
        def f(e, o=out, i=in_):
            return e.reciprocal(o, i)
        self._op("vector", f)

    def memset(self, ap, c):
        def f(e, a=ap, v=c):
            return e.memset(a, v)
        self._op("vector", f)

    def act(self, out, in_, func, bias=None, scale=None):
        def f(e, o=out, i=in_, fu=func, b=bias, s=scale):
            kw = {}
            if b is not None:
                kw["bias"] = b
            if s is not None:
                kw["scale"] = s
            return e.activation(o, i, fu, **kw)
        self._op("scalar", f)

    def copy(self, out, in_):
        def f(e, o=out, i=in_):
            return e.copy(o, i)
        self._op("scalar", f)

    def sign(self, out, in_):
        def f(e, o=out, i=in_):
            return e.sign(o, i)
        self._op("scalar", f)

    def finalize(self):
        nc = self.nc
        chain = self.chain
        total = self.t
        self.sched.append(
            ("sync", lambda e, c=chain, v=total: e.wait_ge(c, v), -1, 0))
        with nc.Block() as blk:
            for engname in ("sync", "tensor", "vector", "scalar", "gpsimd"):
                entries = [(f, w, i) for (e, f, w, i) in self.sched
                           if e == engname]
                if not entries:
                    continue

                def mk(entries, c=chain):
                    def body(eng):
                        for (f, w, i) in entries:
                            if w > 0:
                                eng.wait_ge(c, w)
                            ins = f(eng)
                            if i:
                                ins.then_inc(c, i)
                    return body
                getattr(blk, engname)(mk(entries))
        return nc

    def setup_consts(self):
        vals = [0.0, 1.0, -1.0, 1e-5, 1.0 / 1024, -1.0 / 1024, 0.125,
                1.702, 1.0 / 1.702, 7.0, -7.0, 100.0, -100.0, -1e30]
        cs = self.sb([128, len(vals)])
        cb = self.sb([128, len(vals)], BF)
        for j, v in enumerate(vals):
            self.memset(cs[:, j:j + 1], v)
            self.memset(cb[:, j:j + 1], v)
            self.nc.const_aps.register(F32, float(v), cs[:, j:j + 1])
            self.nc.const_aps.register(BF, float(v), cb[:, j:j + 1])

    def load_w_bf16(self, src_ap, rows, cols, dst_sb, stage, do_sign=False):
        nk = rows // 128
        for k in range(nk):
            self.dma(stage[:, 0:cols], src_ap[k * 128:(k + 1) * 128, :])
            if do_sign:
                self.sign(dst_sb[:, k * cols:(k + 1) * cols], stage[:, 0:cols])
            else:
                self.copy(dst_sb[:, k * cols:(k + 1) * cols], stage[:, 0:cols])

    def layernorm(self, x_sb, out_bf, tmp_xc, tmp_sq, st1, st2):
        self.red(st1[:, 0:1], x_sb[:, 0:DM], OP.add)
        self.ts1(st1[:, 1:2], st1[:, 0:1], -1.0 / DM, OP.mult)
        self.ts1(tmp_xc[:, 0:DM], x_sb[:, 0:DM], st1[:, 1:2], OP.add)
        self.tt(tmp_sq[:, 0:DM], tmp_xc[:, 0:DM], tmp_xc[:, 0:DM], OP.mult)
        self.red(st2[:, 0:1], tmp_sq[:, 0:DM], OP.add)
        self.act(st2[:, 1:2], st2[:, 0:1], AF.Sqrt, bias=1e-5, scale=1.0 / DM)
        self.recip(st2[:, 2:3], st2[:, 1:2])
        self.ts1(out_bf[:, 0:DM], tmp_xc[:, 0:DM], st2[:, 2:3], OP.mult)


def emit_A(p, V, PSF, PSB, id_sb, ones):
    stage = p.sb([128, DM])
    wq_sb = p.sb([128, 8 * DM], BF)
    wk_sb = p.sb([128, 8 * DM], BF)
    wv_sb = p.sb([128, 8 * DM], BF)
    wr_sb = p.sb([128, 8 * 160], BF)
    wf_sb = p.sb([128, 8 * DM], BF)
    bq_sb = p.sb([128, 8])
    bq8 = p.sb([128, 8])
    brf = p.sb([1, 160])
    brb = p.sb([1, 160], BF)
    sc_sb = p.sb([128, DM])
    bi_sb = p.sb([128, DM])

    p.load_w_bf16(V["Wq"], DM, DM, wq_sb, stage)
    p.load_w_bf16(V["Wk"], DM, DM, wk_sb, stage)
    p.load_w_bf16(V["Wv"], DM, DM, wv_sb, stage)
    p.load_w_bf16(V["Wr"], DM, 160, wr_sb, stage)
    p.load_w_bf16(V["WfT"], DM, DM, wf_sb, stage, do_sign=True)
    p.dma(bq_sb[:, :], V["bqc"])
    p.ts1(bq8[:, 0:8], bq_sb[:, 0:8], 0.125, OP.mult)
    p.dma(brf[0:1, :], V["br1"])
    p.copy(brb[0:1, :], brf[0:1, :])
    p.dma(sc_sb[:, :], V["scb"])
    p.dma(bi_sb[:, :], V["bib"])

    x_sb = p.sb([128, DM])
    xc = p.sb([128, DM])
    sq = p.sb([128, DM])
    st1 = p.sb([128, 4])
    st2 = p.sb([128, 4])
    xn = p.sb([128, DM], BF)
    xnT = p.sb([128, DM], BF)
    qt_sb = p.sb([128, DM], BF)
    kt_sb = p.sb([128, DM], BF)
    v_sb = p.sb([128, DM], BF)
    r_sb = p.sb([128, 160])
    y_sb = p.sb([128, DM])
    g_sb = p.sb([128, DM])
    pst = PSB[:, 0:128]
    ps1 = PSF[:, 0:128]
    ps5 = PSF[:, 512:1024]
    psr = PSF[:, 1024:1184]

    for i in range(NT):
        p.dma(x_sb[:, :], V["x"][i * 128:(i + 1) * 128, :])
        p.layernorm(x_sb, xn, xc, sq, st1, st2)
        for k in range(8):
            p.trans(PSB[:, k * 128:(k + 1) * 128],
                    xn[:, k * 128:(k + 1) * 128], id_sb[:, :],
                    wait=(k == 0), inc=(1 if k == 7 else 0))
        p.copy(xnT[:, 0:DM], PSB[:, 0:DM])
        for (wsb, dst, biased) in ((wq_sb, qt_sb, True), (wk_sb, kt_sb, False)):
            for m in range(8):
                for k in range(8):
                    p.mm(ps1[:, :],
                         wsb[:, k * DM + m * 128:k * DM + (m + 1) * 128],
                         xnT[:, k * 128:(k + 1) * 128],
                         start=(k == 0), stop=(k == 7),
                         wait=(k == 0), inc=(1 if k == 7 else 0))
                if biased:
                    p.ts(dst[:, m * 128:(m + 1) * 128], ps1[:, :], 0.125,
                         bq8[:, m:m + 1], OP.mult, OP.add)
                else:
                    p.copy(dst[:, m * 128:(m + 1) * 128], ps1[:, :])
        for nb in range(2):
            for k in range(8):
                p.mm(ps5[:, :], xnT[:, k * 128:(k + 1) * 128],
                     wv_sb[:, k * DM + nb * 512:k * DM + nb * 512 + 512],
                     start=(k == 0), stop=(k == 7),
                     wait=(k == 0), inc=(1 if k == 7 else 0))
            p.copy(v_sb[:, nb * 512:(nb + 1) * 512], ps5[:, :])
        for k in range(8):
            p.mm(psr[:, :], xnT[:, k * 128:(k + 1) * 128],
                 wr_sb[:, k * 160:(k + 1) * 160],
                 start=(k == 0), stop=False,
                 wait=(k == 0), inc=0)
        p.mm(psr[:, :], ones[0:1, 0:128], brb[0:1, :], start=False, stop=True,
             wait=False, inc=1)
        p.copy(r_sb[:, :], psr[:, :])
        for nb in range(2):
            for k in range(8):
                p.mm(ps5[:, :], xnT[:, k * 128:(k + 1) * 128],
                     wf_sb[:, k * DM + nb * 512:k * DM + nb * 512 + 512],
                     start=(k == 0), stop=(k == 7),
                     wait=(k == 0), inc=(1 if k == 7 else 0))
            p.copy(y_sb[:, nb * 512:(nb + 1) * 512], ps5[:, :])
        p.tt(y_sb[:, 0:DM], y_sb[:, 0:DM], sc_sb[:, 0:DM], OP.mult)
        p.tt(y_sb[:, 0:DM], y_sb[:, 0:DM], bi_sb[:, 0:DM], OP.add)
        p.ts(y_sb[:, 0:DM], y_sb[:, 0:DM], 100.0, -100.0, OP.min, OP.max)
        p.act(g_sb[:, 0:DM], y_sb[:, 0:DM], AF.Silu)
        p.dma_par([
            (V["QT"][i], qt_sb[:, 0:DM].rearrange("p (m c) -> p m c", m=8)),
            (V["KT"][i], kt_sb[:, 0:DM].rearrange("p (m c) -> p m c", m=8)),
            (V["V"][i], v_sb[:, :]),
            (V["R"][i], r_sb[:, :]),
            (V["G"][i], g_sb[:, :])])


def emit_B(p, V, PSF, PSB, id_sb, ones):
    u_sb = p.sb([128, NBI * 1152])
    for n in range(NBI):
        p.dma(u_sb[:, n * 1152:(n + 1) * 1152], V["U"][n])
    m0_sb = p.sb([128, 128])
    m8_sb = p.sb([128, 128])
    p.dma(m0_sb[:, :], V["MV0"])
    p.dma(m8_sb[:, :], V["MV8"])

    qt_sb = p.sb([128, DM], BF)
    r_sb = p.sb([128, 160])
    ktw = p.sb([128, 8 * 1152], BF)
    vw = p.sb([128, 9 * DM], BF)
    buf = p.sb([128, 1152])
    buf2 = p.sb([128, 1152])
    P_sb = p.sb([128, 1152], BF)
    PT_sb = p.sb([128, 1152], BF)
    at_sb = p.sb([128, DM], BF)
    st = p.sb([128, 4])
    ps_s = PSF[:, 0:1536]
    ptp = PSB[:, 0:1152]
    av = PSF[0:64, 0:128]

    for i in range(NT):
        glo = max(0, i - 8)
        nt = i - glo + 1
        W = nt * 128
        u_off = (9 - nt) * 128
        pairs = [(qt_sb[:, 0:DM].rearrange("p (m c) -> p m c", m=8),
                  V["QT"][i]),
                 (r_sb[:, :], V["R"][i])]
        for m in range(8):
            pairs.append((ktw[:, m * 1152:m * 1152 + W]
                          .rearrange("p (i t) -> p i t", i=nt),
                          V["KT"][glo:i + 1, :, m, :]
                          .rearrange("i p t -> p i t")))
        for it, gt in enumerate(range(glo, i + 1)):
            pairs.append((vw[:, it * DM:(it + 1) * DM], V["V"][gt]))
        p.dma_par(pairs)
        for h in range(NH):
            hp = 64 * (h % 2)
            hm = h // 2
            p.ts1(buf[:, 0:W],
                  u_sb[:, 0 * 1152 + u_off:0 * 1152 + u_off + W],
                  r_sb[:, h * 10:h * 10 + 1], OP.mult)
            for n in range(1, NBI):
                p.stt(buf[:, 0:W],
                      u_sb[:, n * 1152 + u_off:n * 1152 + u_off + W],
                      r_sb[:, h * 10 + n:h * 10 + n + 1],
                      buf[:, 0:W], OP.mult, OP.add)
            for c0 in range(0, W, 512):
                w = min(512, W - c0)
                p.mm(ps_s[:, c0:c0 + w],
                     qt_sb[hp:hp + 64, hm * 128:(hm + 1) * 128],
                     ktw[hp:hp + 64, hm * 1152 + c0:hm * 1152 + c0 + w])
            p.tt(buf[:, 0:W], ps_s[:, 0:W], buf[:, 0:W], OP.add)
            p.tt(buf[:, W - 128:W], buf[:, W - 128:W], m0_sb[:, :], OP.add)
            if nt == 9:
                p.tt(buf[:, 0:128], buf[:, 0:128], m8_sb[:, :], OP.add)
            p.red(st[:, 0:1], buf[:, 0:W], OP.max)
            p.ts1(st[:, 1:2], st[:, 0:1], -1.0, OP.mult)
            p.act(buf2[:, 0:W], buf[:, 0:W], AF.Exp, bias=st[:, 1:2],
                  scale=1.0)
            p.red(st[:, 2:3], buf2[:, 0:W], OP.add)
            p.recip(st[:, 3:4], st[:, 2:3])
            p.ts1(P_sb[:, 0:W], buf2[:, 0:W], st[:, 3:4], OP.mult)
            for kt in range(nt):
                p.trans(ptp[:, kt * 128:(kt + 1) * 128],
                        P_sb[:, kt * 128:(kt + 1) * 128], id_sb[:, :],
                        wait=(kt == 0), inc=(1 if kt == nt - 1 else 0))
            p.copy(PT_sb[:, 0:W], ptp[:, 0:W])
            for kt in range(nt):
                p.mm(av[0:64, :],
                     vw[:, kt * DM + h * 64:kt * DM + h * 64 + 64],
                     PT_sb[:, kt * 128:(kt + 1) * 128],
                     start=(kt == 0), stop=(kt == nt - 1),
                     wait=(kt == 0), inc=(1 if kt == nt - 1 else 0))
            p.copy(at_sb[hp:hp + 64, hm * 128:(hm + 1) * 128], av[0:64, :])
        p.dma(V["AT"][i], at_sb[:, 0:DM].rearrange("p (m c) -> p m c", m=8))


def emit_C(p, V, PSF, PSB, id_sb, ones):
    stage = p.sb([128, DM])
    wp_sb = p.sb([128, 8 * DM], BF)
    p.load_w_bf16(V["Wp"], DM, DM, wp_sb, stage)
    bpf = p.sb([1, DM])
    bpb = p.sb([1, DM], BF)
    p.dma(bpf[0:1, :], V["bp1"])
    p.copy(bpb[0:1, :], bpf[0:1, :])

    at_sb = p.sb([128, DM], BF)
    s_sb = p.sb([128, DM])
    x_sb = p.sb([128, DM])
    g_sb = p.sb([128, DM])
    x1_sb = p.sb([128, DM])
    xc = p.sb([128, DM])
    sq = p.sb([128, DM])
    st1 = p.sb([128, 4])
    st2 = p.sb([128, 4])
    xn = p.sb([128, DM], BF)
    xnT = p.sb([128, DM], BF)
    ps5 = PSF[:, 0:512]
    pst = PSB[:, 0:128]

    for i in range(NT):
        p.dma(at_sb[:, 0:DM].rearrange("p (m c) -> p m c", m=8), V["AT"][i])
        for nb in range(2):
            for m in range(8):
                p.mm(ps5[:, :], at_sb[:, m * 128:(m + 1) * 128],
                     wp_sb[:, m * DM + nb * 512:m * DM + nb * 512 + 512],
                     start=(m == 0), stop=False,
                     wait=(m == 0), inc=0)
            p.mm(ps5[:, :], ones[0:1, 0:128],
                 bpb[0:1, nb * 512:nb * 512 + 512], start=False, stop=True,
                 wait=False, inc=1)
            p.copy(s_sb[:, nb * 512:(nb + 1) * 512], ps5[:, :])
        p.dma_par([(x_sb[:, :], V["x"][i * 128:(i + 1) * 128, :]),
                   (g_sb[:, :], V["G"][i])])
        p.tt(s_sb[:, 0:DM], s_sb[:, 0:DM], g_sb[:, 0:DM], OP.mult)
        p.tt(x1_sb[:, 0:DM], x_sb[:, 0:DM], s_sb[:, 0:DM], OP.add)
        p.dma(V["X1"][i * 128:(i + 1) * 128, :], x1_sb[:, :])
        p.layernorm(x1_sb, xn, xc, sq, st1, st2)
        for k in range(8):
            p.trans(PSB[:, k * 128:(k + 1) * 128],
                    xn[:, k * 128:(k + 1) * 128], id_sb[:, :],
                    wait=(k == 0), inc=(1 if k == 7 else 0))
        p.copy(xnT[:, 0:DM], PSB[:, 0:DM])
        p.dma(V["XN1T"][i], xnT[:, 0:DM].rearrange("p (m c) -> p m c", m=8))


def emit_D(p, V, PSF, PSB, id_sb, ones):
    stage = p.sb([128, DM])
    wq_sb = p.sb([128, 8 * DM], BF)
    kvd_sb = p.sb([128, 8 * 512], BF)
    kup_sb = p.sb([128, 4 * DM], BF)
    vup_sb = p.sb([128, 4 * DM], BF)
    wg_sb = p.sb([128, 8 * 16], BF)
    p.load_w_bf16(V["wqT"], DM, DM, wq_sb, stage)
    p.load_w_bf16(V["kvdT"], DM, 512, kvd_sb, stage)
    p.load_w_bf16(V["kupT"], 512, DM, kup_sb, stage)
    p.load_w_bf16(V["vupT"], 512, DM, vup_sb, stage)
    p.load_w_bf16(V["WgT"], DM, 16, wg_sb, stage)
    bgf = p.sb([1, 16])
    bgb = p.sb([1, 16], BF)
    p.dma(bgf[0:1, :], V["bg1"])
    p.copy(bgb[0:1, :], bgf[0:1, :])
    cos_sb = p.sb([128, T])
    sin_sb = p.sb([128, T])
    p.dma(cos_sb[:, :], V["cosT"])
    p.dma(sin_sb[:, :], V["sinT"])
    sw_sb = p.sb([128, 128], BF)
    p.dma(sw_sb[:, :], V["SW"])

    xt_sb = p.sb([128, DM], BF)
    pre_sb = p.sb([128, DM], BF)
    q_sb = p.sb([128, DM], BF)
    k_sb = p.sb([128, DM], BF)
    c_sb = p.sb([128, 512], BF)
    v_sb = p.sb([128, DM], BF)
    tt1 = p.sb([128, 128])
    tt2 = p.sb([128, 128])
    lg = p.sb([128, 16])
    cwb = p.sb([128, 16])
    st = p.sb([128, 8])
    ps1 = PSF[:, 0:128]
    psw = PSF[:, 512:640]
    ps5 = PSF[:, 1024:1536]

    def rope(src_sb, dst_sb, i):
        for m in range(8):
            p.mm(psw[:, :], sw_sb[:, :], src_sb[:, m * 128:(m + 1) * 128])
            p.tt(tt1[:, :], src_sb[:, m * 128:(m + 1) * 128],
                 cos_sb[:, i * 128:(i + 1) * 128], OP.mult)
            p.tt(tt2[:, :], psw[:, :], sin_sb[:, i * 128:(i + 1) * 128],
                 OP.mult)
            p.tt(dst_sb[:, m * 128:(m + 1) * 128], tt1[:, :], tt2[:, :],
                 OP.add)

    for i in range(NT):
        p.dma(xt_sb[:, 0:DM].rearrange("p (m c) -> p m c", m=8), V["XN1T"][i])
        for m in range(8):
            for k in range(8):
                p.mm(ps1[:, :],
                     wq_sb[:, k * DM + m * 128:k * DM + (m + 1) * 128],
                     xt_sb[:, k * 128:(k + 1) * 128],
                     start=(k == 0), stop=(k == 7),
                     wait=(k == 0), inc=(1 if k == 7 else 0))
            p.copy(pre_sb[:, m * 128:(m + 1) * 128], ps1[:, :])
        rope(pre_sb, q_sb, i)
        for m in range(4):
            for k in range(8):
                p.mm(ps1[:, :],
                     kvd_sb[:, k * 512 + m * 128:k * 512 + (m + 1) * 128],
                     xt_sb[:, k * 128:(k + 1) * 128],
                     start=(k == 0), stop=(k == 7),
                     wait=(k == 0), inc=(1 if k == 7 else 0))
            p.copy(c_sb[:, m * 128:(m + 1) * 128], ps1[:, :])
        for m in range(8):
            for kc in range(4):
                p.mm(ps1[:, :],
                     kup_sb[:, kc * DM + m * 128:kc * DM + (m + 1) * 128],
                     c_sb[:, kc * 128:(kc + 1) * 128],
                     start=(kc == 0), stop=(kc == 3),
                     wait=(kc == 0), inc=(1 if kc == 3 else 0))
            p.copy(pre_sb[:, m * 128:(m + 1) * 128], ps1[:, :])
        rope(pre_sb, k_sb, i)
        p.dma(V["KT1"][i], k_sb[:, 0:DM].rearrange("p (m c) -> p m c", m=8))
        for nb in range(2):
            for kc in range(4):
                p.mm(ps5[:, :], c_sb[:, kc * 128:(kc + 1) * 128],
                     vup_sb[:, kc * DM + nb * 512:kc * DM + nb * 512 + 512],
                     start=(kc == 0), stop=(kc == 3),
                     wait=(kc == 0), inc=(1 if kc == 3 else 0))
            p.copy(v_sb[:, nb * 512:(nb + 1) * 512], ps5[:, :])
        for k in range(8):
            p.mm(ps1[:, 0:16], xt_sb[:, k * 128:(k + 1) * 128],
                 wg_sb[:, k * 16:(k + 1) * 16],
                 start=(k == 0), stop=False, wait=(k == 0), inc=0)
        p.mm(ps1[:, 0:16], ones[0:1, 0:128], bgb[0:1, :], start=False,
             stop=True, wait=False, inc=1)
        p.copy(lg[:, :], ps1[:, 0:16])
        p.red(st[:, 0:1], lg[:, 0:16], OP.max)
        p.ts1(st[:, 1:2], st[:, 0:1], -1.0, OP.mult)
        p.ts1(cwb[:, 0:16], lg[:, 0:16], st[:, 0:1], OP.is_ge)
        p.stt(cwb[:, 0:16], cwb[:, 0:16], -1e30, lg[:, 0:16], OP.mult, OP.add)
        p.red(st[:, 2:3], cwb[:, 0:16], OP.max)
        p.ts1(cwb[:, 0:16], lg[:, 0:16], st[:, 2:3], OP.is_ge)
        p.act(lg[:, 0:16], lg[:, 0:16], AF.Exp, bias=st[:, 1:2], scale=1.0)
        p.tt(lg[:, 0:16], lg[:, 0:16], cwb[:, 0:16], OP.mult)
        p.red(st[:, 3:4], lg[:, 0:16], OP.add)
        p.recip(st[:, 4:5], st[:, 3:4])
        p.ts1(cwb[:, 0:16], lg[:, 0:16], st[:, 4:5], OP.mult)
        p.dma(V["QT1"][i], q_sb[:, 0:DM].rearrange("p (m c) -> p m c", m=8))
        p.dma(V["V1"][i], v_sb[:, :])
        p.dma(V["CW"][i], cwb[:, :])


def emit_E(p, V, PSF, PSB, id_sb, ones):
    m_sb = p.sb([128, 4 * 512], BF)
    for v in range(4):
        p.dma(m_sb[:, v * 512:(v + 1) * 512], V["M01"][v])
    on_sb = p.sb([1, 64])
    p.dma(on_sb[0:1, :], V["ones64"])

    qb_sb = p.sb([64, 512], BF)
    kt_sb = p.sb([64, 128], BF)
    vext = p.sb([128, 65], BF)
    p.memset(vext[:, 64:65], 1.0)
    pt_sb = p.sb([128, 512], BF)
    rs_sb = p.sb([1, 512])
    bc_sb = p.sb([64, 512])
    atn_sb = p.sb([64, 512], BF)
    ps_s = PSF[:, 0:512]
    av = PSF[:, 512:1024]
    bc = PSF[0:64, 1024:1536]

    for h in range(NH):
        hp = 64 * (h % 2)
        hm = h // 2
        for qb in range(8):
            p.dma(qb_sb[0:64, 0:512].rearrange("p (i t) -> p i t", i=4),
                  V["QT1"][4 * qb:4 * qb + 4, hp:hp + 64, hm, :]
                  .rearrange("i p t -> p i t"))
            last = 4 * qb + 3
            for kt in range(4 * qb + 4):
                p.dma_par([(kt_sb[0:64, :], V["KT1"][kt, hp:hp + 64, hm, :]),
                           (vext[:, 0:64], V["V1"][kt, :, h * 64:h * 64 + 64])])
                p.mm(ps_s[:, :], kt_sb[0:64, :], qb_sb[0:64, :])
                p.act(pt_sb[:, :], ps_s[:, :], AF.Exp, scale=0.125)
                if kt >= 4 * qb:
                    vv = kt - 4 * qb
                    p.tt(pt_sb[:, :], pt_sb[:, :],
                         m_sb[:, vv * 512:(vv + 1) * 512], OP.mult)
                p.mm(av[0:65, :], vext[:, 0:65], pt_sb[:, :],
                     start=(kt == 0), stop=(kt == last),
                     wait=True, inc=1)
            p.recip(rs_sb[0:1, :], av[64:65, :])
            p.mm(bc[0:64, :], on_sb[0:1, :], rs_sb[0:1, :])
            p.copy(bc_sb[:, :], bc[0:64, :])
            p.tt(atn_sb[:, :], av[0:64, :], bc_sb[:, :], OP.mult)
            p.dma(V["ATN"][4 * qb:4 * qb + 4, hp:hp + 64, hm, :]
                  .rearrange("i p t -> p i t"),
                  atn_sb[0:64, 0:512].rearrange("p (i t) -> p i t", i=4))


def emit_EP(p, V, PSF, PSB, id_sb, ones):
    stage = p.sb([128, DM])
    wo_sb = p.sb([128, 8 * DM], BF)
    p.load_w_bf16(V["woT"], DM, DM, wo_sb, stage)
    a_sb = p.sb([128, DM], BF)
    s_sb = p.sb([128, DM])
    ps5 = PSF[:, 0:512]
    for i in range(NT):
        p.dma(a_sb[:, 0:DM].rearrange("p (m c) -> p m c", m=8), V["ATN"][i])
        for nb in range(2):
            for m in range(8):
                p.mm(ps5[:, :], a_sb[:, m * 128:(m + 1) * 128],
                     wo_sb[:, m * DM + nb * 512:m * DM + nb * 512 + 512],
                     start=(m == 0), stop=(m == 7),
                     wait=(m == 0), inc=(1 if m == 7 else 0))
            p.copy(s_sb[:, nb * 512:(nb + 1) * 512], ps5[:, :])
        p.dma(V["S1"][i * 128:(i + 1) * 128, :], s_sb[:, :])


def emit_F(p, V, PSF, PSB, id_sb, ones):
    stage = p.sb([128, 3 * 2048])
    wgu_sb = p.sb([128, 8 * 2048], BF)
    wd_sb = p.sb([128, 8 * DM], BF)
    bguf = p.sb([1, 2048])
    bgub = p.sb([1, 2048], BF)
    bdf = p.sb([1, DM])
    bdb = p.sb([1, DM], BF)

    xb_sb = p.sb([128, 8 * 512], BF)
    acc_sb = p.sb([128, 4 * DM])
    cw_sb = p.sb([128, NT * 16])
    ga = p.sb([128, 512])
    gb = p.sb([128, 512])
    ua = p.sb([128, 512])
    ub = p.sb([128, 512])
    gated = p.sb([128, 8 * 512], BF)
    x1t = p.sb([128, DM])
    s1t = p.sb([128, DM])
    g1 = p.sb([128, DM])
    o_sb = p.sb([128, DM])
    pg = PSF[:, 0:512]
    pu = PSF[:, 512:1024]
    pd = PSF[:, 0:1024]

    p.dma_par([(cw_sb[:, ti * 16:(ti + 1) * 16], V["CW"][ti])
               for ti in range(NT)])
    for ge in range(16):
        if True:
            for k0 in range(0, 8, 3):
                ks = list(range(k0, min(k0 + 3, 8)))
                p.dma_par([(stage[:, (k - k0) * 2048:(k - k0) * 2048 + 2048],
                            V["Wgu"][ge, k * 128:(k + 1) * 128, :])
                           for k in ks])
                for k in ks:
                    p.copy(wgu_sb[:, k * 2048:(k + 1) * 2048],
                           stage[:, (k - k0) * 2048:(k - k0) * 2048 + 2048])
            for k0 in range(0, 8, 3):
                ks = list(range(k0, min(k0 + 3, 8)))
                p.dma_par([(stage[:, (k - k0) * 2048:(k - k0) * 2048 + DM],
                            V["Wd"][ge, k * 128:(k + 1) * 128, :])
                           for k in ks])
                for k in ks:
                    p.copy(wd_sb[:, k * DM:(k + 1) * DM],
                           stage[:, (k - k0) * 2048:(k - k0) * 2048 + DM])
            p.dma_par([(bguf[0:1, :], V["bgu"][ge:ge + 1, :]),
                       (bdf[0:1, :], V["bd"][ge:ge + 1, :])])
            p.copy(bgub[0:1, :], bguf[0:1, :])
            p.copy(bdb[0:1, :], bdf[0:1, :])
        for tb in range(8):
            pairs = [(xb_sb[:, k * 512:(k + 1) * 512]
                      .rearrange("p (i t) -> p i t", i=4),
                      V["XN1T"][4 * tb:4 * tb + 4, :, k, :]
                      .rearrange("i p t -> p i t")) for k in range(8)]
            if ge > 0:
                pairs += [(acc_sb[:, t * DM:(t + 1) * DM],
                           V["ACC"][(4 * tb + t) * 128:
                                    (4 * tb + t) * 128 + 128, :])
                          for t in range(4)]
            p.dma_par(pairs)
            if ge == 0:
                p.memset(acc_sb[:, 0:4 * DM], 0.0)
            for j in range(8):
                for (col0, psx) in ((j * 128, pg), (1024 + j * 128, pu)):
                    for k in range(8):
                        p.mm(psx[:, :],
                             wgu_sb[:, k * 2048 + col0:k * 2048 + col0 + 128],
                             xb_sb[:, k * 512:(k + 1) * 512],
                             start=(k == 0), stop=False,
                             wait=(k == 0), inc=0)
                    p.mm(psx[:, :], bgub[0:1, col0:col0 + 128],
                         ones[0:1, 0:512], start=False, stop=True,
                         wait=False, inc=1)
                p.ts1(ga[:, :], pg[:, :], 7.0, OP.min)
                p.act(gb[:, :], ga[:, :], AF.Silu, scale=1.702)
                p.ts(ua[:, :], pu[:, :], 7.0, -7.0, OP.min, OP.max)
                p.act(ub[:, :], ua[:, :], AF.Copy, scale=1.0 / 1.702,
                      bias=1.0 / 1.702)
                p.tt(gated[:, j * 512:(j + 1) * 512], gb[:, :], ub[:, :],
                     OP.mult)
            for t in range(4):
                for nb in range(2):
                    for j in range(8):
                        p.mm(pd[:, nb * 512:nb * 512 + 512],
                             gated[:, j * 512 + t * 128:
                                    j * 512 + t * 128 + 128],
                             wd_sb[:, j * DM + nb * 512:
                                   j * DM + nb * 512 + 512],
                             start=(j == 0), stop=False,
                             wait=(j == 0 and nb == 0), inc=0)
                    p.mm(pd[:, nb * 512:nb * 512 + 512], ones[0:1, 0:128],
                         bdb[0:1, nb * 512:nb * 512 + 512], start=False,
                         stop=True, wait=False, inc=(1 if nb == 1 else 0))
                p.stt(acc_sb[:, t * DM:(t + 1) * DM], pd[:, 0:DM],
                      cw_sb[:, (4 * tb + t) * 16 + ge:
                            (4 * tb + t) * 16 + ge + 1],
                      acc_sb[:, t * DM:(t + 1) * DM], OP.mult, OP.add)
            p.dma_par([(V["ACC"][(4 * tb + t) * 128:
                                 (4 * tb + t) * 128 + 128, :],
                        acc_sb[:, t * DM:(t + 1) * DM])
                       for t in range(4)])
    for ti in range(NT):
        r0 = ti * 128
        p.dma_par([(x1t[:, :], V["X1"][r0:r0 + 128, :]),
                   (s1t[:, :], V["S1"][r0:r0 + 128, :]),
                   (acc_sb[:, 0:DM], V["ACC"][r0:r0 + 128, :])])
        p.act(g1[:, 0:DM], acc_sb[:, 0:DM], AF.Silu)
        p.tt(g1[:, 0:DM], s1t[:, 0:DM], g1[:, 0:DM], OP.mult)
        p.tt(o_sb[:, 0:DM], x1t[:, 0:DM], g1[:, 0:DM], OP.add)
        p.dma(V["OUT"][r0:r0 + 128, :], o_sb[:, :])


def build_fused():
    p = Prog()
    p.setup_consts()
    V = {}
    V["x"] = p.inp("x", [T, DM])
    V["Wq"] = p.inp("Wq", [DM, DM])
    V["Wk"] = p.inp("Wk", [DM, DM])
    V["Wv"] = p.inp("Wv", [DM, DM])
    V["Wr"] = p.inp("Wr", [DM, 160])
    V["WfT"] = p.inp("WfT", [DM, DM])
    V["bqc"] = p.inp("bqc", [128, 8])
    V["br1"] = p.inp("br1", [1, 160])
    V["scb"] = p.inp("scb", [128, DM])
    V["bib"] = p.inp("bib", [128, DM])
    V["ident"] = p.inp("ident", [128, 128], BF)
    V["U"] = p.inp("U", [NBI, 128, 1152])
    V["MV0"] = p.inp("MV0", [128, 128])
    V["MV8"] = p.inp("MV8", [128, 128])
    V["Wp"] = p.inp("Wp", [DM, DM])
    V["bp1"] = p.inp("bp1", [1, DM])
    V["wqT"] = p.inp("wqT", [DM, DM])
    V["kvdT"] = p.inp("kvdT", [DM, 512])
    V["kupT"] = p.inp("kupT", [512, DM])
    V["vupT"] = p.inp("vupT", [512, DM])
    V["WgT"] = p.inp("WgT", [DM, 16])
    V["bg1"] = p.inp("bg1", [1, 16])
    V["cosT"] = p.inp("cosT", [128, T])
    V["sinT"] = p.inp("sinT", [128, T])
    V["SW"] = p.inp("SW", [128, 128], BF)
    V["M01"] = p.inp("M01", [4, 128, 512], BF)
    V["ones64"] = p.inp("ones64", [1, 64])
    V["woT"] = p.inp("woT", [DM, DM])
    V["Wgu"] = p.inp("Wgu", [16, DM, 2048])
    V["bgu"] = p.inp("bgu", [16, 2048])
    V["Wd"] = p.inp("Wd", [16, DM, DM])
    V["bd"] = p.inp("bd", [16, DM])
    V["QT"] = p.scratch("sQT", [NT, 128, 8, 128], BF)
    V["KT"] = p.scratch("sKT", [NT, 128, 8, 128], BF)
    V["V"] = p.scratch("sV", [NT, 128, DM], BF)
    V["R"] = p.scratch("sR", [NT, 128, 160])
    V["G"] = p.scratch("sG", [NT, 128, DM])
    V["AT"] = p.scratch("sAT", [NT, 128, 8, 128], BF)
    V["X1"] = p.scratch("sX1", [T, DM])
    V["XN1T"] = p.scratch("sXN1T", [NT, 128, 8, 128], BF)
    V["QT1"] = p.scratch("sQT1", [NT, 128, 8, 128], BF)
    V["KT1"] = p.scratch("sKT1", [NT, 128, 8, 128], BF)
    V["V1"] = p.scratch("sV1", [NT, 128, DM], BF)
    V["CW"] = p.scratch("sCW", [NT, 128, 16])
    V["ATN"] = p.scratch("sATN", [NT, 128, 8, 128], BF)
    V["S1"] = p.scratch("sS1", [T, DM])
    V["ACC"] = p.scratch("sACC", [T, DM])
    V["OUT"] = p.outp("OUT", [T, DM])

    PSF = p.ps([128, 1536], F32)
    PSB = p.ps([128, 1152], BF)
    id_sb = p.sb([128, 128], BF)
    ones = p.sb([1, 512], BF)
    p.dma(id_sb[:, :], V["ident"])
    p.memset(ones[0:1, :], 1.0)
    p.mark_persist()

    for emit in (emit_A, emit_B, emit_C, emit_D, emit_E, emit_EP, emit_F):
        emit(p, V, PSF, PSB, id_sb, ones)
        p.stage_reset()
    return p.finalize()


_cache = {}


def _consts():
    if "consts" in _cache:
        return _cache["consts"]
    c = {}
    c["ident"] = np.eye(128, dtype=BF_NP)
    q = np.arange(128)[:, None]
    k = np.arange(128)[None, :]
    c["MV0"] = np.where(k <= q, 0.0, -1e9).astype(np.float32)
    c["MV8"] = np.where(k > q, 0.0, -1e9).astype(np.float32)
    r = np.arange(128)[None, :, None]
    cc = np.arange(512)[None, None, :]
    v = np.arange(4)[:, None, None]
    c["M01"] = (128 * v + r <= cc).astype(BF_NP)
    c["ones64"] = np.ones((1, 64), dtype=np.float32)
    d = np.arange(64)
    inv = 1.0 / (10000.0 ** (2 * (d % 32) / 64.0))
    pos = np.arange(T)
    ang = pos[None, :] * inv[:, None]
    cos = np.cos(ang)
    sin = np.sin(ang)
    sgn = np.where(d < 32, -1.0, 1.0)[:, None]
    c["cosT"] = np.tile(cos, (2, 1)).astype(np.float32)
    c["sinT"] = np.tile(sin * sgn, (2, 1)).astype(np.float32)
    S = np.zeros((128, 128), dtype=BF_NP)
    for m in range(128):
        sig = m + 32 if (m % 64) < 32 else m - 32
        S[sig, m] = 1.0
    c["SW"] = S
    _cache["consts"] = c
    return c


def _utables(b_nd):
    qq = np.arange(128)[:, None]
    jj = np.arange(1152)[None, :]
    idx = 1024 + qq - jj
    ok = (idx >= 0) & (idx < 1024)
    idxc = np.clip(idx, 0, 1023)
    U = np.where(ok[None, :, :], b_nd[:, idxc], 0.0).astype(np.float32)
    return U


def kernel(L0_attn_X, L0_attn_Wq, L0_attn_bq, L0_attn_Wk, L0_attn_Wv,
           L0_attn_Wproj, L0_attn_bproj, L0_attn_Wr, L0_attn_br,
           L0_attn_b_nd, L0_ffn_weights, L0_ffn_scale, L0_ffn_bias,
           L1_attn_wq, L1_attn_w_kv_down, L1_attn_w_k_up, L1_attn_w_v_up,
           L1_attn_wo, L1_ffn_Wg, L1_ffn_bg, L1_ffn_Wgu, L1_ffn_bgu,
           L1_ffn_Wd, L1_ffn_bd):
    if "fused" not in _cache:
        _cache["fused"] = build_fused()
    nc = _cache["fused"]
    C = _consts()
    f32 = np.float32
    asf = np.ascontiguousarray
    X = np.asarray(L0_attn_X, f32)

    perm = np.concatenate([np.arange(0, 2048, 2), np.arange(1, 2048, 2)])
    shared = dict(
        Wq=np.asarray(L0_attn_Wq, f32), Wk=np.asarray(L0_attn_Wk, f32),
        Wv=np.asarray(L0_attn_Wv, f32), Wr=np.asarray(L0_attn_Wr, f32),
        WfT=asf(np.asarray(L0_ffn_weights, f32).T),
        bqc=asf(np.asarray(L0_attn_bq, f32).reshape(8, 128).T),
        br1=np.asarray(L0_attn_br, f32).reshape(1, 160),
        scb=asf(np.broadcast_to(np.asarray(L0_ffn_scale, f32), (128, DM))),
        bib=asf(np.broadcast_to(np.asarray(L0_ffn_bias, f32), (128, DM))),
        ident=C["ident"], U=_utables(np.asarray(L0_attn_b_nd, f32)),
        MV0=C["MV0"], MV8=C["MV8"],
        Wp=np.asarray(L0_attn_Wproj, f32),
        bp1=np.asarray(L0_attn_bproj, f32).reshape(1, DM),
        wqT=asf(np.asarray(L1_attn_wq, f32).T),
        kvdT=asf(np.asarray(L1_attn_w_kv_down, f32).T),
        kupT=asf(np.asarray(L1_attn_w_k_up, f32).T),
        vupT=asf(np.asarray(L1_attn_w_v_up, f32).T),
        WgT=asf(np.asarray(L1_ffn_Wg, f32).T),
        bg1=np.asarray(L1_ffn_bg, f32).reshape(1, 16),
        cosT=C["cosT"], sinT=C["sinT"], SW=C["SW"], M01=C["M01"],
        ones64=C["ones64"],
        woT=asf(np.asarray(L1_attn_wo, f32).T),
        Wgu=asf(np.asarray(L1_ffn_Wgu, f32)[:, :, perm]),
        bgu=asf(np.asarray(L1_ffn_bgu, f32)[:, perm]),
        Wd=np.asarray(L1_ffn_Wd, f32),
        bd=np.asarray(L1_ffn_bd, f32),
    )
    maps = [dict(x=asf(X[c]), **shared) for c in range(8)]
    res = run_bass_kernel_spmd(nc, maps, core_ids=list(range(8)))
    out = np.stack([res.results[c]["OUT"] for c in range(8)], axis=0)
    return out.astype(np.float32)
